# version 05
# baseline (speedup 1.0000x reference)
import numpy as np
import concourse.bass as bass
import concourse.mybir as mybir
from concourse.bass_utils import run_bass_kernel_spmd

F32 = mybir.dt.float32
BF16 = mybir.dt.bfloat16
I32 = mybir.dt.int32
AF = mybir.ActivationFunctionType
ALU = mybir.AluOpType
AX = mybir.AxisListType

NCORE = 8
D = 1024
NT = 4096
HALO1 = 512
NEG = -1.0e30


class Ch:

    def __init__(self, nc, name):
        self.sem = nc.alloc_semaphore(name)
        self.n = 0

    def sig(self, inst, k=1):
        inst.then_inc(self.sem, k)
        self.n += k
        return self.n

    def wait(self, eng, n):
        eng.wait_ge(self.sem, n)


def make_eye(nc, blk, eye, scr, ch):
    @blk.vector
    def _(v):
        ch.sig(v.memset(scr, 1.0))

    @blk.gpsimd
    def _(g):
        ch.wait(g, ch.n)
        ch.sig(g.affine_select(eye, scr, pattern=[[1, 128]], compare_op=ALU.is_equal,
                               fill=0.0, base=0, channel_multiplier=-1))


def build_l1():
    nc = bass.Bass("TRN2", target_bir_lowering=False)
    T = NT + HALO1
    Tc = 512
    NCH = T // Tc

    x_in = nc.dram_tensor("x_l1", [T, D], F32, kind="ExternalInput").ap()
    w_in_r = nc.dram_tensor("w_in_r", [128, 8 * 2048], F32, kind="ExternalInput").ap()
    bin_u = nc.dram_tensor("bin_u", [128, 8], F32, kind="ExternalInput").ap()
    bin_z = nc.dram_tensor("bin_z", [128, 8], F32, kind="ExternalInput").ap()
    bconv = nc.dram_tensor("bconv", [128, 8], F32, kind="ExternalInput").ap()
    wconv = nc.dram_tensor("wconv", [128, 32], F32, kind="ExternalInput").ap()
    wxp_r = nc.dram_tensor("wxp_r", [128, 8 * 48], F32, kind="ExternalInput").ap()
    bxp = nc.dram_tensor("bxp", [48, 1], F32, kind="ExternalInput").ap()
    wdt_r = nc.dram_tensor("wdt_r", [16, 1024], F32, kind="ExternalInput").ap()
    bdt = nc.dram_tensor("bdt", [128, 8], F32, kind="ExternalInput").ap()
    umask = nc.dram_tensor("umask", [128, Tc], F32, kind="ExternalInput").ap()

    u_out = nc.dram_tensor("u_out", [D, NT], F32, kind="ExternalOutput").ap()
    dt_out = nc.dram_tensor("dt_out", [D, NT], F32, kind="ExternalOutput").ap()
    sz_out = nc.dram_tensor("sz_out", [D, NT], F32, kind="ExternalOutput").ap()
    bc_out = nc.dram_tensor("bc_out", [32, NT], F32, kind="ExternalOutput").ap()

    w_in_bf_d = nc.dram_tensor("w_in_bf_d", [128, 8 * 2048], BF16).ap()

    sb = lambda n, sh, dt=F32: nc.alloc_sbuf_tensor(n, sh, dt).ap()
    w_in_sb = sb("w_in_sb", [128, 8 * 2048], BF16)
    wxp_sb = sb("wxp_sb", [128, 8 * 48])
    wdt_sb = sb("wdt_sb", [16, 1024])
    binu_sb = sb("binu_sb", [128, 8]); binz_sb = sb("binz_sb", [128, 8])
    bconv_sb = sb("bconv_sb", [128, 8]); wconv_sb = sb("wconv_sb", [128, 32])
    bxp_sb = sb("bxp_sb", [48, 1]); bdt_sb = sb("bdt_sb", [128, 8])
    umask_sb = sb("umask_sb", [128, Tc])
    eye = sb("eye", [128, 128]); scr128 = sb("scr128", [128, 128])
    diagc = sb("diagc", [128, 32 * 128])
    xtok = sb("xtok", [128, 4 * D])
    xT = sb("xT", [128, 8 * Tc], BF16)
    uraw = sb("uraw", [128, 8 * (Tc + 3)])
    u_sb = sb("u_sb", [128, 8 * Tc])
    sz_sb = sb("sz_sb", [128, 8 * Tc])
    xdb = sb("xdb", [48, Tc])
    dt_sb = sb("dt_sb", [128, 8 * Tc])
    sp1 = sb("sp1", [128, Tc]); sp2 = sb("sp2", [128, Tc]); sp3 = sb("sp3", [128, Tc])

    pA = nc.alloc_psum_tensor("pA", [128, 512], F32).ap()
    pB = nc.alloc_psum_tensor("pB", [128, 512], F32).ap()
    pC = nc.alloc_psum_tensor("pC", [128, 512], F32).ap()
    pD = nc.alloc_psum_tensor("pD", [48, 512], F32).ap()

    ld = Ch(nc, "ld"); st = Ch(nc, "st"); cst = Ch(nc, "cst")
    c_tr = Ch(nc, "c_tr")
    c_mm = Ch(nc, "c_mm")
    c_xt = Ch(nc, "c_xt")
    c_ur = Ch(nc, "c_ur")
    c_sz = Ch(nc, "c_sz")
    c_uc = Ch(nc, "c_uc")
    c_xdb = Ch(nc, "c_xdb")
    c_sp = Ch(nc, "c_sp")
    c_msk = Ch(nc, "c_msk")
    c_dta = Ch(nc, "c_dta")
    c_tail = Ch(nc, "c_tail")

    with nc.Block("prep0") as blk:
        @blk.gpsimd
        def _(g):
            cst.sig(g.dma_start(out=w_in_bf_d, in_=w_in_r), 16)

        @blk.sync
        def _(s):
            for dst, src_ in [(wxp_sb, wxp_r), (wdt_sb, wdt_r), (binu_sb, bin_u),
                              (binz_sb, bin_z), (bconv_sb, bconv), (wconv_sb, wconv),
                              (bxp_sb, bxp), (bdt_sb, bdt), (umask_sb, umask)]:
                ld.sig(s.dma_start(dst, src_), 16)
            ld.wait(s, ld.n)
            cst.wait(s, cst.n)
            ld.sig(s.dma_start(w_in_sb, w_in_bf_d), 16)
            ld.wait(s, ld.n)

    with nc.Block("prep1") as blk:
        make_eye(nc, blk, eye, scr128, c_msk)

        @blk.vector
        def _(v):
            for dc in range(8):
                v.memset(uraw[:, dc * (Tc + 3): dc * (Tc + 3) + 3], 0.0)

        @blk.scalar
        def _(s):
            c_msk.wait(s, c_msk.n)
            for dc in range(8):
                for j in range(4):
                    i = dc * 4 + j
                    s.activation(diagc[:, i * 128:(i + 1) * 128], eye, AF.Copy,
                                 scale=wconv_sb[:, i:i + 1])

    for k in range(NCH):
        own = k >= 1
        oc0 = (k - 1) * Tc
        B = dict(ld=ld.n, tr=c_tr.n, mm=c_mm.n, xt=c_xt.n, ur=c_ur.n, sz=c_sz.n,
                 uc=c_uc.n, xdb=c_xdb.n, sp=c_sp.n, msk=c_msk.n, dta=c_dta.n)
        with nc.Block(f"l1c{k}") as blk:
            @blk.sync
            def _(s, k=k, B=B, oc0=oc0, own=own):
                for ti in range(4):
                    ld.sig(s.dma_start(xtok[:, ti * D:(ti + 1) * D],
                                       x_in[k * Tc + ti * 128: k * Tc + (ti + 1) * 128, :]), 16)
                if own:
                    c_sz.wait(s, B["sz"] + 8)
                    for dc in range(8):
                        st.sig(s.dma_start(sz_out[dc * 128:(dc + 1) * 128, oc0:oc0 + Tc],
                                           sz_sb[:, dc * Tc:(dc + 1) * Tc]), 16)
                    c_uc.wait(s, B["uc"] + 8)
                    for dc in range(8):
                        st.sig(s.dma_start(u_out[dc * 128:(dc + 1) * 128, oc0:oc0 + Tc],
                                           u_sb[:, dc * Tc:(dc + 1) * Tc]), 16)
                    c_xdb.wait(s, B["xdb"] + 1)
                    st.sig(s.dma_start(bc_out[:, oc0:oc0 + Tc], xdb[16:48, :]), 16)
                    for dc in range(8):
                        c_dta.wait(s, B["dta"] + dc + 1)
                        st.sig(s.dma_start(dt_out[dc * 128:(dc + 1) * 128, oc0:oc0 + Tc],
                                           dt_sb[:, dc * Tc:(dc + 1) * Tc]), 16)
                st.wait(s, st.n)

            @blk.tensor
            def _(t, k=k, B=B):
                ld.wait(t, B["ld"] + 64)
                for dj in range(8):
                    if dj > 0:
                        c_xt.wait(t, B["xt"] + dj)
                    for ti in range(4):
                        inst = t.transpose(pA[:, ti * 128:(ti + 1) * 128],
                                           xtok[:, ti * D + dj * 128: ti * D + (dj + 1) * 128], eye)
                    c_tr.sig(inst)
                c_xt.wait(t, B["xt"] + 8)
                for oc in range(16):
                    ps = pB if (oc % 2 == 0) else pC
                    if oc >= 2:
                        if oc - 2 < 8:
                            c_ur.wait(t, B["ur"] + oc - 1)
                        else:
                            c_sz.wait(t, B["sz"] + oc - 9)
                    for dc in range(8):
                        inst = t.matmul(ps, w_in_sb[:, dc * 2048 + oc * 128: dc * 2048 + (oc + 1) * 128],
                                        xT[:, dc * Tc:(dc + 1) * Tc],
                                        start=(dc == 0), stop=(dc == 7))
                    c_mm.sig(inst)
                c_ur.wait(t, B["ur"] + 8)
                if k == 0:
                    c_msk.wait(t, B["msk"] + 8)
                for dc in range(8):
                    ps = pB if (dc % 2 == 0) else pC
                    if dc < 2:
                        c_sz.wait(t, B["sz"] + 7 + dc)
                    else:
                        c_uc.wait(t, B["uc"] + dc - 1)
                    for j in range(4):
                        i = dc * 4 + j
                        inst = t.matmul(ps, diagc[:, i * 128:(i + 1) * 128],
                                        uraw[:, dc * (Tc + 3) + j: dc * (Tc + 3) + j + Tc],
                                        start=(j == 0), stop=(j == 3))
                    c_mm.sig(inst)
                c_uc.wait(t, B["uc"] + 8)
                for dc in range(8):
                    inst = t.matmul(pD, wxp_sb[:, dc * 48:(dc + 1) * 48],
                                    u_sb[:, dc * Tc:(dc + 1) * Tc],
                                    start=(dc == 0), stop=(dc == 7))
                c_mm.sig(inst)
                c_xdb.wait(t, B["xdb"] + 1)
                for dc in range(8):
                    ps = pB if (dc % 2 == 0) else pC
                    if dc < 2:
                        c_uc.wait(t, B["uc"] + 7 + dc)
                    else:
                        c_sp.wait(t, B["sp"] + 4 * (dc - 2) + 2)
                    c_mm.sig(t.matmul(ps, wdt_sb[:, dc * 128:(dc + 1) * 128], xdb[0:16, :],
                                      start=True, stop=True))

            @blk.scalar
            def _(s, k=k, B=B):
                for dj in range(8):
                    c_tr.wait(s, B["tr"] + dj + 1)
                    c_xt.sig(s.copy(xT[:, dj * Tc:(dj + 1) * Tc], pA))
                for oc in range(8, 16):
                    ps = pB if (oc % 2 == 0) else pC
                    c_mm.wait(s, B["mm"] + oc + 1)
                    c_sz.sig(s.activation(sz_sb[:, (oc - 8) * Tc:(oc - 7) * Tc],
                                          ps, AF.Silu, bias=binz_sb[:, oc - 8:oc - 7]))
                for dc in range(8):
                    ps = pB if (dc % 2 == 0) else pC
                    c_mm.wait(s, B["mm"] + 16 + dc + 1)
                    c_uc.sig(s.activation(u_sb[:, dc * Tc:(dc + 1) * Tc], ps, AF.Silu,
                                          bias=bconv_sb[:, dc:dc + 1]))
                for dc in range(8):
                    ps = pB if (dc % 2 == 0) else pC
                    c_mm.wait(s, B["mm"] + 25 + dc + 1)
                    if dc >= 1:
                        c_dta.wait(s, B["dta"] + dc)
                    c_sp.sig(s.activation(sp1, ps, AF.Abs, bias=bdt_sb[:, dc:dc + 1]))
                    c_sp.sig(s.activation(sp3, ps, AF.Relu, bias=bdt_sb[:, dc:dc + 1]))
                    c_sp.wait(s, c_sp.n)
                    c_sp.sig(s.activation(sp2, sp1, AF.Exp, scale=-1.0))
                    c_sp.wait(s, c_sp.n)
                    c_sp.sig(s.activation(sp1, sp2, AF.Ln, bias=1.0))

            @blk.vector
            def _(v, k=k, B=B):
                for oc in range(8):
                    ps = pB if (oc % 2 == 0) else pC
                    c_mm.wait(v, B["mm"] + oc + 1)
                    c_ur.sig(v.tensor_scalar(uraw[:, oc * (Tc + 3) + 3: oc * (Tc + 3) + 3 + Tc],
                                             ps, binu_sb[:, oc:oc + 1], None, ALU.add))
                if k == 0:
                    for dc in range(8):
                        c_ur.wait(v, B["ur"] + dc + 1)
                        c_msk.sig(v.tensor_mul(uraw[:, dc * (Tc + 3) + 3: dc * (Tc + 3) + 3 + Tc],
                                               uraw[:, dc * (Tc + 3) + 3: dc * (Tc + 3) + 3 + Tc],
                                               umask_sb))
                c_mm.wait(v, B["mm"] + 25)
                c_xdb.sig(v.tensor_scalar(xdb, pD[0:48, :], bxp_sb, None, ALU.add))
                for dc in range(8):
                    c_sp.wait(v, B["sp"] + 4 * (dc + 1))
                    c_dta.sig(v.tensor_add(dt_sb[:, dc * Tc:(dc + 1) * Tc], sp1, sp3))

            @blk.gpsimd
            def _(g, B=B):
                for dc in range(8):
                    c_mm.wait(g, B["mm"] + 16 + dc + 1)
                    c_tail.sig(g.tensor_copy(uraw[:, dc * (Tc + 3): dc * (Tc + 3) + 3],
                                             uraw[:, dc * (Tc + 3) + Tc: dc * (Tc + 3) + Tc + 3]))

    with nc.Block("fin") as blk:
        @blk.sync
        def _(s):
            st.wait(s, st.n)
    return nc


def stage_l1(ins):
    x = ins["L0_attn_x"]
    W_in = ins["L0_attn_W_in"]; b_in = ins["L0_attn_b_in"]
    W_conv = ins["L0_attn_W_conv"]; b_conv = ins["L0_attn_b_conv"]
    W_xp = ins["L0_attn_W_xp"]; b_xp = ins["L0_attn_b_xp"]
    W_dt = ins["L0_attn_W_dt"]; b_dt = ins["L0_attn_b_dt"]

    w_in_r = np.ascontiguousarray(
        W_in.reshape(1024, 16, 128).transpose(2, 0, 1), np.float32)
    w_in_r = np.ascontiguousarray(
        W_in.reshape(8, 128, 2048).transpose(1, 0, 2).reshape(128, 8 * 2048), dtype=np.float32)
    wxp_r = np.ascontiguousarray(
        W_xp.reshape(8, 128, 48).transpose(1, 0, 2).reshape(128, 8 * 48), dtype=np.float32)
    bin_u = np.ascontiguousarray(b_in[:1024].reshape(8, 128).T, dtype=np.float32)
    bin_z = np.ascontiguousarray(b_in[1024:].reshape(8, 128).T, dtype=np.float32)
    bconv_r = np.ascontiguousarray(b_conv.reshape(8, 128).T, dtype=np.float32)
    wconv_r = np.ascontiguousarray(
        W_conv[:, 0, :].reshape(8, 128, 4).transpose(1, 0, 2).reshape(128, 32), dtype=np.float32)
    bdt_r = np.ascontiguousarray(b_dt.reshape(8, 128).T, dtype=np.float32)
    bxp_r = np.ascontiguousarray(b_xp.reshape(48, 1), dtype=np.float32)
    wdt_r = np.ascontiguousarray(W_dt, dtype=np.float32)

    maps = []
    for c in range(NCORE):
        b, half = c // 2, c % 2
        s0 = half * NT
        xs = np.zeros((NT + HALO1, D), np.float32)
        lo = s0 - HALO1
        src_lo = max(lo, 0)
        xs[src_lo - lo:, :] = x[b, src_lo: s0 + NT, :]
        um = (np.zeros((128, 512), np.float32) if half == 0
              else np.ones((128, 512), np.float32))
        maps.append({"x_l1": xs, "w_in_r": w_in_r, "bin_u": bin_u, "bin_z": bin_z,
                     "bconv": bconv_r, "wconv": wconv_r, "wxp_r": wxp_r, "bxp": bxp_r,
                     "wdt_r": wdt_r, "bdt": bdt_r, "umask": um})
    return maps


def build_l2(debug=False):
    nc = bass.Bass("TRN2", target_bir_lowering=False)
    DH = 512
    L = 8192
    Ts = 512
    NCH = L // Ts

    u_in = nc.dram_tensor("u_h", [DH, L], F32, kind="ExternalInput").ap()
    dt_in = nc.dram_tensor("dt_h", [DH, L], F32, kind="ExternalInput").ap()
    bc_in = nc.dram_tensor("bc_h", [32, L], F32, kind="ExternalInput").ap()
    alog_in = nc.dram_tensor("alog_h", [128, 4 * 16], F32, kind="ExternalInput").ap()
    y_out = nc.dram_tensor("y_h", [DH, L], F32, kind="ExternalOutput").ap()
    dbg_st = nc.dram_tensor("dbg_st", [128, 64], F32, kind="ExternalOutput").ap()
    dbg_da = nc.dram_tensor("dbg_da", [128, 512], F32, kind="ExternalOutput").ap()
    dbg_db = nc.dram_tensor("dbg_db", [128, 512], F32, kind="ExternalOutput").ap()
    dbg_h = nc.dram_tensor("dbg_h", [128, 512], F32, kind="ExternalOutput").ap()
    dbg_cr = nc.dram_tensor("dbg_cr", [128, 512], F32, kind="ExternalOutput").ap()
    dbg_db_sb = nc.alloc_sbuf_tensor("dbg_db_sb", [128, 512], F32).ap()

    sb = lambda n, sh, dt=F32: nc.alloc_sbuf_tensor(n, sh, dt).ap()
    eye = sb("eye", [128, 128]); scr128 = sb("scr128", [128, 128])
    ones1 = sb("ones1", [1, 128])
    A_sb = sb("A_sb", [128, 64])
    alog_sb = sb("alog_sb", [128, 64])
    u_sb = sb("u_sb", [128, 4 * Ts])
    dt_sb = sb("dt_sb", [128, 4 * Ts])
    bm_sb = sb("bm_sb", [16, Ts])
    cm_sb = sb("cm_sb", [16, Ts])
    sel_sb = sb("sel_sb", [16, 16 * 128])
    ones16 = sb("ones16", [16, 128])
    du_sb = sb("du_sb", [128, 4 * Ts])
    duT = sb("duT", [128, 4 * Ts])
    bmT = sb("bmT", [128, 64])
    dA = [sb(f"dA{p}", [128, 4 * Ts]) for p in range(2)]
    dg = [sb(f"dg{p}", [128, 4 * 128]) for p in range(2)]
    crep = [sb(f"crep{p}", [128, Ts]) for p in range(2)]
    h_sb = [sb(f"h{p}", [128, 4 * Ts]) for p in range(2)]
    yt_sb = [sb(f"yt{p}", [128, 4 * Ts]) for p in range(2)]
    st_sb = sb("st_sb", [128, 64])
    y_sb = sb("y_sb", [128, 4 * Ts])

    p_y = [nc.alloc_psum_tensor(f"p_y{i}", [128, 512], F32).ap() for i in range(4)]
    p_db = [nc.alloc_psum_tensor(f"p_db{i}", [128, 512], F32).ap() for i in range(2)]
    p_cr = nc.alloc_psum_tensor("p_cr", [128, 512], F32).ap()
    p_tr = nc.alloc_psum_tensor("p_tr", [128, 512], F32).ap()

    ld = Ch(nc, "ld"); st = Ch(nc, "st")
    c_bmt = Ch(nc, "c_bmt"); c_bev = Ch(nc, "c_bev")
    c_du = Ch(nc, "c_du"); c_dut = Ch(nc, "c_dut"); c_due = Ch(nc, "c_due")
    c_da = Ch(nc, "c_da"); c_dg = Ch(nc, "c_dg")
    c_cm = Ch(nc, "c_cm"); c_ce = Ch(nc, "c_ce")
    c_db = Ch(nc, "c_db"); c_sc = Ch(nc, "c_sc")
    c_yt = Ch(nc, "c_yt"); c_ya = Ch(nc, "c_ya"); c_ye = Ch(nc, "c_ye")
    c_stc = Ch(nc, "c_stc")

    with nc.Block("prep0") as blk:
        @blk.sync
        def _(s):
            ld.sig(s.dma_start(alog_sb, alog_in), 16)
            ld.wait(s, ld.n)

        @blk.vector
        def _(v):
            c_du.sig(v.memset(scr128, 1.0))
            v.memset(ones1, 1.0)
            v.memset(st_sb, 0.0)
            c_du.sig(v.memset(ones16, 1.0))

        @blk.gpsimd
        def _(g):
            c_du.wait(g, c_du.n)
            c_du.sig(g.affine_select(eye, scr128, pattern=[[1, 128]], compare_op=ALU.is_equal,
                                     fill=0.0, base=0, channel_multiplier=-1))
            for nn in range(16):
                g.affine_select(sel_sb[:, nn * 128:(nn + 1) * 128], ones16,
                                pattern=[[0, 128]], compare_op=ALU.is_equal,
                                fill=0.0, base=-nn, channel_multiplier=1)

    with nc.Block("prep1") as blk:
        @blk.scalar
        def _(s):
            ld.wait(s, ld.n)
            c_da.sig(s.activation(A_sb, alog_sb, AF.Exp))

        @blk.vector
        def _(v):
            c_da.wait(v, c_da.n)
            v.tensor_scalar(A_sb, A_sb, -1.0, None, ALU.mult)

    for k in range(NCH):
        t0 = k * Ts
        B = dict(ld=ld.n, bmt=c_bmt.n, bev=c_bev.n, du=c_du.n, dut=c_dut.n,
                 due=c_due.n, da=c_da.n, dg=c_dg.n, cm=c_cm.n, ce=c_ce.n,
                 db=c_db.n, sc=c_sc.n, yt=c_yt.n, ya=c_ya.n, ye=c_ye.n, st=st.n,
                 stc=c_stc.n)
        with nc.Block(f"l2c{k}") as blk:
            @blk.sync
            def _(s, k=k, B=B, t0=t0):
                if debug and k == 1:
                    st.sig(s.dma_start(dbg_st, st_sb), 16)
                for pt in range(4):
                    ld.sig(s.dma_start(u_sb[:, pt * Ts:(pt + 1) * Ts],
                                       u_in[pt * 128:(pt + 1) * 128, t0:t0 + Ts]), 16)
                    ld.sig(s.dma_start(dt_sb[:, pt * Ts:(pt + 1) * Ts],
                                       dt_in[pt * 128:(pt + 1) * 128, t0:t0 + Ts]), 16)
                ld.sig(s.dma_start(bm_sb, bc_in[0:16, t0:t0 + Ts]), 16)
                ld.sig(s.dma_start(cm_sb, bc_in[16:32, t0:t0 + Ts]), 16)
                if debug and k == 1:
                    c_da.wait(s, B["da"] + 1)
                    st.sig(s.dma_start(dbg_da, dA[0][:, 0:Ts]), 16)
                    c_sc.wait(s, B["sc"] + 1)
                    st.sig(s.dma_start(dbg_h, h_sb[0][:, 0:Ts]), 16)
                    c_ce.wait(s, B["ce"] + 1)
                    st.sig(s.dma_start(dbg_cr, crep[0]), 16)
                c_ye.wait(s, B["ye"] + 4)
                for pt in range(4):
                    st.sig(s.dma_start(y_out[pt * 128:(pt + 1) * 128, t0:t0 + Ts],
                                       y_sb[:, pt * Ts:(pt + 1) * Ts]), 16)
                st.wait(s, st.n)

            @blk.tensor
            def _(t, B=B):
                ld.wait(t, B["ld"] + 160)
                for tb in range(4):
                    inst = t.transpose(p_tr[:, tb * 16:(tb + 1) * 16],
                                       bm_sb[0:16, tb * 128:(tb + 1) * 128], eye[0:16, 0:16])
                c_bmt.sig(inst)
                c_bev.wait(t, B["bev"] + 1)
                for tt in range(4):
                    if tt > 0:
                        c_due.wait(t, B["due"] + tt)
                    c_du.wait(t, B["du"] + 4)
                    for pt in range(4):
                        inst = t.transpose(p_tr[:, pt * 128:(pt + 1) * 128],
                                           du_sb[:, pt * Ts + tt * 128: pt * Ts + (tt + 1) * 128], eye)
                    c_dut.sig(inst)
                for nn in range(16):
                    par = nn % 2
                    c_ce.wait(t, B["ce"] + nn)
                    if nn >= 2:
                        c_yt.wait(t, B["yt"] + 4 * (nn - 1))
                    inst = t.matmul(p_cr, sel_sb[:, nn * 128:(nn + 1) * 128], cm_sb,
                                    start=True, stop=True)
                    c_cm.sig(inst)
                    c_dg.wait(t, B["dg"] + nn + 1)
                    c_due.wait(t, B["due"] + 4)
                    for pt in range(4):
                        if nn > 0 or pt >= 2:
                            c_sc.wait(t, B["sc"] + 4 * nn + pt - 1)
                        for tb in range(4):
                            inst = t.matmul(p_db[pt % 2][:, tb * 128:(tb + 1) * 128],
                                            duT[:, tb * Ts + pt * 128: tb * Ts + (pt + 1) * 128],
                                            dg[par][:, tb * 128:(tb + 1) * 128],
                                            start=True, stop=True)
                        c_db.sig(inst)
                    if nn > 0:
                        parp = (nn - 1) % 2
                        for pt in range(4):
                            c_yt.wait(t, B["yt"] + 4 * (nn - 1) + pt + 1)
                            c_ya.sig(t.matmul(p_y[pt], eye, yt_sb[parp][:, pt * Ts:(pt + 1) * Ts],
                                              start=(nn - 1 == 0), stop=False))
                for pt in range(4):
                    c_yt.wait(t, B["yt"] + 60 + pt + 1)
                    c_ya.sig(t.matmul(p_y[pt], eye, yt_sb[15 % 2][:, pt * Ts:(pt + 1) * Ts],
                                      start=False, stop=True))

            @blk.scalar
            def _(s, B=B):
                c_bmt.wait(s, B["bmt"] + 1)
                c_bev.sig(s.copy(bmT, p_tr[:, 0:64]))
                for tt in range(4):
                    c_dut.wait(s, B["dut"] + tt + 1)
                    c_due.sig(s.copy(duT[:, tt * Ts:(tt + 1) * Ts], p_tr))
                for nn in range(16):
                    par = nn % 2
                    if nn >= 2:
                        c_sc.wait(s, B["sc"] + 4 * (nn - 1))
                    for pt in range(4):
                        c_da.sig(s.activation(dA[par][:, pt * Ts:(pt + 1) * Ts],
                                              dt_sb[:, pt * Ts:(pt + 1) * Ts], AF.Exp,
                                              scale=A_sb[:, pt * 16 + nn: pt * 16 + nn + 1]))
                    if nn >= 2:
                        c_db.wait(s, B["db"] + 4 * (nn - 1))
                    for tb in range(4):
                        inst = s.activation(dg[par][:, tb * 128:(tb + 1) * 128], eye, AF.Copy,
                                            scale=bmT[:, tb * 16 + nn: tb * 16 + nn + 1])
                    c_dg.sig(inst)
                    c_cm.wait(s, B["cm"] + nn + 1)
                    if nn >= 2:
                        c_yt.wait(s, B["yt"] + 4 * (nn - 1))
                    c_ce.sig(s.copy(crep[par], p_cr))
                for pt in range(4):
                    c_ya.wait(s, B["ya"] + 64)
                    c_ye.sig(s.copy(y_sb[:, pt * Ts:(pt + 1) * Ts], p_y[pt]))

            @blk.vector
            def _(v, B=B):
                ld.wait(v, B["ld"] + 160)
                for pt in range(4):
                    c_du.sig(v.tensor_mul(du_sb[:, pt * Ts:(pt + 1) * Ts],
                                          dt_sb[:, pt * Ts:(pt + 1) * Ts],
                                          u_sb[:, pt * Ts:(pt + 1) * Ts]))
                for nn in range(16):
                    par = nn % 2
                    for pt in range(4):
                        c_da.wait(v, B["da"] + 4 * nn + pt + 1)
                        c_db.wait(v, B["db"] + 4 * nn + pt + 1)
                        if nn >= 2:
                            c_yt.wait(v, B["yt"] + 4 * (nn - 1))
                        c_stc.wait(v, B["stc"])
                        c_sc.sig(v.tensor_tensor_scan(h_sb[par][:, pt * Ts:(pt + 1) * Ts],
                                                      dA[par][:, pt * Ts:(pt + 1) * Ts],
                                                      p_db[pt % 2],
                                                      st_sb[:, pt * 16 + nn: pt * 16 + nn + 1],
                                                      ALU.mult, ALU.add))
                        c_sc.wait(v, c_sc.n)
                        c_stc.sig(v.tensor_copy(st_sb[:, pt * 16 + nn: pt * 16 + nn + 1],
                                                h_sb[par][:, (pt + 1) * Ts - 1:(pt + 1) * Ts]))

            @blk.gpsimd
            def _(g, B=B):
                for nn in range(16):
                    par = nn % 2
                    for pt in range(4):
                        c_sc.wait(g, B["sc"] + 4 * nn + pt + 1)
                        c_ce.wait(g, B["ce"] + nn + 1)
                        if nn >= 2:
                            c_ya.wait(g, B["ya"] + 4 * (nn - 1))
                        c_yt.sig(g.tensor_tensor(yt_sb[par][:, pt * Ts:(pt + 1) * Ts],
                                                 h_sb[par][:, pt * Ts:(pt + 1) * Ts],
                                                 crep[par], ALU.mult))

    with nc.Block("fin") as blk:
        @blk.sync
        def _(s):
            st.wait(s, st.n)
    return nc


def stage_l2(ins, l1_outs):
    A_log = ins["L0_attn_A_log"]
    maps = []
    for c in range(NCORE):
        b, dh = c // 2, c % 2
        u_full = np.concatenate([l1_outs[2 * b]["u_out"], l1_outs[2 * b + 1]["u_out"]], axis=1)
        dt_full = np.concatenate([l1_outs[2 * b]["dt_out"], l1_outs[2 * b + 1]["dt_out"]], axis=1)
        bc_full = np.concatenate([l1_outs[2 * b]["bc_out"], l1_outs[2 * b + 1]["bc_out"]], axis=1)
        r0 = dh * 512
        alog_r = np.ascontiguousarray(
            A_log[r0:r0 + 512].reshape(4, 128, 16).transpose(1, 0, 2).reshape(128, 64), dtype=np.float32)
        maps.append({"u_h": np.ascontiguousarray(u_full[r0:r0 + 512]),
                     "dt_h": np.ascontiguousarray(dt_full[r0:r0 + 512]),
                     "bc_h": np.ascontiguousarray(bc_full),
                     "alog_h": alog_r})
    return maps


def build_l3():
    nc = bass.Bass("TRN2", target_bir_lowering=False)
    Tc = 512
    NCH = NT // Tc

    y_in = nc.dram_tensor("y_t", [D, NT], F32, kind="ExternalInput").ap()
    sz_in = nc.dram_tensor("sz_t", [D, NT], F32, kind="ExternalInput").ap()
    x_in = nc.dram_tensor("x_t", [NT, D], F32, kind="ExternalInput").ap()
    wout_r = nc.dram_tensor("wout_r", [128, 8 * 1024], F32, kind="ExternalInput").ap()
    bout_r = nc.dram_tensor("bout_r", [1, 1024], F32, kind="ExternalInput").ap()
    x1_out = nc.dram_tensor("x1_t", [NT, D], F32, kind="ExternalOutput").ap()
    wout_bf_d = nc.dram_tensor("wout_bf_d", [128, 8 * 1024], BF16).ap()

    sb = lambda n, sh, dt=F32: nc.alloc_sbuf_tensor(n, sh, dt).ap()
    wout_sb = sb("wout_sb", [128, 8 * 1024], BF16)
    bout_sb = sb("bout_sb", [1, 1024])
    ones1 = sb("ones1", [1, 128])
    y_sb = sb("y_sb", [128, 8 * Tc])
    szt = sb("szt", [128, 8 * Tc])
    g_sb = sb("g_sb", [128, 8 * Tc], BF16)
    x_sb = sb("x_sb", [128, 4 * D])
    xs_sb = sb("xs_sb", [128, 4 * D])
    x1_sb = sb("x1_sb", [128, 4 * D])
    bns = sb("bns", [128, 12])
    mv = sb("mv", [128, 2])
    sd = sb("sd", [128, 1])
    rs = sb("rs", [128, 1])
    eps = sb("eps", [128, 1])

    pM = [nc.alloc_psum_tensor(f"pM{i}", [128, 512], F32).ap() for i in range(4)]

    ld = Ch(nc, "ld"); st = Ch(nc, "st"); cst = Ch(nc, "cst")
    c_g = Ch(nc, "c_g")
    c_mm = Ch(nc, "c_mm")
    c_ev = Ch(nc, "c_ev")
    c_v = Ch(nc, "c_v")
    c_a = Ch(nc, "c_a")
    c_x1 = Ch(nc, "c_x1")

    with nc.Block("prep0") as blk:
        @blk.gpsimd
        def _(g):
            cst.sig(g.dma_start(out=wout_bf_d, in_=wout_r), 16)

        @blk.sync
        def _(s):
            ld.sig(s.dma_start(bout_sb, bout_r), 16)
            ld.wait(s, ld.n)
            cst.wait(s, cst.n)
            ld.sig(s.dma_start(wout_sb, wout_bf_d), 16)
            ld.wait(s, ld.n)

        @blk.vector
        def _(v):
            v.memset(ones1, 1.0)
            v.memset(eps, 1e-5)

    for k in range(NCH):
        t0 = k * Tc
        B = dict(ld=ld.n, g=c_g.n, mm=c_mm.n, ev=c_ev.n, v=c_v.n, a=c_a.n,
                 x1=c_x1.n, st=st.n)
        with nc.Block(f"l3c{k}") as blk:
            @blk.sync
            def _(s, k=k, B=B, t0=t0):
                for dc in range(8):
                    ld.sig(s.dma_start(y_sb[:, dc * Tc:(dc + 1) * Tc],
                                       y_in[dc * 128:(dc + 1) * 128, t0:t0 + Tc]), 16)
                    ld.sig(s.dma_start(szt[:, dc * Tc:(dc + 1) * Tc],
                                       sz_in[dc * 128:(dc + 1) * 128, t0:t0 + Tc]), 16)
                for m in range(4):
                    ld.sig(s.dma_start(x_sb[:, m * D:(m + 1) * D],
                                       x_in[t0 + m * 128: t0 + (m + 1) * 128, :]), 16)
                for m in range(4):
                    c_x1.wait(s, B["x1"] + m + 1)
                    st.sig(s.dma_start(x1_out[t0 + m * 128: t0 + (m + 1) * 128, :],
                                       x1_sb[:, m * D:(m + 1) * D]), 16)
                st.wait(s, st.n)

            @blk.vector
            def _(v, B=B):
                ld.wait(v, B["ld"] + 20 * 16)
                for dc in range(8):
                    c_g.sig(v.tensor_mul(g_sb[:, dc * Tc:(dc + 1) * Tc],
                                         y_sb[:, dc * Tc:(dc + 1) * Tc],
                                         szt[:, dc * Tc:(dc + 1) * Tc]))
                for m in range(4):
                    for oh in range(2):
                        c_mm.wait(v, B["mm"] + 2 * m + oh + 1)
                        c_ev.sig(v.tensor_add(xs_sb[:, m * D + oh * 512: m * D + (oh + 1) * 512],
                                              pM[2 * (m % 2) + oh],
                                              x_sb[:, m * D + oh * 512: m * D + (oh + 1) * 512]))
                    c_ev.wait(v, c_ev.n)
                    c_v.sig(v.bn_stats(bns[:, 0:6], xs_sb[:, m * D: m * D + 512]))
                    c_v.sig(v.bn_stats(bns[:, 6:12], xs_sb[:, m * D + 512: m * D + 1024]))
                    c_v.wait(v, c_v.n)
                    c_v.sig(v.bn_aggr(mv, bns))
                    c_a.wait(v, B["a"] + m + 1)
                    c_v.sig(v.reciprocal(rs, sd))
                    c_v.wait(v, c_v.n)
                    c_x1.sig(v.tensor_scalar(x1_sb[:, m * D:(m + 1) * D],
                                             xs_sb[:, m * D:(m + 1) * D],
                                             mv[:, 0:1], rs, ALU.subtract, ALU.mult))

            @blk.tensor
            def _(t, B=B):
                for m in range(4):
                    for oh in range(2):
                        ps = pM[2 * (m % 2) + oh]
                        if m >= 2:
                            c_ev.wait(t, B["ev"] + 2 * (m - 1) + oh + 1)
                        for dc in range(8):
                            c_g.wait(t, B["g"] + dc + 1)
                            t.matmul(ps, g_sb[:, dc * Tc + m * 128: dc * Tc + m * 128 + 128],
                                     wout_sb[:, dc * 1024 + oh * 512: dc * 1024 + (oh + 1) * 512],
                                     start=(dc == 0), stop=False)
                        inst = t.matmul(ps, ones1, bout_sb[:, oh * 512:(oh + 1) * 512],
                                        start=False, stop=True)
                        c_mm.sig(inst)

            @blk.scalar
            def _(s, B=B):
                for m in range(4):
                    c_v.wait(s, B["v"] + 4 * m + 3)
                    c_a.sig(s.activation(sd, mv[:, 1:2], AF.Sqrt, bias=eps))

    with nc.Block("fin") as blk:
        @blk.sync
        def _(s):
            st.wait(s, st.n)
    return nc






class Em:
    KINDS = ("tensor", "scalar", "vector", "gpsimd", "sync")

    def __init__(self, nc):
        self.nc = nc
        self.qs = {}
        self.tasks = []
        self.wm = {k: {} for k in self.KINDS}

    def queue(self, name, kind, inc=1):
        q = dict(ch=Ch(self.nc, "q_" + name), kind=kind, inc=inc, created=0)
        if inc != 1:
            q["grp"] = Ch(self.nc, "g_" + name)
            q["open"] = 0
            q["gno"] = 0
        self.qs[name] = q
        return name

    def close_group(self, qn):
        q = self.qs[qn]
        if q.get("open", 0) == 0:
            return
        q["open"] = 0
        q["gno"] += 1
        total = q["created"] * q["inc"]

        def fn(eng, q=q, total=total):
            eng.wait_ge(q["ch"].sem, total)
            return eng.sem_inc(q["grp"].sem, 1)
        self.tasks.append((qn, fn, [], True))

    def t(self, qn, fn, deps=()):
        q = self.qs[qn]
        rdeps = []
        for d in deps:
            if d is None:
                continue
            dq, dk = d
            dqq = self.qs[dq]
            if dqq["inc"] != 1:
                self.close_group(dq)
                rdeps.append((dq + "~g", dqq["gno"]))
            else:
                rdeps.append((dq, dk))
        q["created"] += 1
        if q["inc"] != 1:
            q["open"] += 1
        tok = (qn, q["created"])
        self.tasks.append((qn, fn, rdeps, False))
        return tok

    def flush(self, blk):
        for qn, q in self.qs.items():
            if q.get("open", 0):
                self.close_group(qn)
        per = {k: [] for k in self.KINDS}
        for qn, fn, rdeps, isclose in self.tasks:
            per[self.qs[qn]["kind"]].append((qn, fn, rdeps, isclose))
        self.tasks = []

        def body(kind):
            items = per[kind]

            def run(eng):
                wm = self.wm[kind]
                for qn, fn, rdeps, isclose in items:
                    for dq, val in rdeps:
                        if wm.get(dq, 0) < val:
                            if dq.endswith("~g"):
                                eng.wait_ge(self.qs[dq[:-2]]["grp"].sem, val)
                            else:
                                eng.wait_ge(self.qs[dq]["ch"].sem, val)
                            wm[dq] = val
                    if isclose:
                        fn(eng)
                        continue
                    inst = fn(eng)
                    q = self.qs[qn]
                    inst.then_inc(q["ch"].sem, q["inc"])
                    q["ch"].n += q["inc"]
            return run

        for kind in self.KINDS:
            if per[kind]:
                getattr(blk, kind)(body(kind))

    def done(self, qn):
        return (qn, self.qs[qn]["created"])


class Res:

    def __init__(self, tok=None):
        self.w = tok
        self.rs = []

    def wdeps(self):
        d = list(self.rs)
        if self.w is not None:
            d.append(self.w)
        return d

    def rdep(self):
        return [self.w] if self.w is not None else []

    def write(self, tok):
        self.w = tok
        self.rs = []
        return tok

    def read(self, tok):
        self.rs.append(tok)
        return tok


H4 = 1024


def build_l4(final=False, ncap=None, trunc=99):
    nc = bass.Bass("TRN2", target_bir_lowering=False)
    Tc = 512
    NCH = (NT + H4) // Tc
    if ncap is not None:
        NCH = ncap
    HID = 2816
    NHh = 22

    x_in = nc.dram_tensor("xl_h", [NT + H4, D], F32, kind="ExternalInput").ap()
    wqkv_r = nc.dram_tensor("wqkv_r", [128, 8 * 1536], F32, kind="ExternalInput").ap()
    wo_r = nc.dram_tensor("wo_r", [128, 4 * 1024], F32, kind="ExternalInput").ap()
    w1a_r = nc.dram_tensor("w1a_r", [128, 8 * HID], F32, kind="ExternalInput").ap()
    w1g_r = nc.dram_tensor("w1g_r", [128, 8 * HID], F32, kind="ExternalInput").ap()
    w2_r = nc.dram_tensor("w2_r", [128, NHh * 1024], F32, kind="ExternalInput").ap()
    b1a_r = nc.dram_tensor("b1a_r", [128, NHh], F32, kind="ExternalInput").ap()
    b1g_r = nc.dram_tensor("b1g_r", [128, NHh], F32, kind="ExternalInput").ap()
    b2_r = nc.dram_tensor("b2_r", [1, 1024], F32, kind="ExternalInput").ap()
    lng_row = nc.dram_tensor("lng_row", [1, 1024], F32, kind="ExternalInput").ap()
    lnb_row = nc.dram_tensor("lnb_row", [1, 1024], F32, kind="ExternalInput").ap()
    qs_row = nc.dram_tensor("qs_row", [1, 64], F32, kind="ExternalInput").ap()
    ks_row = nc.dram_tensor("ks_row", [1, 64], F32, kind="ExternalInput").ap()
    pos_r = nc.dram_tensor("pos_r", [128, NCH * 4], F32, kind="ExternalInput").ap()
    mf_r = nc.dram_tensor("mf_r", [128, 4 * 640], BF16, kind="ExternalInput").ap()
    shm_r = nc.dram_tensor("shm_r", [128, 1], F32, kind="ExternalInput").ap()
    if final:
        mres_in = nc.dram_tensor("mres", [NT, D], F32, kind="ExternalInput").ap()
    xo = nc.dram_tensor("xo_t", [NT, D], F32, kind="ExternalOutput").ap()

    wqkv_bf = nc.dram_tensor("wqkv_bf", [128, 8 * 1536], BF16).ap()
    wo_bf = nc.dram_tensor("wo_bf", [128, 4 * 1024], BF16).ap()
    w1a_bf = nc.dram_tensor("w1a_bf", [128, 8 * HID], BF16).ap()
    w1g_bf = nc.dram_tensor("w1g_bf", [128, 8 * HID], BF16).ap()
    w2_bf = nc.dram_tensor("w2_bf", [128, NHh * 1024], BF16).ap()

    sb = lambda n, sh, dt=F32: nc.alloc_sbuf_tensor(n, sh, dt).ap()
    eye = sb("eye", [128, 128]); eyebf = sb("eyebf", [128, 128], BF16)
    ones1 = sb("ones1", [1, 128]); eps = sb("eps", [128, 1]); e12 = sb("e12", [128, 8])
    maskL = sb("maskL", [128, 128]); maskR = sb("maskR", [128, 128])
    shA = sb("shA", [128, 128]); shB = sb("shB", [128, 128])
    wqkv_sb = sb("wqkv_sb", [128, 8 * 1536], BF16)
    wo_sb = sb("wo_sb", [128, 4 * 1024], BF16)
    w2s_sb = [sb(f"w2s_sb{i}", [128, 1024], BF16) for i in range(2)]
    w1a_sb = [sb(f"w1a_sb{i}", [128, 8 * 128], BF16) for i in range(2)]
    w1g_sb = [sb(f"w1g_sb{i}", [128, 8 * 128], BF16) for i in range(2)]
    b1a_sb = sb("b1a_sb", [128, NHh]); b1g_sb = sb("b1g_sb", [128, NHh])
    b2_sb = sb("b2_sb", [1, 1024])
    gam_rep = sb("gam_rep", [128, 1024]); bet_rep = sb("bet_rep", [128, 1024])
    lngrow_sb = sb("lngrow_sb", [1, 1024]); lnbrow_sb = sb("lnbrow_sb", [1, 1024])
    qsrow_sb = sb("qsrow_sb", [1, 64]); ksrow_sb = sb("ksrow_sb", [1, 64])
    qscrep = sb("qscrep", [128, 64]); kscrep = sb("kscrep", [128, 64])
    pos_sb = sb("pos_sb", [128, NCH * 4])
    mf_sb = sb("mf_sb", [128, 4 * 640], BF16)
    shm_sb = sb("shm_sb", [128, 1])
    invf = sb("invf", [128, 32]); lnbase = sb("lnbase", [128, 32])
    frq = sb("frq", [128, 4 * 32])
    rrf = sb("rrf", [128, 4 * 32]); ang = sb("ang", [128, 4 * 32]); angc = sb("angc", [128, 4 * 32])
    rri = nc.alloc_sbuf_tensor("rri", [128, 4 * 32], I32).ap()
    halfpi = sb("halfpi", [128, 1])
    snt = sb("snt", [128, 4 * 32]); cstt = sb("cstt", [128, 4 * 32])
    pwv = sb("pwv", [128, 4]); sclog = sb("sclog", [128, 4 * 32])
    scl = sb("scl", [128, 4 * 32]); iscl = sb("iscl", [128, 4 * 32])
    csx = sb("csx", [128, 4 * 32]); ssx = sb("ssx", [128, 4 * 32])
    cski = sb("cski", [128, 4 * 32]); sski = sb("sski", [128, 4 * 32])
    csq2 = sb("csq2", [128, 4 * 64]); nsq2 = sb("nsq2", [128, 4 * 32]); sq2 = sb("sq2", [128, 4 * 32])
    csk2 = sb("csk2", [128, 4 * 64]); nsk2 = sb("nsk2", [128, 4 * 32]); sk2 = sb("sk2", [128, 4 * 32])
    x_sb = sb("x_sb", [128, 4 * D])
    x1T = sb("x1T", [128, 8 * (Tc + 1)], BF16)
    q_sb = sb("q_sb", [128, 512]); k_sb = sb("k_sb", [128, 512])
    qss = sb("qss", [128, 8]); kss = sb("kss", [128, 8])
    rq = sb("rq", [128, 8]); rk = sb("rk", [128, 8])
    qr2 = sb("qr2", [128, 512]); kr2 = sb("kr2", [128, 512])
    tt1 = sb("tt1", [128, 512]); tt2 = sb("tt2", [128, 512])
    qT = sb("qT", [128, 512], BF16)
    NRING = 6
    kring = [sb(f"kring{i}", [128, 512], BF16) for i in range(NRING)]
    vring = [sb(f"vring{i}", [128, 512], BF16) for i in range(NRING)]
    s8 = sb("s8", [128, 8]); nm = sb("nm", [128, 1]); nmz = sb("nmz", [128, 1])
    P_sb = sb("P_sb", [128, 640])
    PT_sb = sb("PT_sb", [128, 5 * 128], BF16)
    sume = sb("sume", [128, 1]); rsume = sb("rsume", [128, 1])
    attn = sb("attn", [128, 512])
    attnT = sb("attnT", [128, 512], BF16)
    x2_sb = sb("x2_sb", [128, 4 * D])
    x2tail = sb("x2tail", [128, D])
    bns = sb("bns", [128, 12]); mv = sb("mv", [128, 2]); sd = sb("sd", [128, 1]); rs = sb("rs", [128, 1])
    hh_sb = sb("hh_sb", [128, 4 * D], BF16)
    hT = sb("hT", [128, 8 * 512], BF16)
    a_sb = sb("a_sb", [128, 512]); gg_sb = sb("gg_sb", [128, 512])
    gate = sb("gate", [128, NHh * 512], BF16)
    x3_sb = sb("x3_sb", [128, D])
    mres_sb = x_sb[:, 0:D]

    p1 = nc.alloc_psum_tensor("p1", [128, 512], F32).ap()
    p2 = nc.alloc_psum_tensor("p2", [128, 512], F32).ap()
    p3 = nc.alloc_psum_tensor("p3", [128, 512], F32).ap()
    pS = nc.alloc_psum_tensor("pS", [128, 1024], F32).ap()
    pT = nc.alloc_psum_tensor("pT", [128, 512], F32).ap()
    pVa = nc.alloc_psum_tensor("pVa", [128, 512], F32).ap()
    pVb = nc.alloc_psum_tensor("pVb", [128, 512], F32).ap()

    em = Em(nc)
    for qn, kind, inc in [("pe", "tensor", 1), ("act", "scalar", 1), ("dve", "vector", 1),
                          ("gp", "gpsimd", 1), ("ld", "sync", 16), ("ldw", "sync", 16),
                          ("st", "sync", 16), ("gc", "gpsimd", 16)]:
        em.queue(qn, kind, inc)

    R = lambda: Res()
    r_eye = R(); r_masks = R(); r_sh = R(); r_invf = R(); r_lnb = R()
    r_gam = R(); r_bet = R(); r_qsc = R(); r_ksc = R()
    r_x = [R() for _ in range(4)]
    r_x1T = [R() for _ in range(8)]
    r_q = R(); r_k = R(); r_qss = R(); r_kss = R(); r_rq = R(); r_rk = R()
    r_qr2 = R(); r_kr2 = R(); r_tt1 = R(); r_tt2 = R()
    r_qT = R(); r_kring = [R() for _ in range(NRING)]; r_vring = [R() for _ in range(NRING)]
    r_s8 = R(); r_nm = R(); r_nmz = R(); r_P = R(); r_PT = R(); r_sume = R(); r_rsume = R()
    r_attn = R(); r_attnT = R()
    r_x2 = [R() for _ in range(4)]; r_x2tail = R()
    r_bns = R(); r_mv = R(); r_sd = R(); r_rs = R()
    r_hh = [R() for _ in range(4)]; r_hT = R()
    r_a = R(); r_gg = R(); r_gate = [R() for _ in range(NHh)]
    r_x3 = R(); r_xf = R(); r_mres = R()
    r_w1a = [R(), R()]; r_w1g = [R(), R()]
    r_p1 = R(); r_p2 = R(); r_p3 = R(); r_pS = R(); r_pT = R(); r_pV = [R(), R()]
    r_tabs = R()
    r_lns = R()

    tcast = []
    for dst, src_ in [(wqkv_bf, wqkv_r), (wo_bf, wo_r), (w1a_bf, w1a_r),
                      (w1g_bf, w1g_r), (w2_bf, w2_r)]:
        tcast.append(em.t("gc", lambda g, d=dst, s_=src_: g.dma_start(out=d, in_=s_)))
    tld = []
    for dst, src_ in [(b1a_sb, b1a_r), (b1g_sb, b1g_r), (b2_sb, b2_r),
                      (pos_sb, pos_r), (mf_sb, mf_r), (shm_sb, shm_r),
                      (lngrow_sb, lng_row), (lnbrow_sb, lnb_row),
                      (qsrow_sb, qs_row), (ksrow_sb, ks_row)]:
        tld.append(em.t("ld", lambda s, d=dst, s_=src_: s.dma_start(d, s_)))
    t_wq = em.t("ld", lambda s: s.dma_start(wqkv_sb, wqkv_bf), [tcast[0]])
    t_wo = em.t("ld", lambda s: s.dma_start(wo_sb, wo_bf), [tcast[1]])
    r_wq = Res(t_wq); r_wo = Res(t_wo)
    r_w2s = [Res(), Res()]

    t0_ = em.t("dve", lambda v: v.memset(ones1, 1.0))
    em.t("dve", lambda v: v.memset(eps, 1e-5))
    em.t("dve", lambda v: v.memset(e12, 1e-12))
    t_hp = em.t("dve", lambda v: v.memset(halfpi, float(np.pi / 2)))
    r_x2tail.write(em.t("dve", lambda v: v.memset(x2tail, 0.0)))
    for i in range(8):
        r_x1T[i].write(em.t("dve", lambda v, i=i: v.memset(x1T[:, i * (Tc + 1):i * (Tc + 1) + 1], 0.0)))
    t_scr = em.t("dve", lambda v: v.memset(attn[:, 0:128], 1.0))
    t_mL0 = em.t("dve", lambda v: v.memset(maskL, 0.0))
    t_mR0 = em.t("dve", lambda v: v.memset(maskR, 0.0))
    r_eye.write(em.t("gp", lambda g: g.affine_select(eye, attn[:, 0:128], pattern=[[1, 128]],
                                                     compare_op=ALU.is_equal, fill=0.0,
                                                     base=0, channel_multiplier=-1), [t_scr]))
    r_sh.write(em.t("gp", lambda g: g.affine_select(shA, attn[:, 0:128], pattern=[[1, 128]],
                                                    compare_op=ALU.is_equal, fill=0.0,
                                                    base=-1, channel_multiplier=-1), [t_scr]))
    t_shb0 = em.t("gp", lambda g: g.memset(shB, 0.0))
    r_sh.write(em.t("gp", lambda g: g.affine_select(shB[:, 0:1], attn[:, 0:1],
                                                    pattern=[[1, 1]], compare_op=ALU.is_equal,
                                                    fill=0.0, base=-127, channel_multiplier=1),
               [t_scr, t_shb0]))
    r_masks.write(em.t("gp", lambda g: g.affine_select(maskL, maskL, pattern=[[1, 128]],
                                                       compare_op=ALU.is_ge, fill=NEG,
                                                       base=0, channel_multiplier=-1), [t_mL0]))
    r_masks.write(em.t("gp", lambda g: g.affine_select(maskR, maskR, pattern=[[-1, 128]],
                                                       compare_op=ALU.is_ge, fill=NEG,
                                                       base=0, channel_multiplier=1), [t_mR0]))
    r_attn.write(t_scr)
    t_i1 = em.t("gp", lambda g: g.iota(invf, pattern=[[1, 32]], base=0, channel_multiplier=0,
                                       allow_small_or_imprecise_dtypes=True))
    t_i2 = em.t("gp", lambda g: g.iota(lnbase, pattern=[[1, 32]], base=0, channel_multiplier=0,
                                       allow_small_or_imprecise_dtypes=True))
    r_invf.write(em.t("act", lambda s: s.activation(invf, invf, AF.Exp,
                                                    scale=float(-np.log(10000.0) / 32.0)), [t_i1]))
    r_eye.write(em.t("act", lambda s: s.copy(eyebf, eye), [r_eye.w]))
    t_lb0 = em.t("dve", lambda v: v.tensor_scalar(lnbase, lnbase, 2.0, 25.6, ALU.mult, ALU.add), [t_i2])
    t_lb = em.t("act", lambda s: s.activation(lnbase, lnbase, AF.Ln), [t_lb0])
    r_lnb.write(em.t("dve", lambda v: v.tensor_scalar(lnbase, lnbase, float(np.log(89.6)),
                                                      None, ALU.subtract), [t_lb]))
    for (dst, src_, res) in [(gam_rep, lngrow_sb, r_gam), (bet_rep, lnbrow_sb, r_bet)]:
        for oh in range(2):
            tm = em.t("pe", lambda t, d=dst, s_=src_, oh=oh: t.matmul(
                pT, ones1, s_[:, oh * 512:(oh + 1) * 512], start=True, stop=True),
                [t0_, tld[6], tld[7]] + r_pT.wdeps())
            r_pT.write(tm)
            te = em.t("act", lambda s, d=dst, oh=oh: s.copy(d[:, oh * 512:(oh + 1) * 512], pT), [tm])
            r_pT.read(te)
            res.write(te)
    for (dst, src_, res) in [(qscrep, qsrow_sb, r_qsc), (kscrep, ksrow_sb, r_ksc)]:
        tm = em.t("pe", lambda t, d=dst, s_=src_: t.matmul(pT[:, 0:64], ones1, s_,
                                                           start=True, stop=True),
                  [t0_, tld[8], tld[9]] + r_pT.wdeps())
        r_pT.write(tm)
        te = em.t("act", lambda s, d=dst: s.copy(d, pT[:, 0:64]), [tm])
        r_pT.read(te)
        res.write(te)

    TWO_PI = float(2 * np.pi)

    for ci in range(NCH):
        own = ci >= 2
        attn_tiles = [3] if ci == 1 else ([0, 1, 2, 3] if own else [])
        t0 = ci * Tc
        oc0 = (ci - 2) * Tc

        tlx = []
        for m in range(4):
            tk = em.t("ld", lambda s, m=m, t0=t0: s.dma_start(
                x_sb[:, m * D:(m + 1) * D], x_in[t0 + m * 128: t0 + (m + 1) * 128, :]),
                r_x[m].wdeps())
            r_x[m].write(tk)
            tlx.append(tk)

        for dj in range(8):
            deps = r_pT.wdeps() + [r_x[m].w for m in range(4)] + r_eye.rdep()
            tm = None
            def mk_trx(dj):
                def f(t):
                    for m in range(4):
                        inst = t.transpose(pT[:, m * 128:(m + 1) * 128],
                                           x_sb[:, m * D + dj * 128: m * D + (dj + 1) * 128], eye)
                    return inst
                return f
            tm = em.t("pe", mk_trx(dj), deps)
            r_pT.write(tm)
            for m in range(4):
                r_x[m].read(tm)
            te = em.t("act", lambda s, dj=dj: s.copy(
                x1T[:, dj * (Tc + 1) + 1: dj * (Tc + 1) + 1 + Tc], pT),
                [tm] + r_x1T[dj].wdeps())
            r_pT.read(te)
            r_x1T[dj].write(te)

        last = (ci == NCH - 1)
        if last and trunc < 1:
            continue
        if last and trunc == 11:
            continue
        pslice = pos_sb[:, ci * 4:(ci + 1) * 4]
        bc4 = lambda t_, n=32: t_.unsqueeze(1).broadcast_to([128, 4, n])
        v4 = lambda t_, n=32: t_.rearrange("p (a f) -> p a f", a=4)
        if last and trunc == 12:
            continue
        tF = em.t("dve", lambda v, ps_=pslice: v.tensor_tensor(v4(frq), bc4(invf),
                                                   ps_.unsqueeze(2).broadcast_to([128, 4, 32]),
                                                   ALU.mult),
                  r_invf.rdep() + [tld[3]] + r_tabs.wdeps())
        C1H, C1M, C1L = 6.283203125, -1.7642974853515625e-05, -1.7484555314695172e-07
        t1 = em.t("dve", lambda v: v.tensor_scalar(rrf, frq, float(1.0 / TWO_PI), 0.5,
                                                   ALU.mult, ALU.add), [tF])
        t2 = em.t("dve", lambda v: v.tensor_copy(rri, rrf), [t1])
        t3 = em.t("dve", lambda v: v.tensor_copy(rrf, rri), [t2])
        t4a = em.t("dve", lambda v: v.scalar_tensor_tensor(ang, rrf, -C1H, frq, ALU.mult, ALU.add), [t3])
        t4b = em.t("dve", lambda v: v.scalar_tensor_tensor(angc, rrf, -C1M, ang, ALU.mult, ALU.add), [t4a])
        t4c = em.t("dve", lambda v: v.scalar_tensor_tensor(ang, rrf, -C1L, angc, ALU.mult, ALU.add), [t4b])
        t4d = em.t("dve", lambda v: v.tensor_scalar(rrf, ang, float(np.pi), None, ALU.is_gt), [t4c])
        t4e = em.t("dve", lambda v: v.scalar_tensor_tensor(ang, rrf, -TWO_PI, ang, ALU.mult, ALU.add), [t4d])
        t4f = em.t("dve", lambda v: v.tensor_scalar(rrf, ang, float(-np.pi), None, ALU.is_lt), [t4e])
        t4 = em.t("dve", lambda v: v.scalar_tensor_tensor(ang, rrf, TWO_PI, ang, ALU.mult, ALU.add), [t4f])
        tS = em.t("act", lambda s: s.activation(snt, ang, AF.Sin), [t4])
        tCa = em.t("act", lambda s: s.activation(angc, ang, AF.Abs), [t4])
        tC = em.t("act", lambda s: s.activation(cstt, angc, AF.Sin, scale=-1.0, bias=halfpi), [tCa, t_hp])
        tPW = em.t("dve", lambda v, ps_=pslice: v.tensor_scalar(pwv, ps_, 4096.0, float(1.0 / 256.0),
                                                    ALU.subtract, ALU.mult), [tld[3]] + r_tabs.wdeps())
        tSL = em.t("dve", lambda v: v.tensor_tensor(v4(sclog), bc4(lnbase),
                                                    pwv.unsqueeze(2).broadcast_to([128, 4, 32]),
                                                    ALU.mult), [tPW] + r_lnb.rdep())
        tE1 = em.t("act", lambda s: s.activation(scl, sclog, AF.Exp), [tSL])
        tE2 = em.t("act", lambda s: s.activation(iscl, sclog, AF.Exp, scale=-1.0), [tSL])
        tcs = em.t("dve", lambda v: v.tensor_mul(csx, cstt, scl), [tC, tE1])
        tss = em.t("dve", lambda v: v.tensor_mul(ssx, snt, scl), [tS, tE1])
        tck = em.t("dve", lambda v: v.tensor_mul(cski, cstt, iscl), [tC, tE2])
        tsk = em.t("dve", lambda v: v.tensor_mul(sski, snt, iscl), [tS, tE2])
        qv = r_qsc.rdep(); kv = r_ksc.rdep()
        tq1 = em.t("dve", lambda v: v.tensor_tensor(csq2.rearrange("p (a f) -> p a f", a=4),
                                                    bc4(qscrep, 64).rearrange("p a f -> p a f") if False else csq2.rearrange("p (a f) -> p a f", a=4),
                                                    csq2.rearrange("p (a f) -> p a f", a=4), ALU.mult), []) if False else None
        tq1 = em.t("dve", lambda v: v.tensor_tensor(
            csq2.rearrange("p (a b f) -> p a b f", a=4, b=2),
            csx.rearrange("p (a f) -> p a f", a=4).unsqueeze(2).broadcast_to([128, 4, 2, 32]),
            qscrep.rearrange("p (b f) -> p b f", b=2).unsqueeze(1).broadcast_to([128, 4, 2, 32]),
            ALU.mult), [tcs] + qv + r_tabs.wdeps())
        tq2 = em.t("dve", lambda v: v.scalar_tensor_tensor(
            nsq2.rearrange("p (a f) -> p a f", a=4),
            ssx.rearrange("p (a f) -> p a f", a=4), -1.0,
            qscrep[:, 32:64].unsqueeze(1).broadcast_to([128, 4, 32]),
            ALU.mult, ALU.mult), [tss] + qv)
        tq3 = em.t("dve", lambda v: v.tensor_tensor(
            sq2.rearrange("p (a f) -> p a f", a=4),
            ssx.rearrange("p (a f) -> p a f", a=4),
            qscrep[:, 0:32].unsqueeze(1).broadcast_to([128, 4, 32]),
            ALU.mult), [tss] + qv)
        tk1 = em.t("dve", lambda v: v.tensor_tensor(
            csk2.rearrange("p (a b f) -> p a b f", a=4, b=2),
            cski.rearrange("p (a f) -> p a f", a=4).unsqueeze(2).broadcast_to([128, 4, 2, 32]),
            kscrep.rearrange("p (b f) -> p b f", b=2).unsqueeze(1).broadcast_to([128, 4, 2, 32]),
            ALU.mult), [tck] + kv)
        tk2 = em.t("dve", lambda v: v.scalar_tensor_tensor(
            nsk2.rearrange("p (a f) -> p a f", a=4),
            sski.rearrange("p (a f) -> p a f", a=4), -1.0,
            kscrep[:, 32:64].unsqueeze(1).broadcast_to([128, 4, 32]),
            ALU.mult, ALU.mult), [tsk] + kv)
        tk3 = em.t("dve", lambda v: v.tensor_tensor(
            sk2.rearrange("p (a f) -> p a f", a=4),
            sski.rearrange("p (a f) -> p a f", a=4),
            kscrep[:, 0:32].unsqueeze(1).broadcast_to([128, 4, 32]),
            ALU.mult), [tsk] + kv)
        t_tabs = [tq1, tq2, tq3, tk1, tk2, tk3]
        r_tabs.write(t_tabs[-1])
        for tt in t_tabs[:-1]:
            r_tabs.read(tt)

        if last and trunc < 2:
            continue
        for m in range(4):
            g_abs = ci * 4 + m
            slot = g_abs % NRING
            pqs = [r_p1, r_p2, r_p3]
            pps = [p1, p2, p3]
            tqkv = []
            for v3 in range(3):
                deps = ([r_x1T[dj].w for dj in range(8)] + pqs[v3].wdeps()
                        + r_wq.rdep())
                def mk_qkv(m=m, v3=v3):
                    def f(t):
                        for dc in range(8):
                            col = dc * (Tc + 1) + (1 if dc < 4 else 0) + m * 128
                            inst = t.matmul(pps[v3], x1T[:, col: col + 128],
                                            wqkv_sb[:, dc * 1536 + v3 * 512: dc * 1536 + (v3 + 1) * 512],
                                            start=(dc == 0), stop=(dc == 7))
                        return inst
                    return f
                tk_ = em.t("pe", mk_qkv(), deps)
                pqs[v3].write(tk_)
                for dj in range(8):
                    r_x1T[dj].read(tk_)
                r_wq.read(tk_)
                tqkv.append(tk_)
            te_q = em.t("act", lambda s: s.copy(q_sb, p1), [tqkv[0]] + r_q.wdeps())
            r_p1.read(te_q); r_q.write(te_q)
            te_k = em.t("act", lambda s: s.copy(k_sb, p2), [tqkv[1]] + r_k.wdeps())
            r_p2.read(te_k); r_k.write(te_k)
            te_v = em.t("act", lambda s, slot=slot: s.copy(vring[slot], p3),
                        [tqkv[2]] + r_vring[slot].wdeps())
            r_p3.read(te_v); r_vring[slot].write(te_v)

            tsq = em.t("act", lambda s: s.activation(tt1, q_sb, AF.Square),
                       [te_q] + r_tt1.wdeps())
            r_tt1.write(tsq); r_q.read(tsq)
            trd = em.t("dve", lambda v: v.reduce_sum(qss, tt1.rearrange("p (h c) -> p h c", h=8),
                                                     axis=AX.X), [tsq] + r_qss.wdeps())
            r_tt1.read(trd); r_qss.write(trd)
            tsq2 = em.t("act", lambda s: s.activation(tt2, k_sb, AF.Square),
                        [te_k] + r_tt2.wdeps())
            r_tt2.write(tsq2); r_k.read(tsq2)
            trd2 = em.t("dve", lambda v: v.reduce_sum(kss, tt2.rearrange("p (h c) -> p h c", h=8),
                                                      axis=AX.X), [tsq2] + r_kss.wdeps())
            r_tt2.read(trd2); r_kss.write(trd2)
            tsr = em.t("act", lambda s: s.activation(qss, qss, AF.Sqrt), [trd])
            r_qss.write(tsr)
            tmx = em.t("dve", lambda v: v.tensor_tensor(qss, qss, e12, ALU.max), [tsr])
            r_qss.write(tmx)
            trq = em.t("dve", lambda v: v.reciprocal(rq, qss), [tmx] + r_rq.wdeps())
            r_qss.read(trq); r_rq.write(trq)
            tsr2 = em.t("act", lambda s: s.activation(kss, kss, AF.Sqrt), [trd2])
            r_kss.write(tsr2)
            tmx2 = em.t("dve", lambda v: v.tensor_tensor(kss, kss, e12, ALU.max), [tsr2])
            r_kss.write(tmx2)
            trk = em.t("dve", lambda v: v.reciprocal(rk, kss), [tmx2] + r_rk.wdeps())
            r_kss.read(trk); r_rk.write(trk)

            vq = lambda t_: t_.rearrange("p (h b c) -> p h b c", h=8, b=2)
            tb = r_tabs.w
            ta1 = em.t("dve", lambda v, m=m: v.tensor_tensor(
                vq(tt1), vq(q_sb),
                csq2[:, m * 64:(m + 1) * 64].rearrange("p (b c) -> p b c", b=2)
                .unsqueeze(1).broadcast_to([128, 8, 2, 32]), ALU.mult),
                [te_q, tb] + r_tt1.wdeps())
            r_tt1.write(ta1); r_q.read(ta1)
            ta2 = em.t("dve", lambda v, m=m: v.tensor_tensor(
                tt2.rearrange("p (h b c) -> p h b c", h=8, b=2)[:, :, 0, :],
                q_sb.rearrange("p (h b c) -> p h b c", h=8, b=2)[:, :, 1, :],
                nsq2[:, m * 32:(m + 1) * 32].unsqueeze(1).broadcast_to([128, 8, 32]),
                ALU.mult), [te_q, tb] + r_tt2.wdeps())
            ta3 = em.t("dve", lambda v, m=m: v.tensor_tensor(
                tt2.rearrange("p (h b c) -> p h b c", h=8, b=2)[:, :, 1, :],
                q_sb.rearrange("p (h b c) -> p h b c", h=8, b=2)[:, :, 0, :],
                sq2[:, m * 32:(m + 1) * 32].unsqueeze(1).broadcast_to([128, 8, 32]),
                ALU.mult), [te_q, tb, ta2])
            r_tt2.write(ta3); r_q.read(ta3)
            ta4 = em.t("dve", lambda v: v.tensor_add(qr2, tt1, tt2), [ta1, ta3] + r_qr2.wdeps())
            r_tt1.read(ta4); r_tt2.read(ta4); r_qr2.write(ta4)
            tb1 = em.t("dve", lambda v, m=m: v.tensor_tensor(
                vq(tt1), vq(k_sb),
                csk2[:, m * 64:(m + 1) * 64].rearrange("p (b c) -> p b c", b=2)
                .unsqueeze(1).broadcast_to([128, 8, 2, 32]), ALU.mult),
                [te_k, tb, ta4] + r_tt1.wdeps())
            r_tt1.write(tb1); r_k.read(tb1)
            tb2 = em.t("dve", lambda v, m=m: v.tensor_tensor(
                tt2.rearrange("p (h b c) -> p h b c", h=8, b=2)[:, :, 0, :],
                k_sb.rearrange("p (h b c) -> p h b c", h=8, b=2)[:, :, 1, :],
                nsk2[:, m * 32:(m + 1) * 32].unsqueeze(1).broadcast_to([128, 8, 32]),
                ALU.mult), [te_k, tb, ta4] + r_tt2.wdeps())
            tb3 = em.t("dve", lambda v, m=m: v.tensor_tensor(
                tt2.rearrange("p (h b c) -> p h b c", h=8, b=2)[:, :, 1, :],
                k_sb.rearrange("p (h b c) -> p h b c", h=8, b=2)[:, :, 0, :],
                sk2[:, m * 32:(m + 1) * 32].unsqueeze(1).broadcast_to([128, 8, 32]),
                ALU.mult), [te_k, tb, tb2])
            r_tt2.write(tb3); r_k.read(tb3)
            tb4 = em.t("dve", lambda v: v.tensor_add(kr2, tt1, tt2), [tb1, tb3] + r_kr2.wdeps())
            r_tt1.read(tb4); r_tt2.read(tb4); r_kr2.write(tb4)
            tb5 = em.t("dve", lambda v: v.tensor_tensor(
                kr2.rearrange("p (h c) -> p h c", h=8),
                kr2.rearrange("p (h c) -> p h c", h=8),
                rk.unsqueeze(2).broadcast_to([128, 8, 64]), ALU.mult),
                [tb4, trk])
            r_rk.read(tb5); r_kr2.write(tb5)

            def mk_tr(src, m=m):
                def f(t):
                    for p4 in range(4):
                        inst = t.transpose(pT[:, p4 * 128:(p4 + 1) * 128],
                                           src[:, p4 * 128:(p4 + 1) * 128], eye)
                    return inst
                return f
            tqt = em.t("pe", mk_tr(qr2), [ta4] + r_pT.wdeps() + r_eye.rdep())
            r_pT.write(tqt); r_qr2.read(tqt)
            tqe = em.t("act", lambda s: s.copy(qT, pT), [tqt] + r_qT.wdeps())
            r_pT.read(tqe); r_qT.write(tqe)
            tkt = em.t("pe", mk_tr(kr2), [tb5] + r_pT.wdeps())
            r_pT.write(tkt); r_kr2.read(tkt)
            tke = em.t("act", lambda s, slot=slot: s.copy(kring[slot], pT),
                       [tkt] + r_kring[slot].wdeps())
            r_pT.read(tke); r_kring[slot].write(tke)

            if m in attn_tiles and not (last and trunc < 3):
                for h in range(8):
                    jj, pp = h % 2, h // 2
                    deps = (r_pS.wdeps() + [tqe] +
                            [r_kring[(g_abs - 4 + kc) % NRING].w for kc in range(5)] +
                            r_masks.rdep() + r_eye.rdep())
                    def mk_qk(g_abs=g_abs, jj=jj, pp=pp):
                        def f(t):
                            for kc in range(5):
                                slot2 = (g_abs - 4 + kc) % NRING
                                inst = t.matmul(pS[:, kc * 128:(kc + 1) * 128],
                                                qT[jj * 64:(jj + 1) * 64, pp * 128:(pp + 1) * 128],
                                                kring[slot2][jj * 64:(jj + 1) * 64, pp * 128:(pp + 1) * 128],
                                                start=True, stop=(kc not in (0, 4)))
                                if kc == 0:
                                    inst = t.matmul(pS[:, 0:128], eye, maskL, start=False, stop=True)
                                elif kc == 4:
                                    inst = t.matmul(pS[:, 512:640], eye, maskR, start=False, stop=True)
                            return inst
                        return f
                    tsc = em.t("pe", mk_qk(), deps)
                    r_pS.write(tsc)
                    r_qT.read(tsc)
                    for kc in range(5):
                        r_kring[(g_abs - 4 + kc) % NRING].read(tsc)
                    if ci == 2:
                        tmf = em.t("dve", lambda v, m=m: v.tensor_tensor(
                            pS.flatten()[0:0] if False else pS[:, 0:640],
                            pS[:, 0:640], mf_sb[:, m * 640:(m + 1) * 640], ALU.add),
                            [tsc, tld[4]])
                        r_pS.write(tmf)
                        tsc = tmf
                    tmx_ = em.t("dve", lambda v: v.reduce_max(nm, pS[:, 0:640], axis=AX.X,
                                                              negate=True),
                                [tsc] + r_nm.wdeps())
                    r_pS.read(tmx_); r_nm.write(tmx_)
                    tnm = em.t("dve", lambda v, h=h: v.tensor_tensor(nmz, nm, s8[:, h:h + 1],
                                                                     ALU.mult),
                               [tmx_, ts8] + r_nmz.wdeps())
                    r_nm.read(tnm); r_s8.read(tnm); r_nmz.write(tnm)
                    tex = em.t("act", lambda s, h=h: s.activation(P_sb, pS[:, 0:640], AF.Exp,
                                                             bias=nmz, scale=s8[:, h:h + 1],
                                                             accum_out=sume),
                               [tnm] + r_P.wdeps() + r_sume.wdeps())
                    r_pS.read(tex); r_nmz.read(tex); r_s8.read(tex)
                    r_P.write(tex); r_sume.write(tex)
                    trs = em.t("dve", lambda v: v.reciprocal(rsume, sume),
                               [tex] + r_rsume.wdeps())
                    r_sume.read(trs); r_rsume.write(trs)
                    def mk_ptr1():
                        def f(t):
                            for kc in range(4):
                                inst = t.transpose(pT[:, kc * 128:(kc + 1) * 128],
                                                   P_sb[:, kc * 128:(kc + 1) * 128], eye)
                            return inst
                        return f
                    tp1 = em.t("pe", mk_ptr1(), [tex] + r_pT.wdeps())
                    r_pT.write(tp1); r_P.read(tp1)
                    tpe1 = em.t("act", lambda s: s.copy(PT_sb[:, 0:512], pT),
                                [tp1] + r_PT.wdeps())
                    r_pT.read(tpe1); r_PT.write(tpe1)
                    tp2 = em.t("pe", lambda t: t.transpose(pT[:, 0:128], P_sb[:, 512:640], eye),
                               [tpe1] + r_pT.wdeps())
                    r_pT.write(tp2); r_P.read(tp2)
                    tpe2 = em.t("act", lambda s: s.copy(PT_sb[:, 512:640], pT[:, 0:128]), [tp2])
                    r_pT.read(tpe2); r_PT.write(tpe2)
                    pv = pVa if h % 2 == 0 else pVb
                    rpv = r_pV[h % 2]
                    def mk_pv(g_abs=g_abs, h=h, pv=pv):
                        def f(t):
                            for kc in range(5):
                                slot2 = (g_abs - 4 + kc) % NRING
                                inst = t.matmul(pv[:, 0:64], PT_sb[:, kc * 128:(kc + 1) * 128],
                                                vring[slot2][:, h * 64:(h + 1) * 64],
                                                start=(kc == 0), stop=(kc == 4))
                            return inst
                        return f
                    tpv = em.t("pe", mk_pv(), [tpe1, tpe2] + rpv.wdeps() +
                               [r_vring[(g_abs - 4 + kc) % NRING].w for kc in range(5)])
                    rpv.write(tpv); r_PT.read(tpv)
                    for kc in range(5):
                        r_vring[(g_abs - 4 + kc) % NRING].read(tpv)
                    tae = em.t("dve", lambda v, h=h, pv=pv: v.tensor_scalar(
                        attn[:, h * 64:(h + 1) * 64], pv[:, 0:64], rsume, None, ALU.mult),
                        [tpv, trs] + (r_attn.wdeps() if h == 0 else []))
                    rpv.read(tae); r_rsume.read(tae)
                    if h == 0:
                        r_attn.write(tae)
                    else:
                        r_attn.write(tae)
                tat = em.t("pe", mk_tr(attn), [r_attn.w] + r_pT.wdeps())
                r_pT.write(tat); r_attn.read(tat)
                tate = em.t("act", lambda s: s.copy(attnT, pT), [tat] + r_attnT.wdeps())
                r_pT.read(tate); r_attnT.write(tate)
                for oh in range(2):
                    pw_ = p1 if oh == 0 else p2
                    rw = r_p1 if oh == 0 else r_p2
                    def mk_wo(oh=oh, pw_=pw_):
                        def f(t):
                            for p4 in range(4):
                                inst = t.matmul(pw_, attnT[:, p4 * 128:(p4 + 1) * 128],
                                                wo_sb[:, p4 * 1024 + oh * 512: p4 * 1024 + (oh + 1) * 512],
                                                start=(p4 == 0), stop=(p4 == 3))
                            return inst
                        return f
                    two = em.t("pe", mk_wo(), [tate] + rw.wdeps() + r_wo.rdep())
                    rw.write(two); r_attnT.read(two); r_wo.read(two)
                    if ci == 1:
                        dst = x2tail
                        rd = r_x2tail
                    else:
                        dst = x2_sb
                        rd = r_x2[m]
                    off = (0 if ci == 1 else m * D) + oh * 512
                    tx2 = em.t("dve", lambda v, dst=dst, off=off, pw_=pw_, m=m, oh=oh: v.tensor_add(
                        dst[:, off:off + 512], pw_[:, 0:512],
                        x_sb[:, m * D + oh * 512: m * D + (oh + 1) * 512]),
                        [two, r_x[m].w] + rd.wdeps())
                    rw.read(tx2); r_x[m].read(tx2); rd.write(tx2)
                if ci == 1:
                    tshm = em.t("dve", lambda v: v.tensor_scalar(x2tail, x2tail, shm_sb[:, 0:1],
                                                                 None, ALU.mult),
                                [r_x2tail.w, tld[5]])
                    r_x2tail.write(tshm)

        if last and trunc == 13:
            continue
        for dj in range(8):
            tc_ = em.t("dve", lambda v, dj=dj: v.tensor_copy(
                x1T[:, dj * (Tc + 1): dj * (Tc + 1) + 1],
                x1T[:, dj * (Tc + 1) + Tc: dj * (Tc + 1) + Tc + 1]),
                r_x1T[dj].wdeps() + [r_x1T[dj].w])
            r_x1T[dj].write(tc_)

        if not own or (last and trunc < 4):
            continue

        for m in range(4):
            def mk_sh(m=m):
                def f(t):
                    inst = t.matmul(pS[:, 0:512], eye,
                                    x2_sb[:, m * D: m * D + 512],
                                    start=True, stop=True)
                    prev = (x2_sb[:, (m - 1) * D + 512:(m - 1) * D + 1024]
                            if m > 0 else x2tail[:, 512:1024])
                    inst = t.matmul(pS[:, 512:1024], shA,
                                    x2_sb[:, m * D + 512: m * D + 1024],
                                    start=True, stop=False)
                    inst = t.matmul(pS[:, 512:1024], shB, prev,
                                    start=False, stop=True)
                    return inst
                return f
            prev_res = r_x2[m - 1] if m > 0 else r_x2tail
            tsh = em.t("pe", mk_sh(), r_pS.wdeps() + [r_x2[m].w, prev_res.w] + r_sh.rdep())
            r_pS.write(tsh); r_x2[m].read(tsh); prev_res.read(tsh)
            tb1_ = em.t("dve", lambda v: v.bn_stats(bns[:, 0:6], pS[:, 0:512]),
                        [tsh] + r_lns.wdeps())
            tb2_ = em.t("dve", lambda v: v.bn_stats(bns[:, 6:12], pS[:, 512:1024]), [tsh, tb1_])
            tag = em.t("dve", lambda v: v.bn_aggr(mv, bns), [tb2_])
            tsd = em.t("act", lambda s: s.activation(sd, mv[:, 1:2], AF.Sqrt, bias=eps), [tag])
            trs_ = em.t("dve", lambda v: v.reciprocal(rs, sd), [tsd])
            tap = em.t("dve", lambda v, m=m: v.tensor_scalar(hh_sb[:, m * D:(m + 1) * D],
                                                             pS[:, 0:1024], mv[:, 0:1], rs,
                                                             ALU.subtract, ALU.mult),
                       [trs_] + r_hh[m].wdeps())
            r_pS.read(tap)
            tgm = em.t("dve", lambda v, m=m: v.tensor_mul(hh_sb[:, m * D:(m + 1) * D],
                                                          hh_sb[:, m * D:(m + 1) * D], gam_rep),
                       [tap] + r_gam.rdep())
            tbt = em.t("dve", lambda v, m=m: v.tensor_add(hh_sb[:, m * D:(m + 1) * D],
                                                          hh_sb[:, m * D:(m + 1) * D], bet_rep),
                       [tgm] + r_bet.rdep())
            r_lns.write(tbt)
            r_hh[m].write(tbt)
        ttail = em.t("dve", lambda v: v.tensor_copy(x2tail, x2_sb[:, 3 * D:4 * D]),
                     [r_x2[3].w] + r_x2tail.wdeps())
        r_x2[3].read(ttail)
        r_x2tail.write(ttail)
        pT_bf = pT.bitcast(BF16)
        for dj in range(8):
            def mk_ht(dj=dj):
                def f(t):
                    for m in range(4):
                        inst = t.transpose(pT_bf[:, m * 128:(m + 1) * 128],
                                           hh_sb[:, m * D + dj * 128: m * D + (dj + 1) * 128], eyebf)
                    return inst
                return f
            tht = em.t("pe", mk_ht(), [r_hh[m].w for m in range(4)] + r_pT.wdeps())
            r_pT.write(tht)
            for m in range(4):
                r_hh[m].read(tht)
            the = em.t("act", lambda s, dj=dj: s.copy(hT[:, dj * 512:(dj + 1) * 512],
                                                      pT_bf[:, 0:512]),
                       [tht] + ([] if dj > 0 else r_hT.wdeps()))
            r_pT.read(the)
            r_hT.write(the)
        for hh in range(NHh):
            pb = hh % 2
            tw1l = []
            for dc in range(8):
                tw1l.append(em.t("ldw", lambda s, dc=dc, hh=hh, pb=pb: s.dma_start(
                    w1a_sb[pb][:, dc * 128:(dc + 1) * 128],
                    w1a_bf[:, dc * HID + hh * 128: dc * HID + (hh + 1) * 128]),
                    r_w1a[pb].wdeps() + [tcast[2]]))
                tw1l.append(em.t("ldw", lambda s, dc=dc, hh=hh, pb=pb: s.dma_start(
                    w1g_sb[pb][:, dc * 128:(dc + 1) * 128],
                    w1g_bf[:, dc * HID + hh * 128: dc * HID + (hh + 1) * 128]),
                    r_w1g[pb].wdeps() + [tcast[3]]))
            r_w1a[pb].write(tw1l[-2]); r_w1g[pb].write(tw1l[-1])
            def mk_w1(wsb, pp_, pb=pb):
                def f(t):
                    for dc in range(8):
                        inst = t.matmul(pp_, wsb[pb][:, dc * 128:(dc + 1) * 128],
                                        hT[:, dc * 512:(dc + 1) * 512],
                                        start=(dc == 0), stop=(dc == 7))
                    return inst
                return f
            ta_ = em.t("pe", mk_w1(w1a_sb, p1), [tw1a, r_hT.w] + r_p1.wdeps())
            r_p1.write(ta_); r_w1a[pb].read(ta_); r_hT.read(ta_)
            tg_ = em.t("pe", mk_w1(w1g_sb, p2), [tw1g, r_hT.w] + r_p2.wdeps())
            r_p2.write(tg_); r_w1g[pb].read(tg_)
            tae_ = em.t("dve", lambda v, hh=hh: v.tensor_scalar(a_sb, p1, b1a_sb[:, hh:hh + 1],
                                                                None, ALU.add),
                        [ta_, tld[0]] + r_a.wdeps())
            r_p1.read(tae_); r_a.write(tae_)
            tge_ = em.t("act", lambda s, hh=hh: s.activation(gg_sb, p2, AF.Gelu,
                                                             bias=b1g_sb[:, hh:hh + 1]),
                        [tg_, tld[1]] + r_gg.wdeps())
            r_p2.read(tge_); r_gg.write(tge_)
            tgt_ = em.t("dve", lambda v, hh=hh: v.tensor_mul(gate[:, hh * 512:(hh + 1) * 512],
                                                             a_sb, gg_sb),
                        [tae_, tge_] + r_gate[hh].wdeps())
            r_a.read(tgt_); r_gg.read(tgt_); r_gate[hh].write(tgt_)
        def do_store(m):
            if final:
                tmr = em.t("ld", lambda s, m=m, oc0=oc0: s.dma_start(
                    mres_sb, mres_in[oc0 + m * 128: oc0 + (m + 1) * 128, :]),
                    r_x[0].wdeps())
                r_x[0].write(tmr)
                tfa = em.t("dve", lambda v: v.tensor_add(x3_sb, x3_sb, mres_sb),
                           [r_x3.w, tmr])
                r_x[0].read(tfa); r_x3.write(tfa)
                tb1f = em.t("dve", lambda v: v.bn_stats(bns[:, 0:6], x3_sb[:, 0:512]),
                            [tfa] + r_lns.wdeps())
                tb2f = em.t("dve", lambda v: v.bn_stats(bns[:, 6:12], x3_sb[:, 512:1024]),
                            [tfa, tb1f])
                tagf = em.t("dve", lambda v: v.bn_aggr(mv, bns), [tb2f])
                tsdf = em.t("act", lambda s: s.activation(sd, mv[:, 1:2], AF.Sqrt, bias=eps), [tagf])
                trsf = em.t("dve", lambda v: v.reciprocal(rs, sd), [tsdf])
                tapf = em.t("dve", lambda v: v.tensor_scalar(x3_sb, x3_sb, mv[:, 0:1], rs,
                                                             ALU.subtract, ALU.mult),
                            [trsf, r_x3.w])
                r_lns.write(tapf)
                r_x3.write(tapf)
                out_t = x3_sb
                out_r = r_x3
            else:
                out_t = x3_sb
                out_r = r_x3
            tst = em.t("st", lambda s, m=m, oc0=oc0, out_t=out_t: s.dma_start(
                xo[oc0 + m * 128: oc0 + (m + 1) * 128, :], out_t), [out_r.w])
            out_r.read(tst)

        for mp in range(2):
            pws = [p1, p2, p3, pVa]
            rws = [r_p1, r_p2, r_p3, r_pV[0]]
            tw2g = [None] * 4
            for hh in range(NHh):
                pb2 = hh % 2
                tl2 = em.t("ldw", lambda s, hh=hh, pb2=pb2: s.dma_start(
                    w2s_sb[pb2], w2_bf[:, hh * 1024:(hh + 1) * 1024]),
                    r_w2s[pb2].wdeps() + [tcast[4]])
                r_w2s[pb2].write(tl2)
                for mi2 in range(2):
                    m = 2 * mp + mi2
                    for oh in range(2):
                        sl = mi2 * 2 + oh
                        def mk_w2p(m=m, oh=oh, sl=sl, hh=hh, pb2=pb2):
                            def f(t):
                                return t.matmul(pws[sl][:, 0:512],
                                                gate[:, hh * 512 + m * 128: hh * 512 + (m + 1) * 128],
                                                w2s_sb[pb2][:, oh * 512:(oh + 1) * 512],
                                                start=(hh == 0), stop=False)
                            return f
                        deps = [tl2, r_gate[hh].w]
                        if hh == 0:
                            deps += rws[sl].wdeps()
                        tk2 = em.t("pe", mk_w2p(), deps)
                        if hh == 0:
                            rws[sl].write(tk2)
                        tw2g[sl] = tk2
                        r_gate[hh].read(tk2)
                        r_w2s[pb2].read(tk2)
            for sl in range(4):
                oh = sl % 2
                def mk_b2(sl=sl, oh=oh):
                    def f(t):
                        return t.matmul(pws[sl][:, 0:512], ones1,
                                        b2_sb[:, oh * 512:(oh + 1) * 512],
                                        start=False, stop=True)
                    return f
                tkb = em.t("pe", mk_b2(), [tw2g[sl], tld[2], t0_])
                rws[sl].write(tkb)
                tw2g[sl] = tkb
            for mi2 in range(2):
                m = 2 * mp + mi2
                for oh in range(2):
                    sl = mi2 * 2 + oh
                    tx3 = em.t("dve", lambda v, m=m, oh=oh, sl=sl: v.tensor_add(
                        x3_sb[:, oh * 512:(oh + 1) * 512], pws[sl][:, 0:512],
                        x2_sb[:, m * D + oh * 512: m * D + (oh + 1) * 512]),
                        [tw2g[sl], r_x2[m].w] + (r_x3.wdeps() if oh == 0 else [r_x3.w]))
                    rws[sl].read(tx3); r_x2[m].read(tx3); r_x3.write(tx3)
                do_store(m)

    with nc.Block("main") as blk:
        em.flush(blk)
    with nc.Block("fin") as blk:
        @blk.sync
        def _(s):
            s.wait_ge(em.qs["st"]["ch"].sem, em.qs["st"]["ch"].n)
    return nc


def stage_l4(ins, x_full, li, final=False, mres_full=None):
    wqkv = ins["L0_ffn_wqkv"][li]; wo_ = ins["L0_ffn_wo"][li]
    w1 = ins["L0_ffn_w1"][li]; b1 = ins["L0_ffn_b1"][li]
    w2 = ins["L0_ffn_w2"][li]; b2 = ins["L0_ffn_b2"][li]
    lng = ins["L0_ffn_ln_g"][li]; lnb = ins["L0_ffn_ln_b"][li]
    qs = ins["L0_ffn_q_scale"][li]; ks = ins["L0_ffn_k_scale"][li]
    HID = 2816; NHh = 22
    w1a = np.zeros((1024, HID), np.float32); w1a[:, :2730] = w1[:, :2730]
    w1g = np.zeros((1024, HID), np.float32); w1g[:, :2730] = w1[:, 2730:]
    b1a = np.zeros(HID, np.float32); b1a[:2730] = b1[:2730]
    b1g = np.zeros(HID, np.float32); b1g[:2730] = b1[2730:]
    w2p = np.zeros((HID, 1024), np.float32); w2p[:2730] = w2
    rsh = lambda w, r: np.ascontiguousarray(
        w.reshape(w.shape[0] // 128, 128, w.shape[1]).transpose(1, 0, 2)
        .reshape(128, (w.shape[0] // 128) * w.shape[1]), dtype=np.float32)
    wqkv_r = rsh(wqkv, 8)
    wo_r = rsh(wo_, 4)
    w1a_r = rsh(w1a, 8); w1g_r = rsh(w1g, 8)
    w2_r = rsh(w2p, NHh)
    b1a_r = np.ascontiguousarray(b1a.reshape(NHh, 128).T)
    b1g_r = np.ascontiguousarray(b1g.reshape(NHh, 128).T)
    b2_r = b2.reshape(1, 1024).astype(np.float32)
    lng_row = lng.reshape(1, 1024).astype(np.float32)
    lnb_row = lnb.reshape(1, 1024).astype(np.float32)
    qs_row = qs.reshape(1, 64).astype(np.float32)
    ks_row = ks.reshape(1, 64).astype(np.float32)
    NCH = (NT + H4) // 512
    maps = []
    for c in range(NCORE):
        b, half = c // 2, c % 2
        s0 = half * NT
        xs = np.zeros((NT + H4, D), np.float32)
        lo = s0 - H4
        src_lo = max(lo, 0)
        xs[src_lo - lo:, :] = x_full[b, src_lo: s0 + NT, :]
        pos = np.zeros((128, NCH * 4), np.float32)
        for ti in range(NCH * 4):
            pos[:, ti] = s0 - H4 + ti * 128 + np.arange(128)
        import ml_dtypes
        mf = np.zeros((128, 4 * 640), ml_dtypes.bfloat16)
        if half == 0:
            for m in range(4):
                cut = 512 - m * 128
                mf[:, m * 640: m * 640 + cut] = NEG
        shm = (np.ones((128, 1), np.float32) if half == 1
               else np.zeros((128, 1), np.float32))
        mp = {"xl_h": xs, "wqkv_r": wqkv_r, "wo_r": wo_r, "w1a_r": w1a_r,
              "w1g_r": w1g_r, "w2_r": w2_r, "b1a_r": b1a_r, "b1g_r": b1g_r,
              "b2_r": b2_r, "lng_row": lng_row, "lnb_row": lnb_row,
              "qs_row": qs_row, "ks_row": ks_row, "pos_r": pos, "mf_r": mf,
              "shm_r": shm}
        if final:
            mp["mres"] = np.ascontiguousarray(mres_full[b, s0:s0 + NT, :])
        maps.append(mp)
    return maps


def _stage_l3(ins, l1_outs, l2_outs):
    W_out = ins["L0_attn_W_out"]; b_out = ins["L0_attn_b_out"]
    x = ins["L0_attn_x"]
    wout_r = np.ascontiguousarray(
        W_out.reshape(8, 128, 1024).transpose(1, 0, 2).reshape(128, 8 * 1024), dtype=np.float32)
    bout_r = np.ascontiguousarray(b_out.reshape(1, 1024), dtype=np.float32)
    maps = []
    for c in range(NCORE):
        b, half = c // 2, c % 2
        s0 = half * NT
        y_full = np.concatenate([l2_outs[2 * b]["y_h"], l2_outs[2 * b + 1]["y_h"]], axis=0)
        maps.append({"y_t": np.ascontiguousarray(y_full[:, s0:s0 + NT]),
                     "sz_t": l1_outs[c]["sz_out"],
                     "x_t": np.ascontiguousarray(x[b, s0:s0 + NT, :]),
                     "wout_r": wout_r, "bout_r": bout_r})
    return maps


def kernel(**inputs):
    ins = inputs
    nc = build_l1()
    res1 = run_bass_kernel_spmd(nc, stage_l1(ins), core_ids=list(range(NCORE)))
    l1_outs = [res1.results[c] for c in range(NCORE)]
    nc = build_l2()
    res2 = run_bass_kernel_spmd(nc, stage_l2(ins, l1_outs), core_ids=list(range(NCORE)))
    l2_outs = [res2.results[c] for c in range(NCORE)]
    nc = build_l3()
    res3 = run_bass_kernel_spmd(nc, _stage_l3(ins, l1_outs, l2_outs),
                                core_ids=list(range(NCORE)))
    x1_full = np.empty((4, 2 * NT, D), np.float32)
    for c in range(NCORE):
        b, half = c // 2, c % 2
        x1_full[b, half * NT:(half + 1) * NT] = res3.results[c]["x1_t"]
    nc = build_l4()
    res4 = run_bass_kernel_spmd(nc, stage_l4(ins, x1_full, 0), core_ids=list(range(NCORE)))
    x2_full = np.empty((4, 2 * NT, D), np.float32)
    for c in range(NCORE):
        b, half = c // 2, c % 2
        x2_full[b, half * NT:(half + 1) * NT] = res4.results[c]["xo_t"]
    nc = build_l4(final=True)
    res5 = run_bass_kernel_spmd(nc, stage_l4(ins, x2_full, 1, final=True, mres_full=x1_full),
                                core_ids=list(range(NCORE)))
    out = np.empty((4, 2 * NT, D), np.float32)
    for c in range(NCORE):
        b, half = c // 2, c % 2
        out[b, half * NT:(half + 1) * NT] = res5.results[c]["xo_t"]
    return out



def emit_ffn(nc, em, V, S, li, W, x_at, out_dst, xrow_prev, xrow_new, tail_toks,
             coll2_tok, final):
    Tc = 512
    NCH = 10
    HID = 2816
    NHh = 22
    NRING = 6
    eye = V["eye"]; eyebf = V["eyebf"]; ones1 = V["ones1"]; eps = V["eps"]
    e12 = V["e12"]; halfpi = V["halfpi"]; maskL = V["maskL"]; maskR = V["maskR"]
    shA = V["shA"]; shB = V["shB"]
    wqkv_sb = V["wqkv_sb"]; wo_sb = V["wo_sb"]; w2s_sb = V["w2s_sb"]
    w1a_sb = V["w1a_sb"]; w1g_sb = V["w1g_sb"]
    b1a_sb = V["b1a_sb"]; b1g_sb = V["b1g_sb"]; b2_sb = V["b2_sb"]
    gam_rep = V["gam_rep"]; bet_rep = V["bet_rep"]
    lngrow_sb = V["lngrow_sb"]; lnbrow_sb = V["lnbrow_sb"]
    qsrow_sb = V["qsrow_sb"]; ksrow_sb = V["ksrow_sb"]
    qscrep = V["qscrep"]; kscrep = V["kscrep"]
    pos_sb = V["pos_sb"]; mf_sb = V["mf_sb"]; shm_sb = V["shm_sb"]
    invf = V["invf"]; lnbase = V["lnbase"]
    frq = V["frq"]; rrf = V["rrf"]; ang = V["ang"]; angc = V["angc"]; rri = V["rri"]
    snt = V["snt"]; cstt = V["cstt"]; pwv = V["pwv"]; sclog = V["sclog"]
    scl = V["scl"]; iscl = V["iscl"]; csx = V["csx"]; ssx = V["ssx"]
    cski = V["cski"]; sski = V["sski"]
    csq2 = V["csq2"]; nsq2 = V["nsq2"]; sq2 = V["sq2"]
    csk2 = V["csk2"]; nsk2 = V["nsk2"]; sk2 = V["sk2"]
    x_sb = V["x_sb"]; x1T = V["x1T"]
    q_sb = V["q_sb"]; k_sb = V["k_sb"]; qss = V["qss"]; kss = V["kss"]
    rq = V["rq"]; rk = V["rk"]; qr2 = V["qr2"]; kr2 = V["kr2"]
    tt1 = V["tt1"]; tt2 = V["tt2"]; qT = V["qT"]
    kring = V["kring"]; vring = V["vring"]
    s8 = V["s8"]; nm = V["nm"]; nmz = V["nmz"]
    P_sb = V["P_sb"]; PT_sb = V["PT_sb"]; sume = V["sume"]; rsume = V["rsume"]
    attn = V["attn"]; attnT = V["attnT"]
    x2_sb = V["x2_sb"]; x2tail = V["x2tail"]
    bns = V["bns"]; mv = V["mv"]; sd = V["sd"]; rs = V["rs"]
    hh_sb = V["hh_sb"]; hT = V["hT"]; a_sb = V["a_sb"]; gg_sb = V["gg_sb"]
    gate = V["gate"]; x3_sb = V["x3_sb"]; mres_sb = V["mres_sb"]
    mt0 = V["mt0"]; mt1 = V["mt1"]
    p1 = V["p1"]; p2 = V["p2"]; p3 = V["p3"]; pS = V["pS"]; pT = V["pT"]
    pVa = V["pVa"]; pVb = V["pVb"]
    collsem = V["collsem"]
    cstw = V["cstw"]

    r_eye = S["r_eye"]; r_masks = S["r_masks"]; r_sh = S["r_sh"]
    r_invf = S["r_invf"]; r_lnb = S["r_lnb"]
    r_gam = S["r_gam"]; r_bet = S["r_bet"]; r_qsc = S["r_qsc"]; r_ksc = S["r_ksc"]
    r_x = S["r_x"]; r_x1T = S["r_x1T"]
    r_q = S["r_q"]; r_k = S["r_k"]; r_qss = S["r_qss"]; r_kss = S["r_kss"]
    r_rq = S["r_rq"]; r_rk = S["r_rk"]
    r_qr2 = S["r_qr2"]; r_kr2 = S["r_kr2"]; r_tt1 = S["r_tt1"]; r_tt2 = S["r_tt2"]
    r_qT = S["r_qT"]; r_kring = S["r_kring"]; r_vring = S["r_vring"]
    r_s8 = S["r_s8"]; r_nm = S["r_nm"]; r_nmz = S["r_nmz"]
    r_P = S["r_P"]; r_PT = S["r_PT"]; r_sume = S["r_sume"]; r_rsume = S["r_rsume"]
    r_attn = S["r_attn"]; r_attnT = S["r_attnT"]
    r_x2 = S["r_x2"]; r_x2tail = S["r_x2tail"]
    r_hh = S["r_hh"]; r_hT = S["r_hT"]
    r_a = S["r_a"]; r_gg = S["r_gg"]; r_gate = S["r_gate"]
    r_x3 = S["r_x3"]
    r_w1a = S["r_w1a"]; r_w1g = S["r_w1g"]
    r_p1 = S["r_p1"]; r_p2 = S["r_p2"]; r_p3 = S["r_p3"]
    r_pS = S["r_pS"]; r_pT = S["r_pT"]; r_pV = S["r_pV"]
    r_tabs = S["r_tabs"]; r_lns = S["r_lns"]
    r_wq = S["r_wq"]; r_wo = S["r_wo"]; r_w2s = S["r_w2s"]
    r_b1a = S["r_b1a"]; r_b1g = S["r_b1g"]; r_b2 = S["r_b2"]
    r_lng = S["r_lng"]; r_lnbr = S["r_lnbr"]; r_qsr = S["r_qsr"]; r_ksr = S["r_ksr"]
    r_mf = S["r_mf"]

    def cwait(mk):
        def f(s):
            s.wait_ge(*cstw)
            return mk(s)
        return f
    t_wq = em.t("ld", cwait(lambda s: s.dma_start(wqkv_sb, W["wqkv_bf"])), r_wq.wdeps())
    r_wq.write(t_wq)
    t_wo = em.t("ld", lambda s: s.dma_start(wo_sb, W["wo_bf"]), r_wo.wdeps())
    r_wo.write(t_wo)
    t_b1a = em.t("ld", lambda s: s.dma_start(b1a_sb, W["b1a_r"]), r_b1a.wdeps())
    r_b1a.write(t_b1a)
    t_b1g = em.t("ld", lambda s: s.dma_start(b1g_sb, W["b1g_r"]), r_b1g.wdeps())
    r_b1g.write(t_b1g)
    t_b2 = em.t("ld", lambda s: s.dma_start(b2_sb, W["b2_r"]), r_b2.wdeps())
    r_b2.write(t_b2)
    t_lng = em.t("ld", lambda s: s.dma_start(lngrow_sb, W["lng_row"]), r_lng.wdeps())
    r_lng.write(t_lng)
    t_lnb = em.t("ld", lambda s: s.dma_start(lnbrow_sb, W["lnb_row"]), r_lnbr.wdeps())
    r_lnbr.write(t_lnb)
    t_qs = em.t("ld", lambda s: s.dma_start(qsrow_sb, W["qs_row"]), r_qsr.wdeps())
    r_qsr.write(t_qs)
    t_ks = em.t("ld", lambda s: s.dma_start(ksrow_sb, W["ks_row"]), r_ksr.wdeps())
    r_ksr.write(t_ks)
    if li == 0:
        t_mf = em.t("ld", lambda s: s.dma_start(mf_sb, W["mf_r"]), r_mf.wdeps())
        r_mf.write(t_mf)
    else:
        t_mf = r_mf.w
    tld = [t_b1a, t_b1g, t_b2, None, t_mf, None, t_lng, t_lnb, t_qs, t_ks]

    t_x2z = em.t("dve", lambda v: v.memset(x2tail, 0.0), r_x2tail.wdeps())
    r_x2tail.write(t_x2z)
    for i in range(8):
        tz = em.t("dve", lambda v, i=i: v.memset(
            x1T[:, i * (Tc + 1):i * (Tc + 1) + 1], 0.0), r_x1T[i].wdeps())
        r_x1T[i].write(tz)

    for (dst, src_, res, tl) in [(gam_rep, lngrow_sb, r_gam, t_lng),
                                 (bet_rep, lnbrow_sb, r_bet, t_lnb)]:
        for oh in range(2):
            tm = em.t("pe", lambda t, d=dst, s_=src_, oh=oh: t.matmul(
                pT, ones1, s_[:, oh * 512:(oh + 1) * 512], start=True, stop=True),
                [tl] + r_pT.wdeps() + res.wdeps())
            r_pT.write(tm)
            if res is r_lng or True:
                pass
            (r_lng if res is r_gam else r_lnbr).read(tm)
            te = em.t("act", lambda s, d=dst, oh=oh: s.copy(d[:, oh * 512:(oh + 1) * 512], pT), [tm])
            r_pT.read(te)
            res.write(te)
    for (dst, src_, res, tl, rsr) in [(qscrep, qsrow_sb, r_qsc, t_qs, r_qsr),
                                      (kscrep, ksrow_sb, r_ksc, t_ks, r_ksr)]:
        tm = em.t("pe", lambda t, d=dst, s_=src_: t.matmul(pT[:, 0:64], ones1, s_,
                                                           start=True, stop=True),
                  [tl] + r_pT.wdeps() + res.wdeps())
        r_pT.write(tm)
        rsr.read(tm)
        te = em.t("act", lambda s, d=dst: s.copy(d, pT[:, 0:64]), [tm])
        r_pT.read(te)
        res.write(te)

    TWO_PI = float(2 * np.pi)

    for ci in range(NCH):
        own = ci >= 2
        attn_tiles = [3] if ci == 1 else ([0, 1, 2, 3] if own else [])
        t0 = ci * Tc
        oc0 = (ci - 2) * Tc

        tlx = []
        for m in range(4):
            srcap = x_at(t0 + m * 128)
            deps = r_x[m].wdeps()
            if li == 1:
                if ci < 2:
                    deps = deps + [coll2_tok]
                else:
                    deps = deps + xrow_prev[ci - 2]
            if li == 0 and ci == 0:
                def mkx(dst, src):
                    def f(s):
                        s.wait_ge(collsem, 3)
                        return s.dma_start(dst, src)
                    return f
                tk = em.t("ld", mkx(x_sb[:, m * D:(m + 1) * D], srcap), deps)
            else:
                tk = em.t("ld", lambda s, m=m, srcap=srcap: s.dma_start(
                    x_sb[:, m * D:(m + 1) * D], srcap), deps)
            r_x[m].write(tk)
            tlx.append(tk)

        for dj in range(8):
            deps = r_pT.wdeps() + [r_x[m].w for m in range(4)] + r_eye.rdep()
            def mk_trx(dj):
                def f(t):
                    for m in range(4):
                        inst = t.transpose(pT[:, m * 128:(m + 1) * 128],
                                           x_sb[:, m * D + dj * 128: m * D + (dj + 1) * 128], eye)
                    return inst
                return f
            tm = em.t("pe", mk_trx(dj), deps)
            r_pT.write(tm)
            for m in range(4):
                r_x[m].read(tm)
            te = em.t("act", lambda s, dj=dj: s.copy(
                x1T[:, dj * (Tc + 1) + 1: dj * (Tc + 1) + 1 + Tc], pT),
                [tm] + r_x1T[dj].wdeps())
            r_pT.read(te)
            r_x1T[dj].write(te)

        pslice = pos_sb[:, ci * 4:(ci + 1) * 4]
        bc4 = lambda t_, n=32: t_.unsqueeze(1).broadcast_to([128, 4, n])
        v4 = lambda t_, n=32: t_.rearrange("p (a f) -> p a f", a=4)
        tF = em.t("dve", lambda v, ps_=pslice: v.tensor_tensor(v4(frq), bc4(invf),
                                                   ps_.unsqueeze(2).broadcast_to([128, 4, 32]),
                                                   ALU.mult),
                  r_invf.rdep() + r_tabs.wdeps())
        C1H, C1M, C1L = 6.283203125, -1.7642974853515625e-05, -1.7484555314695172e-07
        t1 = em.t("dve", lambda v: v.tensor_scalar(rrf, frq, float(1.0 / TWO_PI), 0.5,
                                                   ALU.mult, ALU.add), [tF])
        t2 = em.t("dve", lambda v: v.tensor_copy(rri, rrf), [t1])
        t3 = em.t("dve", lambda v: v.tensor_copy(rrf, rri), [t2])
        t4a = em.t("dve", lambda v: v.scalar_tensor_tensor(ang, rrf, -C1H, frq, ALU.mult, ALU.add), [t3])
        t4b = em.t("dve", lambda v: v.scalar_tensor_tensor(angc, rrf, -C1M, ang, ALU.mult, ALU.add), [t4a])
        t4c = em.t("dve", lambda v: v.scalar_tensor_tensor(ang, rrf, -C1L, angc, ALU.mult, ALU.add), [t4b])
        t4d = em.t("dve", lambda v: v.tensor_scalar(rrf, ang, float(np.pi), None, ALU.is_gt), [t4c])
        t4e = em.t("dve", lambda v: v.scalar_tensor_tensor(ang, rrf, -TWO_PI, ang, ALU.mult, ALU.add), [t4d])
        t4f = em.t("dve", lambda v: v.tensor_scalar(rrf, ang, float(-np.pi), None, ALU.is_lt), [t4e])
        t4 = em.t("dve", lambda v: v.scalar_tensor_tensor(ang, rrf, TWO_PI, ang, ALU.mult, ALU.add), [t4f])
        tS = em.t("act", lambda s: s.activation(snt, ang, AF.Sin), [t4])
        tCa = em.t("act", lambda s: s.activation(angc, ang, AF.Abs), [t4])
        tC = em.t("act", lambda s: s.activation(cstt, angc, AF.Sin, scale=-1.0, bias=halfpi), [tCa])
        tPW = em.t("dve", lambda v, ps_=pslice: v.tensor_scalar(pwv, ps_, 4096.0, float(1.0 / 256.0),
                                                    ALU.subtract, ALU.mult), r_tabs.wdeps())
        tSL = em.t("dve", lambda v: v.tensor_tensor(v4(sclog), bc4(lnbase),
                                                    pwv.unsqueeze(2).broadcast_to([128, 4, 32]),
                                                    ALU.mult), [tPW] + r_lnb.rdep())
        tE1 = em.t("act", lambda s: s.activation(scl, sclog, AF.Exp), [tSL])
        tE2 = em.t("act", lambda s: s.activation(iscl, sclog, AF.Exp, scale=-1.0), [tSL])
        tcs = em.t("dve", lambda v: v.tensor_mul(csx, cstt, scl), [tC, tE1])
        tss = em.t("dve", lambda v: v.tensor_mul(ssx, snt, scl), [tS, tE1])
        tck = em.t("dve", lambda v: v.tensor_mul(cski, cstt, iscl), [tC, tE2])
        tsk = em.t("dve", lambda v: v.tensor_mul(sski, snt, iscl), [tS, tE2])
        qv = r_qsc.rdep(); kv = r_ksc.rdep()
        tq1 = em.t("dve", lambda v: v.tensor_tensor(
            csq2.rearrange("p (a b f) -> p a b f", a=4, b=2),
            csx.rearrange("p (a f) -> p a f", a=4).unsqueeze(2).broadcast_to([128, 4, 2, 32]),
            qscrep.rearrange("p (b f) -> p b f", b=2).unsqueeze(1).broadcast_to([128, 4, 2, 32]),
            ALU.mult), [tcs] + qv + r_tabs.wdeps())
        tq2 = em.t("dve", lambda v: v.scalar_tensor_tensor(
            nsq2.rearrange("p (a f) -> p a f", a=4),
            ssx.rearrange("p (a f) -> p a f", a=4), -1.0,
            qscrep[:, 32:64].unsqueeze(1).broadcast_to([128, 4, 32]),
            ALU.mult, ALU.mult), [tss] + qv)
        tq3 = em.t("dve", lambda v: v.tensor_tensor(
            sq2.rearrange("p (a f) -> p a f", a=4),
            ssx.rearrange("p (a f) -> p a f", a=4),
            qscrep[:, 0:32].unsqueeze(1).broadcast_to([128, 4, 32]),
            ALU.mult), [tss] + qv)
        tk1 = em.t("dve", lambda v: v.tensor_tensor(
            csk2.rearrange("p (a b f) -> p a b f", a=4, b=2),
            cski.rearrange("p (a f) -> p a f", a=4).unsqueeze(2).broadcast_to([128, 4, 2, 32]),
            kscrep.rearrange("p (b f) -> p b f", b=2).unsqueeze(1).broadcast_to([128, 4, 2, 32]),
            ALU.mult), [tck] + kv)
        tk2 = em.t("dve", lambda v: v.scalar_tensor_tensor(
            nsk2.rearrange("p (a f) -> p a f", a=4),
            sski.rearrange("p (a f) -> p a f", a=4), -1.0,
            kscrep[:, 32:64].unsqueeze(1).broadcast_to([128, 4, 32]),
            ALU.mult, ALU.mult), [tsk] + kv)
        tk3 = em.t("dve", lambda v: v.tensor_tensor(
            sk2.rearrange("p (a f) -> p a f", a=4),
            sski.rearrange("p (a f) -> p a f", a=4),
            kscrep[:, 0:32].unsqueeze(1).broadcast_to([128, 4, 32]),
            ALU.mult), [tsk] + kv)
        for tt in (tq1, tq2, tq3, tk1, tk2):
            r_qsc.read(tt) if tt in (tq1, tq2, tq3) else r_ksc.read(tt)
        r_ksc.read(tk3)
        t_tabs = [tq1, tq2, tq3, tk1, tk2, tk3]
        r_tabs.write(t_tabs[-1])
        for tt in t_tabs[:-1]:
            r_tabs.read(tt)

        for m in range(4):
            g_abs = ci * 4 + m
            slot = g_abs % NRING
            pqs = [r_p1, r_p2, r_p3]
            pps = [p1, p2, p3]
            tqkv = []
            for v3 in range(3):
                deps = ([r_x1T[dj].w for dj in range(8)] + pqs[v3].wdeps()
                        + r_wq.rdep())
                def mk_qkv(m=m, v3=v3):
                    def f(t):
                        for dc in range(8):
                            col = dc * (Tc + 1) + (1 if dc < 4 else 0) + m * 128
                            inst = t.matmul(pps[v3], x1T[:, col: col + 128],
                                            wqkv_sb[:, dc * 1536 + v3 * 512: dc * 1536 + (v3 + 1) * 512],
                                            start=(dc == 0), stop=(dc == 7))
                        return inst
                    return f
                tk_ = em.t("pe", mk_qkv(), deps)
                pqs[v3].write(tk_)
                for dj in range(8):
                    r_x1T[dj].read(tk_)
                r_wq.read(tk_)
                tqkv.append(tk_)
            te_q = em.t("act", lambda s: s.copy(q_sb, p1), [tqkv[0]] + r_q.wdeps())
            r_p1.read(te_q); r_q.write(te_q)
            te_k = em.t("act", lambda s: s.copy(k_sb, p2), [tqkv[1]] + r_k.wdeps())
            r_p2.read(te_k); r_k.write(te_k)
            te_v = em.t("act", lambda s, slot=slot: s.copy(vring[slot], p3),
                        [tqkv[2]] + r_vring[slot].wdeps())
            r_p3.read(te_v); r_vring[slot].write(te_v)

            tsq = em.t("act", lambda s: s.activation(tt1, q_sb, AF.Square),
                       [te_q] + r_tt1.wdeps())
            r_tt1.write(tsq); r_q.read(tsq)
            trd = em.t("dve", lambda v: v.reduce_sum(qss, tt1.rearrange("p (h c) -> p h c", h=8),
                                                     axis=AX.X), [tsq] + r_qss.wdeps())
            r_tt1.read(trd); r_qss.write(trd)
            tsq2 = em.t("act", lambda s: s.activation(tt2, k_sb, AF.Square),
                        [te_k] + r_tt2.wdeps())
            r_tt2.write(tsq2); r_k.read(tsq2)
            trd2 = em.t("dve", lambda v: v.reduce_sum(kss, tt2.rearrange("p (h c) -> p h c", h=8),
                                                      axis=AX.X), [tsq2] + r_kss.wdeps())
            r_tt2.read(trd2); r_kss.write(trd2)
            tsr = em.t("act", lambda s: s.activation(qss, qss, AF.Sqrt), [trd])
            r_qss.write(tsr)
            tmx = em.t("dve", lambda v: v.tensor_tensor(qss, qss, e12, ALU.max), [tsr])
            r_qss.write(tmx)
            trq = em.t("dve", lambda v: v.reciprocal(rq, qss), [tmx] + r_rq.wdeps())
            r_qss.read(trq); r_rq.write(trq)
            tsr2 = em.t("act", lambda s: s.activation(kss, kss, AF.Sqrt), [trd2])
            r_kss.write(tsr2)
            tmx2 = em.t("dve", lambda v: v.tensor_tensor(kss, kss, e12, ALU.max), [tsr2])
            r_kss.write(tmx2)
            trk = em.t("dve", lambda v: v.reciprocal(rk, kss), [tmx2] + r_rk.wdeps())
            r_kss.read(trk); r_rk.write(trk)

            vq = lambda t_: t_.rearrange("p (h b c) -> p h b c", h=8, b=2)
            tb = r_tabs.w
            ta1 = em.t("dve", lambda v, m=m: v.tensor_tensor(
                vq(tt1), vq(q_sb),
                csq2[:, m * 64:(m + 1) * 64].rearrange("p (b c) -> p b c", b=2)
                .unsqueeze(1).broadcast_to([128, 8, 2, 32]), ALU.mult),
                [te_q, tb] + r_tt1.wdeps())
            r_tt1.write(ta1); r_q.read(ta1)
            ta2 = em.t("dve", lambda v, m=m: v.tensor_tensor(
                tt2.rearrange("p (h b c) -> p h b c", h=8, b=2)[:, :, 0, :],
                q_sb.rearrange("p (h b c) -> p h b c", h=8, b=2)[:, :, 1, :],
                nsq2[:, m * 32:(m + 1) * 32].unsqueeze(1).broadcast_to([128, 8, 32]),
                ALU.mult), [te_q, tb] + r_tt2.wdeps())
            ta3 = em.t("dve", lambda v, m=m: v.tensor_tensor(
                tt2.rearrange("p (h b c) -> p h b c", h=8, b=2)[:, :, 1, :],
                q_sb.rearrange("p (h b c) -> p h b c", h=8, b=2)[:, :, 0, :],
                sq2[:, m * 32:(m + 1) * 32].unsqueeze(1).broadcast_to([128, 8, 32]),
                ALU.mult), [te_q, tb, ta2])
            r_tt2.write(ta3); r_q.read(ta3)
            ta4 = em.t("dve", lambda v: v.tensor_add(qr2, tt1, tt2), [ta1, ta3] + r_qr2.wdeps())
            r_tt1.read(ta4); r_tt2.read(ta4); r_qr2.write(ta4)
            tb1 = em.t("dve", lambda v, m=m: v.tensor_tensor(
                vq(tt1), vq(k_sb),
                csk2[:, m * 64:(m + 1) * 64].rearrange("p (b c) -> p b c", b=2)
                .unsqueeze(1).broadcast_to([128, 8, 2, 32]), ALU.mult),
                [te_k, tb, ta4] + r_tt1.wdeps())
            r_tt1.write(tb1); r_k.read(tb1)
            tb2 = em.t("dve", lambda v, m=m: v.tensor_tensor(
                tt2.rearrange("p (h b c) -> p h b c", h=8, b=2)[:, :, 0, :],
                k_sb.rearrange("p (h b c) -> p h b c", h=8, b=2)[:, :, 1, :],
                nsk2[:, m * 32:(m + 1) * 32].unsqueeze(1).broadcast_to([128, 8, 32]),
                ALU.mult), [te_k, tb, ta4] + r_tt2.wdeps())
            tb3 = em.t("dve", lambda v, m=m: v.tensor_tensor(
                tt2.rearrange("p (h b c) -> p h b c", h=8, b=2)[:, :, 1, :],
                k_sb.rearrange("p (h b c) -> p h b c", h=8, b=2)[:, :, 0, :],
                sk2[:, m * 32:(m + 1) * 32].unsqueeze(1).broadcast_to([128, 8, 32]),
                ALU.mult), [te_k, tb, tb2])
            r_tt2.write(tb3); r_k.read(tb3)
            tb4 = em.t("dve", lambda v: v.tensor_add(kr2, tt1, tt2), [tb1, tb3] + r_kr2.wdeps())
            r_tt1.read(tb4); r_tt2.read(tb4); r_kr2.write(tb4)
            tb5 = em.t("dve", lambda v: v.tensor_tensor(
                kr2.rearrange("p (h c) -> p h c", h=8),
                kr2.rearrange("p (h c) -> p h c", h=8),
                rk.unsqueeze(2).broadcast_to([128, 8, 64]), ALU.mult),
                [tb4, trk])
            r_rk.read(tb5); r_kr2.write(tb5)

            def mk_tr(src, m=m):
                def f(t):
                    for p4 in range(4):
                        inst = t.transpose(pT[:, p4 * 128:(p4 + 1) * 128],
                                           src[:, p4 * 128:(p4 + 1) * 128], eye)
                    return inst
                return f
            tqt = em.t("pe", mk_tr(qr2), [ta4] + r_pT.wdeps() + r_eye.rdep())
            r_pT.write(tqt); r_qr2.read(tqt)
            tqe = em.t("act", lambda s: s.copy(qT, pT), [tqt] + r_qT.wdeps())
            r_pT.read(tqe); r_qT.write(tqe)
            tkt = em.t("pe", mk_tr(kr2), [tb5] + r_pT.wdeps())
            r_pT.write(tkt); r_kr2.read(tkt)
            tke = em.t("act", lambda s, slot=slot: s.copy(kring[slot], pT),
                       [tkt] + r_kring[slot].wdeps())
            r_pT.read(tke); r_kring[slot].write(tke)

            if m in attn_tiles:
                ts8 = em.t("dve", lambda v: v.tensor_scalar(s8, rq, 8.0, None, ALU.mult),
                           [trq] + r_s8.wdeps())
                r_rq.read(ts8); r_s8.write(ts8)
                for h in range(8):
                    jj, pp = h % 2, h // 2
                    deps = (r_pS.wdeps() + [tqe] +
                            [r_kring[(g_abs - 4 + kc) % NRING].w for kc in range(5)] +
                            r_masks.rdep() + r_eye.rdep())
                    def mk_qk(g_abs=g_abs, jj=jj, pp=pp):
                        def f(t):
                            for kc in range(5):
                                slot2 = (g_abs - 4 + kc) % NRING
                                inst = t.matmul(pS[:, kc * 128:(kc + 1) * 128],
                                                qT[jj * 64:(jj + 1) * 64, pp * 128:(pp + 1) * 128],
                                                kring[slot2][jj * 64:(jj + 1) * 64, pp * 128:(pp + 1) * 128],
                                                start=True, stop=(kc not in (0, 4)))
                                if kc == 0:
                                    inst = t.matmul(pS[:, 0:128], eye, maskL, start=False, stop=True)
                                elif kc == 4:
                                    inst = t.matmul(pS[:, 512:640], eye, maskR, start=False, stop=True)
                            return inst
                        return f
                    tsc = em.t("pe", mk_qk(), deps)
                    r_pS.write(tsc)
                    r_qT.read(tsc)
                    for kc in range(5):
                        r_kring[(g_abs - 4 + kc) % NRING].read(tsc)
                    if ci == 2:
                        tmf = em.t("dve", lambda v, m=m: v.tensor_tensor(
                            pS[:, 0:640],
                            pS[:, 0:640], mf_sb[:, m * 640:(m + 1) * 640], ALU.add),
                            [tsc, tld[4]])
                        r_pS.write(tmf)
                        r_mf.read(tmf)
                        tsc = tmf
                    tmx_ = em.t("dve", lambda v: v.reduce_max(nm, pS[:, 0:640], axis=AX.X,
                                                              negate=True),
                                [tsc] + r_nm.wdeps())
                    r_pS.read(tmx_); r_nm.write(tmx_)
                    tnm = em.t("dve", lambda v, h=h: v.tensor_tensor(nmz, nm, s8[:, h:h + 1],
                                                                     ALU.mult),
                               [tmx_, ts8] + r_nmz.wdeps())
                    r_nm.read(tnm); r_s8.read(tnm); r_nmz.write(tnm)
                    tex = em.t("act", lambda s, h=h: s.activation(P_sb, pS[:, 0:640], AF.Exp,
                                                             bias=nmz, scale=s8[:, h:h + 1],
                                                             accum_out=sume),
                               [tnm] + r_P.wdeps() + r_sume.wdeps())
                    r_pS.read(tex); r_nmz.read(tex); r_s8.read(tex)
                    r_P.write(tex); r_sume.write(tex)
                    trs = em.t("dve", lambda v: v.reciprocal(rsume, sume),
                               [tex] + r_rsume.wdeps())
                    r_sume.read(trs); r_rsume.write(trs)
                    def mk_ptr1():
                        def f(t):
                            for kc in range(4):
                                inst = t.transpose(pT[:, kc * 128:(kc + 1) * 128],
                                                   P_sb[:, kc * 128:(kc + 1) * 128], eye)
                            return inst
                        return f
                    tp1 = em.t("pe", mk_ptr1(), [tex] + r_pT.wdeps())
                    r_pT.write(tp1); r_P.read(tp1)
                    tpe1 = em.t("act", lambda s: s.copy(PT_sb[:, 0:512], pT),
                                [tp1] + r_PT.wdeps())
                    r_pT.read(tpe1); r_PT.write(tpe1)
                    tp2 = em.t("pe", lambda t: t.transpose(pT[:, 0:128], P_sb[:, 512:640], eye),
                               [tpe1] + r_pT.wdeps())
                    r_pT.write(tp2); r_P.read(tp2)
                    tpe2 = em.t("act", lambda s: s.copy(PT_sb[:, 512:640], pT[:, 0:128]), [tp2])
                    r_pT.read(tpe2); r_PT.write(tpe2)
                    pv = pVa if h % 2 == 0 else pVb
                    rpv = r_pV[h % 2]
                    def mk_pv(g_abs=g_abs, h=h, pv=pv):
                        def f(t):
                            for kc in range(5):
                                slot2 = (g_abs - 4 + kc) % NRING
                                inst = t.matmul(pv[:, 0:64], PT_sb[:, kc * 128:(kc + 1) * 128],
                                                vring[slot2][:, h * 64:(h + 1) * 64],
                                                start=(kc == 0), stop=(kc == 4))
                            return inst
                        return f
                    tpv = em.t("pe", mk_pv(), [tpe1, tpe2] + rpv.wdeps() +
                               [r_vring[(g_abs - 4 + kc) % NRING].w for kc in range(5)])
                    rpv.write(tpv); r_PT.read(tpv)
                    for kc in range(5):
                        r_vring[(g_abs - 4 + kc) % NRING].read(tpv)
                    tae = em.t("dve", lambda v, h=h, pv=pv: v.tensor_scalar(
                        attn[:, h * 64:(h + 1) * 64], pv[:, 0:64], rsume, None, ALU.mult),
                        [tpv, trs] + (r_attn.wdeps() if h == 0 else []))
                    rpv.read(tae); r_rsume.read(tae)
                    r_attn.write(tae)
                tat = em.t("pe", mk_tr(attn), [r_attn.w] + r_pT.wdeps())
                r_pT.write(tat); r_attn.read(tat)
                tate = em.t("act", lambda s: s.copy(attnT, pT), [tat] + r_attnT.wdeps())
                r_pT.read(tate); r_attnT.write(tate)
                for oh in range(2):
                    pw_ = p1 if oh == 0 else p2
                    rw = r_p1 if oh == 0 else r_p2
                    def mk_wo(oh=oh, pw_=pw_):
                        def f(t):
                            for p4 in range(4):
                                inst = t.matmul(pw_, attnT[:, p4 * 128:(p4 + 1) * 128],
                                                wo_sb[:, p4 * 1024 + oh * 512: p4 * 1024 + (oh + 1) * 512],
                                                start=(p4 == 0), stop=(p4 == 3))
                            return inst
                        return f
                    two = em.t("pe", mk_wo(), [tate] + rw.wdeps() + r_wo.rdep())
                    rw.write(two); r_attnT.read(two); r_wo.read(two)
                    if ci == 1:
                        dst = x2tail
                        rd = r_x2tail
                    else:
                        dst = x2_sb
                        rd = r_x2[m]
                    off = (0 if ci == 1 else m * D) + oh * 512
                    tx2 = em.t("dve", lambda v, dst=dst, off=off, pw_=pw_, m=m, oh=oh: v.tensor_add(
                        dst[:, off:off + 512], pw_[:, 0:512],
                        x_sb[:, m * D + oh * 512: m * D + (oh + 1) * 512]),
                        [two, r_x[m].w] + rd.wdeps())
                    rw.read(tx2); r_x[m].read(tx2); rd.write(tx2)
                if ci == 1:
                    tshm = em.t("dve", lambda v: v.tensor_scalar(x2tail, x2tail, shm_sb[:, 0:1],
                                                                 None, ALU.mult),
                                [r_x2tail.w])
                    r_x2tail.write(tshm)

        for dj in range(8):
            if ci == 1:
                tc_ = em.t("dve", lambda v, dj=dj: v.tensor_scalar(
                    x1T[:, dj * (Tc + 1): dj * (Tc + 1) + 1],
                    x1T[:, dj * (Tc + 1) + Tc: dj * (Tc + 1) + Tc + 1],
                    shm_sb[:, 0:1], None, ALU.mult),
                    r_x1T[dj].wdeps() + [r_x1T[dj].w])
            else:
                tc_ = em.t("dve", lambda v, dj=dj: v.tensor_copy(
                    x1T[:, dj * (Tc + 1): dj * (Tc + 1) + 1],
                    x1T[:, dj * (Tc + 1) + Tc: dj * (Tc + 1) + Tc + 1]),
                    r_x1T[dj].wdeps() + [r_x1T[dj].w])
            r_x1T[dj].write(tc_)

        if not own:
            continue

        for m in range(4):
            def mk_sh(m=m):
                def f(t):
                    inst = t.matmul(pS[:, 0:512], eye,
                                    x2_sb[:, m * D: m * D + 512],
                                    start=True, stop=True)
                    prev = (x2_sb[:, (m - 1) * D + 512:(m - 1) * D + 1024]
                            if m > 0 else x2tail[:, 512:1024])
                    inst = t.matmul(pS[:, 512:1024], shA,
                                    x2_sb[:, m * D + 512: m * D + 1024],
                                    start=True, stop=False)
                    inst = t.matmul(pS[:, 512:1024], shB, prev,
                                    start=False, stop=True)
                    return inst
                return f
            prev_res = r_x2[m - 1] if m > 0 else r_x2tail
            tsh = em.t("pe", mk_sh(), r_pS.wdeps() + [r_x2[m].w, prev_res.w] + r_sh.rdep())
            r_pS.write(tsh); r_x2[m].read(tsh); prev_res.read(tsh)
            tb1_ = em.t("dve", lambda v: v.bn_stats(bns[:, 0:6], pS[:, 0:512]),
                        [tsh] + r_lns.wdeps())
            tb2_ = em.t("dve", lambda v: v.bn_stats(bns[:, 6:12], pS[:, 512:1024]), [tsh, tb1_])
            tag = em.t("dve", lambda v: v.bn_aggr(mv, bns), [tb2_])
            tsd = em.t("act", lambda s: s.activation(sd, mv[:, 1:2], AF.Sqrt, bias=eps), [tag])
            trs_ = em.t("dve", lambda v: v.reciprocal(rs, sd), [tsd])
            tap = em.t("dve", lambda v, m=m: v.tensor_scalar(hh_sb[:, m * D:(m + 1) * D],
                                                             pS[:, 0:1024], mv[:, 0:1], rs,
                                                             ALU.subtract, ALU.mult),
                       [trs_] + r_hh[m].wdeps())
            r_pS.read(tap)
            tgm = em.t("dve", lambda v, m=m: v.tensor_mul(hh_sb[:, m * D:(m + 1) * D],
                                                          hh_sb[:, m * D:(m + 1) * D], gam_rep),
                       [tap] + r_gam.rdep())
            r_gam.read(tgm)
            tbt = em.t("dve", lambda v, m=m: v.tensor_add(hh_sb[:, m * D:(m + 1) * D],
                                                          hh_sb[:, m * D:(m + 1) * D], bet_rep),
                       [tgm] + r_bet.rdep())
            r_bet.read(tbt)
            r_lns.write(tbt)
            r_hh[m].write(tbt)
        ttail = em.t("dve", lambda v: v.tensor_copy(x2tail, x2_sb[:, 3 * D:4 * D]),
                     [r_x2[3].w] + r_x2tail.wdeps())
        r_x2[3].read(ttail)
        r_x2tail.write(ttail)
        pT_bf = pT.bitcast(BF16)
        for dj in range(8):
            def mk_ht(dj=dj):
                def f(t):
                    for m in range(4):
                        inst = t.transpose(pT_bf[:, m * 128:(m + 1) * 128],
                                           hh_sb[:, m * D + dj * 128: m * D + (dj + 1) * 128], eyebf)
                    return inst
                return f
            tht = em.t("pe", mk_ht(), [r_hh[m].w for m in range(4)] + r_pT.wdeps())
            r_pT.write(tht)
            for m in range(4):
                r_hh[m].read(tht)
            the = em.t("act", lambda s, dj=dj: s.copy(hT[:, dj * 512:(dj + 1) * 512],
                                                      pT_bf[:, 0:512]),
                       [tht] + ([] if dj > 0 else r_hT.wdeps()))
            r_pT.read(the)
            r_hT.write(the)
        for hh in range(NHh):
            pb = hh % 2
            tw1a = em.t("ldw", lambda s, hh=hh, pb=pb: s.dma_start(
                w1a_sb[pb], W["w1a_bf"][:, hh * 1024:(hh + 1) * 1024]),
                r_w1a[pb].wdeps())
            r_w1a[pb].write(tw1a)
            tw1g = em.t("ldw", lambda s, hh=hh, pb=pb: s.dma_start(
                w1g_sb[pb], W["w1g_bf"][:, hh * 1024:(hh + 1) * 1024]),
                r_w1g[pb].wdeps())
            r_w1g[pb].write(tw1g)
            def mk_w1(wsb, pp_, pb=pb):
                def f(t):
                    for dc in range(8):
                        inst = t.matmul(pp_, wsb[pb][:, dc * 128:(dc + 1) * 128],
                                        hT[:, dc * 512:(dc + 1) * 512],
                                        start=(dc == 0), stop=(dc == 7))
                    return inst
                return f
            ta_ = em.t("pe", mk_w1(w1a_sb, p1), [tw1a, r_hT.w] + r_p1.wdeps())
            r_p1.write(ta_); r_w1a[pb].read(ta_); r_hT.read(ta_)
            tg_ = em.t("pe", mk_w1(w1g_sb, p2), [tw1g, r_hT.w] + r_p2.wdeps())
            r_p2.write(tg_); r_w1g[pb].read(tg_)
            tae_ = em.t("dve", lambda v, hh=hh: v.tensor_scalar(a_sb, p1, b1a_sb[:, hh:hh + 1],
                                                                None, ALU.add),
                        [ta_, tld[0]] + r_a.wdeps())
            r_p1.read(tae_); r_a.write(tae_); r_b1a.read(tae_)
            tge_ = em.t("act", lambda s, hh=hh: s.activation(gg_sb, p2, AF.Gelu,
                                                             bias=b1g_sb[:, hh:hh + 1]),
                        [tg_, tld[1]] + r_gg.wdeps())
            r_p2.read(tge_); r_gg.write(tge_); r_b1g.read(tge_)
            tgt_ = em.t("dve", lambda v, hh=hh: v.tensor_mul(gate[:, hh * 512:(hh + 1) * 512],
                                                             a_sb, gg_sb),
                        [tae_, tge_] + r_gate[hh].wdeps())
            r_a.read(tgt_); r_gg.read(tgt_); r_gate[hh].write(tgt_)

        def do_store(m):
            if final:
                tmr = em.t("ld", lambda s, m=m, oc0=oc0: s.dma_start(
                    mres_sb, W["mres_ap"][oc0 + m * 128: oc0 + (m + 1) * 128, :]),
                    r_x[0].wdeps())
                r_x[0].write(tmr)
                tfa = em.t("dve", lambda v: v.tensor_add(x3_sb, x3_sb, mres_sb),
                           [r_x3.w, tmr])
                r_x[0].read(tfa); r_x3.write(tfa)
                tb1f = em.t("dve", lambda v: v.bn_stats(bns[:, 0:6], x3_sb[:, 0:512]),
                            [tfa] + r_lns.wdeps())
                tb2f = em.t("dve", lambda v: v.bn_stats(bns[:, 6:12], x3_sb[:, 512:1024]),
                            [tfa, tb1f])
                tagf = em.t("dve", lambda v: v.bn_aggr(mv, bns), [tb2f])
                tsdf = em.t("act", lambda s: s.activation(sd, mv[:, 1:2], AF.Sqrt, bias=eps), [tagf])
                trsf = em.t("dve", lambda v: v.reciprocal(rs, sd), [tsdf])
                tapf = em.t("dve", lambda v: v.tensor_scalar(x3_sb, x3_sb, mv[:, 0:1], rs,
                                                             ALU.subtract, ALU.mult),
                            [trsf, r_x3.w])
                r_lns.write(tapf)
                r_x3.write(tapf)
            tst = em.t("st", lambda s, m=m, oc0=oc0: s.dma_start(
                out_dst[oc0 + m * 128: oc0 + (m + 1) * 128, :], x3_sb), [r_x3.w])
            r_x3.read(tst)
            if xrow_new is not None:
                xrow_new[(oc0 + m * 128) // 512].append(tst)
            if (not final) and (oc0 + m * 128 >= NT - H4):
                rr = oc0 + m * 128 - (NT - H4)
                for si in range(2):
                    mts = mt0 if si == 0 else mt1
                    tc0 = em.t("dve", lambda v, mts=mts: v.tensor_scalar(
                        a_sb, x3_sb[:, 0:512], mts, None, ALU.mult),
                        [r_x3.w] + r_a.wdeps())
                    r_x3.read(tc0); r_a.write(tc0)
                    tc1 = em.t("dve", lambda v, mts=mts: v.tensor_scalar(
                        gg_sb, x3_sb[:, 512:1024], mts, None, ALU.mult),
                        [r_x3.w] + r_gg.wdeps())
                    r_x3.read(tc1); r_gg.write(tc1)
                    ts0 = em.t("st", lambda s, si=si, rr=rr: s.dma_start(
                        W["rs_t2"][si, rr:rr + 128, 0:512], a_sb), [tc0])
                    r_a.read(ts0)
                    ts1 = em.t("st", lambda s, si=si, rr=rr: s.dma_start(
                        W["rs_t2"][si, rr:rr + 128, 512:1024], gg_sb), [tc1])
                    r_gg.read(ts1)
                    tail_toks.append(ts0)
                    tail_toks.append(ts1)

        pws8 = [p1, p2, p3, pVa, pVb, pS[:, 0:512], pS[:, 512:1024], pT]
        rws8 = [r_p1, r_p2, r_p3, r_pV[0], r_pV[1], r_pS, r_pS, r_pT]
        tw2g = [None] * 8
        for hh in range(NHh):
            pb2 = hh % 2
            tl2 = em.t("ldw", lambda s, hh=hh, pb2=pb2: s.dma_start(
                w2s_sb[pb2], W["w2_bf"][:, hh * 1024:(hh + 1) * 1024]),
                r_w2s[pb2].wdeps())
            r_w2s[pb2].write(tl2)
            for m in range(4):
                for oh in range(2):
                    sl = m * 2 + oh
                    def mk_w2p(m=m, oh=oh, sl=sl, hh=hh, pb2=pb2):
                        def f(t):
                            return t.matmul(pws8[sl][:, 0:512],
                                            gate[:, hh * 512 + m * 128: hh * 512 + (m + 1) * 128],
                                            w2s_sb[pb2][:, oh * 512:(oh + 1) * 512],
                                            start=(hh == 0), stop=False)
                        return f
                    deps = [tl2, r_gate[hh].w]
                    if hh == 0:
                        deps += rws8[sl].wdeps()
                    tk2 = em.t("pe", mk_w2p(), deps)
                    if hh == 0:
                        rws8[sl].write(tk2)
                    tw2g[sl] = tk2
                    r_gate[hh].read(tk2)
                    r_w2s[pb2].read(tk2)
        for sl in range(8):
            oh = sl % 2
            def mk_b2(sl=sl, oh=oh):
                def f(t):
                    return t.matmul(pws8[sl][:, 0:512], ones1,
                                    b2_sb[:, oh * 512:(oh + 1) * 512],
                                    start=False, stop=True)
                return f
            tkb = em.t("pe", mk_b2(), [tw2g[sl], tld[2]])
            rws8[sl].write(tkb)
            r_b2.read(tkb)
            tw2g[sl] = tkb
        for m in range(4):
            for oh in range(2):
                sl = m * 2 + oh
                tx3 = em.t("dve", lambda v, m=m, oh=oh, sl=sl: v.tensor_add(
                    x3_sb[:, oh * 512:(oh + 1) * 512], pws8[sl][:, 0:512],
                    x2_sb[:, m * D + oh * 512: m * D + (oh + 1) * 512]),
                    [tw2g[sl], r_x2[m].w] + (r_x3.wdeps() if oh == 0 else [r_x3.w]))
                rws8[sl].read(tx3); r_x2[m].read(tx3); r_x3.write(tx3)
            do_store(m)


GROUPS = [[0, 1], [2, 3], [4, 5], [6, 7]]


def build_fused(dbg=False):
    nc = bass.Bass("TRN2", target_bir_lowering=False)
    Tc = 512
    L = 2 * NT
    NCA = 16
    HID = 2816
    NHh = 22

    ein = lambda n, sh, dt=F32: nc.dram_tensor(n, sh, dt, kind="ExternalInput").ap()
    x_a = ein("x_a", [L, D])
    x_own = ein("x_own", [NT, D])
    w_in_r = ein("w_in_h", [128, 8 * 1024])
    bin_ur = ein("bin_uh", [128, 4]); bin_zr = ein("bin_zh", [128, 4])
    bconv_r = ein("bconvh", [128, 4]); wconv_r = ein("wconvh", [128, 16])
    wxp_r = ein("wxp_h", [128, 4 * 48]); bxp_re = ein("bxp_r", [16, 3])
    wdt_r = ein("wdt_h", [16, 512]); bdt_r = ein("bdth", [128, 4])
    alog_r = ein("alog_h", [128, 64])
    mb0_r = ein("mb0", [128, 1]); mb1_r = ein("mb1", [128, 1])
    mt0_r = ein("mt0", [128, 1]); mt1_r = ein("mt1", [128, 1])
    wout_r = ein("wout_r", [128, 8 * 1024]); bout_r = ein("bout_r", [1, 1024])
    pos_r = ein("pos_r", [128, 10 * 4])
    mf_r = ein("mf_r", [128, 4 * 640], BF16)
    shm_r = ein("shm_r", [128, 1])
    FWE = []
    for li in range(2):
        FWE.append(dict(
            wqkv_r=ein(f"wqkv_r{li}", [128, 8 * 1536]),
            wo_r=ein(f"wo_r{li}", [128, 4 * 1024]),
            w1a_r=ein(f"w1a_r{li}", [128, 8 * HID]),
            w1g_r=ein(f"w1g_r{li}", [128, 8 * HID]),
            w2_r=ein(f"w2_r{li}", [128, NHh * 1024]),
            b1a_r=ein(f"b1a_r{li}", [128, NHh]),
            b1g_r=ein(f"b1g_r{li}", [128, NHh]),
            b2_r=ein(f"b2_r{li}", [1, 1024]),
            lng_row=ein(f"lng_row{li}", [1, 1024]),
            lnb_row=ein(f"lnb_row{li}", [1, 1024]),
            qs_row=ein(f"qs_row{li}", [1, 64]),
            ks_row=ein(f"ks_row{li}", [1, 64]),
        ))
    xo = nc.dram_tensor("xo_t", [NT, D], F32, kind="ExternalOutput").ap()

    DBGOK = {"u_h", "sz_h", "x1own_h", "x2own_h"}
    din = lambda n, sh, dt=F32: nc.dram_tensor(
        n, sh, dt, kind=("ExternalOutput" if (dbg and n in DBGOK) else "Internal")).ap()
    u_h = din("u_h", [512, L]); sz_h = din("sz_h", [512, L])
    xdbp_h = din("xdbp_h", [48, L]); xdbs_h = din("xdbs_h", [48, L])
    rs_gin = din("rs_gin", [2, 1024, NT], BF16)
    g_full = din("g_full", [1024, NT], BF16)
    x1own_h = din("x1own_h", [NT, D])
    rs_t_in = din("rs_t_in", [2, H4, D])
    halo_h = din("halo_h", [H4, D])
    x2own_h = din("x2own_h", [NT, D])
    rs_t2_in = din("rs_t2_in", [2, H4, D])
    halo2_h = din("halo2_h", [H4, D])
    w_in_bf = din("w_in_bf", [128, 8 * 1024], BF16)
    wout_bf = din("wout_bf", [128, 8 * 1024], BF16)
    FWB = []
    for li in range(2):
        FWB.append(dict(
            wqkv_bf=din(f"wqkv_bf{li}", [128, 8 * 1536], BF16),
            wo_bf=din(f"wo_bf{li}", [128, 4 * 1024], BF16),
            w1a_bf=din(f"w1a_bf{li}", [128, 8 * HID], BF16),
            w1g_bf=din(f"w1g_bf{li}", [128, 8 * HID], BF16),
            w2_bf=din(f"w2_bf{li}", [128, NHh * 1024], BF16),
        ))

    sb = lambda n, sh, dt=F32: nc.alloc_sbuf_tensor(n, sh, dt).ap()
    eye = sb("eye", [128, 128]); eyebf = sb("eyebf", [128, 128], BF16)
    ones1 = sb("ones1", [1, 128]); eps = sb("eps", [128, 1]); e12 = sb("e12", [128, 8])
    halfpi = sb("halfpi", [128, 1])
    maskL = sb("maskL", [128, 128]); maskR = sb("maskR", [128, 128])
    shA = sb("shA", [128, 128]); shB = sb("shB", [128, 128])
    wqkv_sb = sb("wqkv_sb", [128, 8 * 1536], BF16)
    wo_sb = sb("wo_sb", [128, 4 * 1024], BF16)
    w2s_sb = [sb(f"w2s_sb{i}", [128, 1024], BF16) for i in range(2)]
    w1a_sb = [sb(f"w1a_sb{i}", [128, 8 * 128], BF16) for i in range(2)]
    w1g_sb = [sb(f"w1g_sb{i}", [128, 8 * 128], BF16) for i in range(2)]
    b1a_sb = sb("b1a_sb", [128, NHh]); b1g_sb = sb("b1g_sb", [128, NHh])
    b2_sb = sb("b2_sb", [1, 1024])
    gambet = sb("gambet", [128, 2048])
    gam_rep = gambet[:, 0:1024]; bet_rep = gambet[:, 1024:2048]
    lngrow_sb = sb("lngrow_sb", [1, 1024]); lnbrow_sb = sb("lnbrow_sb", [1, 1024])
    qsrow_sb = sb("qsrow_sb", [1, 64]); ksrow_sb = sb("ksrow_sb", [1, 64])
    qscrep = sb("qscrep", [128, 64]); kscrep = sb("kscrep", [128, 64])
    pos_sb = sb("pos_sb", [128, 10 * 4])
    mf_sb = sb("mf_sb", [128, 4 * 640], BF16)
    shm_sb = sb("shm_sb", [128, 1])
    invf = sb("invf", [128, 32]); lnbase = sb("lnbase", [128, 32])
    frq = sb("frq", [128, 4 * 32])
    rrf = sb("rrf", [128, 4 * 32]); ang = sb("ang", [128, 4 * 32]); angc = sb("angc", [128, 4 * 32])
    rri = nc.alloc_sbuf_tensor("rri", [128, 4 * 32], I32).ap()
    snt = sb("snt", [128, 4 * 32]); cstt = sb("cstt", [128, 4 * 32])
    pwv = sb("pwv", [128, 4]); sclog = sb("sclog", [128, 4 * 32])
    scl = sb("scl", [128, 4 * 32]); iscl = sb("iscl", [128, 4 * 32])
    csx = sb("csx", [128, 4 * 32]); ssx = sb("ssx", [128, 4 * 32])
    cski = sb("cski", [128, 4 * 32]); sski = sb("sski", [128, 4 * 32])
    csq2 = sb("csq2", [128, 4 * 64]); nsq2 = sb("nsq2", [128, 4 * 32]); sq2 = sb("sq2", [128, 4 * 32])
    csk2 = sb("csk2", [128, 4 * 64]); nsk2 = sb("nsk2", [128, 4 * 32]); sk2 = sb("sk2", [128, 4 * 32])
    x_sb = sb("x_sb", [128, 4 * D])
    x1T = sb("x1T", [128, 8 * (Tc + 1)], BF16)
    q_sb = sb("q_sb", [128, 512]); k_sb = sb("k_sb", [128, 512])
    qss = sb("qss", [128, 8]); kss = sb("kss", [128, 8])
    rq = sb("rq", [128, 8]); rk = sb("rk", [128, 8])
    qr2 = sb("qr2", [128, 512]); kr2 = sb("kr2", [128, 512])
    tt1 = sb("tt1", [128, 512]); tt2 = sb("tt2", [128, 512])
    qT = sb("qT", [128, 512], BF16)
    NRING = 6
    big_kv = sb("big_kv", [128, 12 * 512], BF16)
    kring = [big_kv[:, i * 512:(i + 1) * 512] for i in range(NRING)]
    vring = [big_kv[:, (6 + i) * 512:(7 + i) * 512] for i in range(NRING)]
    s8 = sb("s8", [128, 8]); nm = sb("nm", [128, 1]); nmz = sb("nmz", [128, 1])
    P_sb = sb("P_sb", [128, 640])
    PT_sb = sb("PT_sb", [128, 5 * 128], BF16)
    sume = sb("sume", [128, 1]); rsume = sb("rsume", [128, 1])
    attn = sb("attn", [128, 512])
    attnT = sb("attnT", [128, 512], BF16)
    x2_sb = sb("x2_sb", [128, 4 * D])
    x2tail = sb("x2tail", [128, D])
    bns = sb("bns", [128, 12]); mv = sb("mv", [128, 2]); sd = sb("sd", [128, 1]); rs_t = sb("rs", [128, 1])
    hh_sb = sb("hh_sb", [128, 4 * D], BF16)
    hT = sb("hT", [128, 8 * 512], BF16)
    a_sb = sb("a_sb", [128, 512]); gg_sb = sb("gg_sb", [128, 512])
    gate = sb("gate", [128, NHh * 512], BF16)
    x3_sb = sb("x3_sb", [128, D])
    masks4 = sb("masks4", [128, 4])
    mres_sb = x_sb[:, 0:D]

    ar1f = wqkv_sb.bitcast(F32)
    ar2f = gate.bitcast(F32)
    ar5f = hh_sb.bitcast(F32)
    ar6f = x1T.bitcast(F32)
    ar7f = wo_sb.bitcast(F32)
    bkf = big_kv.bitcast(F32)
    mff = mf_sb.bitcast(F32)
    a_win = wqkv_sb[:, 0:8192]
    a_xT = x1T[:, 0:4096]
    a_uraw = x2_sb[:, 0:2060]
    a_u = ar2f[:, 2048:4096]
    a_szt = ar5f[:, 0:2048]
    a_diag = ar2f[:, 0:2048]
    a_xdb_t = attn[0:48, 0:512]
    a_wxp = ar7f[:, 512:704]
    wdt_sb = ar7f[0:16, 0:512]
    binu = cstt[:, 16:20]; binz = cstt[:, 20:24]; bconvh = cstt[:, 24:28]
    bdth = cstt[:, 28:32]; wconvh = cstt[:, 0:16]
    bxp0 = cstt[0:48, 32:33]; bxpr = cstt[0:16, 33:36]
    mb0 = masks4[:, 0:1]; mb1 = masks4[:, 1:2]; mt0 = masks4[:, 2:3]; mt1 = masks4[:, 3:4]
    b_u = x_sb[:, 0:2048]; b_dt = x_sb[:, 2048:4096]
    b_du = x2_sb[:, 0:2048]; b_duT = x2_sb[:, 2048:4096]
    b_szc = x2_sb[:, 0:2048]
    b_dA = [ar1f[:, 0:2048], ar1f[:, 2048:4096]]
    b_h = [ar2f[:, 0:2048], ar2f[:, 2048:4096]]
    b_dg = [ar5f[:, 0:512], ar5f[:, 512:1024]]
    b_crep = [ar5f[:, 1024:1536], ar5f[:, 1536:2048]]
    b_gtmp = ar5f[:, 1024:1536]
    b_yt = [bkf[:, 0:2048], ar6f[:, 0:2048]]
    b_gst = hT
    A_sb = frq[:, 0:64]; bmT = rrf[:, 0:64]; st_sb = ang[:, 0:64]
    alog_sb = angc[:, 0:64]; ones16 = snt[0:16, 0:128]
    selv = gambet[0:16, 0:2048]
    b_bm = P_sb[0:16, 0:512]; b_cm = tt1[0:16, 0:512]
    spA = mff[:, 0:512]; spB = mff[:, 512:1024]
    c_gsb = hh_sb
    c_x1 = ar2f[:, 0:4096]
    bout_sb = lngrow_sb

    def wout_at(dc, oh):
        if dc < 4:
            return hT[:, dc * 1024 + oh * 512: dc * 1024 + (oh + 1) * 512]
        return big_kv[:, (dc - 4) * 1024 + oh * 512: (dc - 4) * 1024 + (oh + 1) * 512]

    p1 = nc.alloc_psum_tensor("p1", [128, 512], F32).ap()
    p2 = nc.alloc_psum_tensor("p2", [128, 512], F32).ap()
    p3 = nc.alloc_psum_tensor("p3", [128, 512], F32).ap()
    pS = nc.alloc_psum_tensor("pS", [128, 1024], F32).ap()
    pT = nc.alloc_psum_tensor("pT", [128, 512], F32).ap()
    pVa = nc.alloc_psum_tensor("pVa", [128, 512], F32).ap()
    pVb = nc.alloc_psum_tensor("pVb", [128, 512], F32).ap()
    pD48 = pT[0:48, :]
    p_y = [p1, p2, p3, pVa]
    p_db = [pVb, pS[:, 0:512]]
    p_cr = pS[:, 512:1024]
    p_tr = pT
    pM = [p1, p2, p3, pVa]

    ld = Ch(nc, "ld"); st = Ch(nc, "st"); cst = Ch(nc, "cst")
    coll = Ch(nc, "coll"); ldc = Ch(nc, "ldc"); c_pp = Ch(nc, "c_pp")
    a_tr = Ch(nc, "a_tr"); a_mm = Ch(nc, "a_mm"); a_xt = Ch(nc, "a_xt")
    a_ur = Ch(nc, "a_ur"); a_sz = Ch(nc, "a_sz"); a_uc = Ch(nc, "a_uc")
    a_xdb = Ch(nc, "a_xdb"); a_tail = Ch(nc, "a_tail")
    b_bmt = Ch(nc, "b_bmt"); b_bev = Ch(nc, "b_bev")
    b_duc = Ch(nc, "b_duc"); b_dut = Ch(nc, "b_dut"); b_due = Ch(nc, "b_due")
    b_da = Ch(nc, "b_da"); b_dgc = Ch(nc, "b_dgc")
    b_cmc = Ch(nc, "b_cmc"); b_ce = Ch(nc, "b_ce")
    b_db = Ch(nc, "b_db"); b_sc = Ch(nc, "b_sc")
    b_ytc = Ch(nc, "b_ytc"); b_ya = Ch(nc, "b_ya"); b_stc = Ch(nc, "b_stc")
    b_spa = Ch(nc, "b_spa"); b_dtm = Ch(nc, "b_dtm"); b_dta = Ch(nc, "b_dta")
    b_xb = Ch(nc, "b_xb"); b_gm = Ch(nc, "b_gm")
    c_mm = Ch(nc, "c_mm"); c_ev = Ch(nc, "c_ev"); c_v = Ch(nc, "c_v")
    c_a = Ch(nc, "c_a"); c_x1c = Ch(nc, "c_x1c"); c_tl = Ch(nc, "c_tl")

    CSTN = 12 * 16

    with nc.Block("prep0") as blk:
        @blk.gpsimd
        def _(g):
            cst.sig(g.dma_start(out=w_in_bf, in_=w_in_r), 16)
            cst.sig(g.dma_start(out=wout_bf, in_=wout_r), 16)
            for li in range(2):
                for kk in ("wqkv", "wo", "w1a", "w1g", "w2"):
                    cst.sig(g.dma_start(out=FWB[li][kk + "_bf"], in_=FWE[li][kk + "_r"]), 16)

        @blk.sync
        def _(s):
            for dst, src_ in [(a_wxp, wxp_r), (wdt_sb, wdt_r), (binu, bin_ur),
                              (binz, bin_zr), (bconvh, bconv_r), (bdth, bdt_r),
                              (wconvh, wconv_r), (bxpr, bxp_re), (alog_sb, alog_r),
                              (mb0, mb0_r), (mb1, mb1_r), (mt0, mt0_r), (mt1, mt1_r),
                              (pos_sb, pos_r), (shm_sb, shm_r), (bout_sb, bout_r)]:
                ld.sig(s.dma_start(dst, src_), 16)
            cst.wait(s, 16)
            ld.sig(s.dma_start(a_win, w_in_bf), 16)
            ld.wait(s, ld.n)

        @blk.vector
        def _(v):
            v.memset(ones1, 1.0)
            v.memset(eps, 1e-5)
            v.memset(e12, 1e-12)
            v.memset(halfpi, float(np.pi / 2))
            v.memset(attn[:, 0:128], 1.0)
            v.memset(maskL, 0.0)
            v.memset(maskR, 0.0)
            v.memset(shB, 0.0)
            v.memset(ones16, 1.0)
            v.memset(bxp0, 0.0)
            v.memset(st_sb, 0.0)
            for dc in range(4):
                v.memset(x2_sb[:, dc * 515: dc * 515 + 3], 0.0)

    with nc.Block("prep1") as blk:
        @blk.gpsimd
        def _(g):
            g.affine_select(eye, attn[:, 0:128], pattern=[[1, 128]],
                            compare_op=ALU.is_equal, fill=0.0, base=0, channel_multiplier=-1)
            g.affine_select(shA, attn[:, 0:128], pattern=[[1, 128]],
                            compare_op=ALU.is_equal, fill=0.0, base=-1, channel_multiplier=-1)
            g.affine_select(shB[:, 0:1], attn[:, 0:1], pattern=[[1, 1]],
                            compare_op=ALU.is_equal, fill=0.0, base=-127, channel_multiplier=1)
            g.affine_select(maskL, maskL, pattern=[[1, 128]], compare_op=ALU.is_ge,
                            fill=NEG, base=0, channel_multiplier=-1)
            g.affine_select(maskR, maskR, pattern=[[-1, 128]], compare_op=ALU.is_ge,
                            fill=NEG, base=0, channel_multiplier=1)
            for nn in range(16):
                g.affine_select(selv[:, nn * 128:(nn + 1) * 128], ones16,
                                pattern=[[0, 128]], compare_op=ALU.is_equal,
                                fill=0.0, base=-nn, channel_multiplier=1)
            g.iota(invf, pattern=[[1, 32]], base=0, channel_multiplier=0,
                   allow_small_or_imprecise_dtypes=True)
            g.iota(lnbase, pattern=[[1, 32]], base=0, channel_multiplier=0,
                   allow_small_or_imprecise_dtypes=True)

    with nc.Block("prep2") as blk:
        @blk.scalar
        def _(s):
            s.copy(eyebf, eye)
            s.activation(invf, invf, AF.Exp, scale=float(-np.log(10000.0) / 32.0))
            c_pp.sig(s.activation(A_sb, alog_sb, AF.Exp))
            for dc in range(4):
                for j in range(4):
                    i = dc * 4 + j
                    s.activation(a_diag[:, i * 128:(i + 1) * 128], eye, AF.Copy,
                                 scale=wconvh[:, i:i + 1])
            c_pp.wait(s, 2)
            c_pp.sig(s.activation(lnbase, lnbase, AF.Ln))

        @blk.vector
        def _(v):
            c_pp.wait(v, 1)
            v.tensor_scalar(A_sb, A_sb, -1.0, None, ALU.mult)
            c_pp.sig(v.tensor_scalar(lnbase, lnbase, 2.0, 25.6, ALU.mult, ALU.add))
            c_pp.wait(v, 3)
            v.tensor_scalar(lnbase, lnbase, float(np.log(89.6)), None, ALU.subtract)

    for k in range(NCA):
        oc0 = k * Tc
        B = dict(ld=ld.n, st=st.n, tr=a_tr.n, mm=a_mm.n, xt=a_xt.n, ur=a_ur.n,
                 sz=a_sz.n, uc=a_uc.n, xdb=a_xdb.n)
        with nc.Block(f"A{k}") as blk:
            @blk.sync
            def _(s, k=k, B=B, oc0=oc0):
                for ti in range(4):
                    ld.sig(s.dma_start(x_sb[:, ti * D:(ti + 1) * D],
                                       x_a[k * Tc + ti * 128: k * Tc + (ti + 1) * 128, :]), 16)
                a_sz.wait(s, B["sz"] + 4)
                for dc in range(4):
                    st.sig(s.dma_start(sz_h[dc * 128:(dc + 1) * 128, oc0:oc0 + Tc],
                                       a_szt[:, dc * Tc:(dc + 1) * Tc]), 16)
                a_uc.wait(s, B["uc"] + 4)
                for dc in range(4):
                    st.sig(s.dma_start(u_h[dc * 128:(dc + 1) * 128, oc0:oc0 + Tc],
                                       a_u[:, dc * Tc:(dc + 1) * Tc]), 16)
                a_xdb.wait(s, B["xdb"] + 1)
                st.sig(s.dma_start(xdbp_h[:, oc0:oc0 + Tc], a_xdb_t), 16)
                st.wait(s, st.n)

            @blk.tensor
            def _(t, k=k, B=B):
                ld.wait(t, B["ld"] + 64)
                for dj in range(8):
                    if dj > 0:
                        a_xt.wait(t, B["xt"] + dj)
                    for ti in range(4):
                        inst = t.transpose(p1[:, ti * 128:(ti + 1) * 128],
                                           x_sb[:, ti * D + dj * 128: ti * D + (dj + 1) * 128], eye)
                    a_tr.sig(inst)
                a_xt.wait(t, B["xt"] + 8)
                for oc in range(8):
                    ps = p2 if (oc % 2 == 0) else p3
                    if oc >= 2:
                        if oc - 2 < 4:
                            a_ur.wait(t, B["ur"] + oc - 1)
                        else:
                            a_sz.wait(t, B["sz"] + oc - 5)
                    for dc in range(8):
                        inst = t.matmul(ps, a_win[:, dc * 1024 + oc * 128: dc * 1024 + (oc + 1) * 128],
                                        a_xT[:, dc * Tc:(dc + 1) * Tc],
                                        start=(dc == 0), stop=(dc == 7))
                    a_mm.sig(inst)
                a_ur.wait(t, B["ur"] + 4)
                for dc in range(4):
                    ps = p2 if (dc % 2 == 0) else p3
                    if dc < 2:
                        a_sz.wait(t, B["sz"] + 3 + dc)
                    else:
                        a_uc.wait(t, B["uc"] + dc - 1)
                    for j in range(4):
                        i = dc * 4 + j
                        inst = t.matmul(ps, a_diag[:, i * 128:(i + 1) * 128],
                                        a_uraw[:, dc * (Tc + 3) + j: dc * (Tc + 3) + j + Tc],
                                        start=(j == 0), stop=(j == 3))
                    a_mm.sig(inst)
                a_uc.wait(t, B["uc"] + 4)
                for dc in range(4):
                    inst = t.matmul(pD48, a_wxp[:, dc * 48:(dc + 1) * 48],
                                    a_u[:, dc * Tc:(dc + 1) * Tc],
                                    start=(dc == 0), stop=(dc == 3))
                a_mm.sig(inst)

            @blk.scalar
            def _(s, k=k, B=B):
                for dj in range(8):
                    a_tr.wait(s, B["tr"] + dj + 1)
                    a_xt.sig(s.copy(a_xT[:, dj * Tc:(dj + 1) * Tc], p1))
                for oc in range(4, 8):
                    ps = p2 if (oc % 2 == 0) else p3
                    a_mm.wait(s, B["mm"] + oc + 1)
                    a_sz.sig(s.activation(a_szt[:, (oc - 4) * Tc:(oc - 3) * Tc],
                                          ps, AF.Silu, bias=binz[:, oc - 4:oc - 3]))
                for dc in range(4):
                    ps = p2 if (dc % 2 == 0) else p3
                    a_mm.wait(s, B["mm"] + 8 + dc + 1)
                    a_uc.sig(s.activation(a_u[:, dc * Tc:(dc + 1) * Tc], ps, AF.Silu,
                                          bias=bconvh[:, dc:dc + 1]))

            @blk.vector
            def _(v, k=k, B=B):
                for oc in range(4):
                    ps = p2 if (oc % 2 == 0) else p3
                    a_mm.wait(v, B["mm"] + oc + 1)
                    a_ur.sig(v.tensor_scalar(a_uraw[:, oc * (Tc + 3) + 3: oc * (Tc + 3) + 3 + Tc],
                                             ps, binu[:, oc:oc + 1], None, ALU.add))
                a_mm.wait(v, B["mm"] + 13)
                a_xdb.sig(v.tensor_scalar(a_xdb_t, pD48, bxp0, None, ALU.add))

            @blk.gpsimd
            def _(g, B=B):
                for dc in range(4):
                    a_mm.wait(g, B["mm"] + 8 + dc + 1)
                    g.tensor_copy(a_uraw[:, dc * (Tc + 3): dc * (Tc + 3) + 3],
                                  a_uraw[:, dc * (Tc + 3) + Tc: dc * (Tc + 3) + Tc + 3])

    with nc.Block("AR") as blk:
        @blk.gpsimd
        def _(g):
            st.wait(g, st.n)
            coll.sig(g.collective_compute("AllReduce", ALU.add, replica_groups=GROUPS,
                                          ins=[xdbp_h.opt()], outs=[xdbs_h.opt()]), 1)

    Ts = 512
    for k in range(NCA):
        t0 = k * Ts
        tbs = t0 // NT
        c0 = t0 % NT
        B = dict(ld=ld.n, st=st.n, bmt=b_bmt.n, bev=b_bev.n, du=b_duc.n, dut=b_dut.n,
                 due=b_due.n, da=b_da.n, dg=b_dgc.n, cm=b_cmc.n, ce=b_ce.n, db=b_db.n,
                 sc=b_sc.n, yt=b_ytc.n, ya=b_ya.n, stc=b_stc.n, spa=b_spa.n,
                 dtm=b_dtm.n, dta=b_dta.n, xb=b_xb.n, gm=b_gm.n)
        with nc.Block(f"B{k}") as blk:
            @blk.sync
            def _(s, k=k, B=B, t0=t0, tbs=tbs, c0=c0):
                if k == 0:
                    coll.wait(s, 1)
                for pt in range(4):
                    ld.sig(s.dma_start(b_u[:, pt * Ts:(pt + 1) * Ts],
                                       u_h[pt * 128:(pt + 1) * 128, t0:t0 + Ts]), 16)
                ld.sig(s.dma_start(attn[0:16, 0:512], xdbs_h[0:16, t0:t0 + Ts]), 16)
                ld.sig(s.dma_start(b_bm, xdbs_h[16:32, t0:t0 + Ts]), 16)
                ld.sig(s.dma_start(b_cm, xdbs_h[32:48, t0:t0 + Ts]), 16)
                b_dut.wait(s, B["dut"] + 4)
                for pt in range(4):
                    ld.sig(s.dma_start(b_szc[:, pt * Ts:(pt + 1) * Ts],
                                       sz_h[pt * 128:(pt + 1) * 128, t0:t0 + Ts]), 16)
                b_gm.wait(s, B["gm"] + 8)
                for pt in range(4):
                    st.sig(s.dma_start(rs_gin[tbs, pt * 128:(pt + 1) * 128, c0:c0 + Ts],
                                       b_gst[:, pt * Ts:(pt + 1) * Ts]), 16)
                    st.sig(s.dma_start(rs_gin[tbs, 512 + pt * 128:512 + (pt + 1) * 128, c0:c0 + Ts],
                                       b_gst[:, 2048 + pt * Ts:2048 + (pt + 1) * Ts]), 16)
                st.wait(s, st.n)

            @blk.tensor
            def _(t, B=B):
                b_xb.wait(t, B["xb"] + 1)
                for pt in range(4):
                    if pt > 0:
                        b_spa.wait(t, B["spa"] + 3 * (pt - 1) + 1)
                    b_dtm.sig(t.matmul(p_cr, wdt_sb[0:16, pt * 128:(pt + 1) * 128],
                                       attn[0:16, 0:512], start=True, stop=True))
                b_xb.wait(t, B["xb"] + 2)
                for tb in range(4):
                    inst = t.transpose(p_tr[:, tb * 16:(tb + 1) * 16],
                                       b_bm[0:16, tb * 128:(tb + 1) * 128], eye[0:16, 0:16])
                b_bmt.sig(inst)
                b_bev.wait(t, B["bev"] + 1)
                for tt_ in range(4):
                    if tt_ > 0:
                        b_due.wait(t, B["due"] + tt_)
                    b_duc.wait(t, B["du"] + 4)
                    for pt in range(4):
                        inst = t.transpose(p_tr[:, pt * 128:(pt + 1) * 128],
                                           b_du[:, pt * Ts + tt_ * 128: pt * Ts + (tt_ + 1) * 128], eye)
                    b_dut.sig(inst)
                for nn in range(16):
                    par = nn % 2
                    b_ce.wait(t, B["ce"] + nn)
                    if nn == 0:
                        b_spa.wait(t, B["spa"] + 10)
                    if nn >= 2:
                        b_ytc.wait(t, B["yt"] + 4 * (nn - 1))
                    b_cmc.sig(t.matmul(p_cr, selv[:, nn * 128:(nn + 1) * 128], b_cm,
                                       start=True, stop=True))
                    b_dgc.wait(t, B["dg"] + nn + 1)
                    b_due.wait(t, B["due"] + 4)
                    for pt in range(4):
                        if nn > 0 or pt >= 2:
                            b_sc.wait(t, B["sc"] + 4 * nn + pt - 1)
                        for tb in range(4):
                            inst = t.matmul(p_db[pt % 2][:, tb * 128:(tb + 1) * 128],
                                            b_duT[:, tb * Ts + pt * 128: tb * Ts + (pt + 1) * 128],
                                            b_dg[par][:, tb * 128:(tb + 1) * 128],
                                            start=True, stop=True)
                        b_db.sig(inst)
                    if nn > 0:
                        parp = (nn - 1) % 2
                        for pt in range(4):
                            b_ytc.wait(t, B["yt"] + 4 * (nn - 1) + pt + 1)
                            b_ya.sig(t.matmul(p_y[pt], eye, b_yt[parp][:, pt * Ts:(pt + 1) * Ts],
                                              start=(nn - 1 == 0), stop=False))
                for pt in range(4):
                    b_ytc.wait(t, B["yt"] + 60 + pt + 1)
                    b_ya.sig(t.matmul(p_y[pt], eye, b_yt[1][:, pt * Ts:(pt + 1) * Ts],
                                      start=False, stop=True))

            @blk.scalar
            def _(s_, B=B):
                for pt in range(4):
                    b_dtm.wait(s_, B["dtm"] + pt + 1)
                    if pt > 0:
                        b_dta.wait(s_, B["dta"] + pt)
                    s_.activation(spA, p_cr, AF.Abs, bias=bdth[:, pt:pt + 1])
                    b_spa.sig(s_.activation(spB, p_cr, AF.Relu, bias=bdth[:, pt:pt + 1]))
                    b_spa.wait(s_, b_spa.n)
                    b_spa.sig(s_.activation(b_dt[:, pt * Ts:(pt + 1) * Ts], spA, AF.Exp,
                                            scale=-1.0))
                    b_spa.wait(s_, b_spa.n)
                    b_spa.sig(s_.activation(spA, b_dt[:, pt * Ts:(pt + 1) * Ts], AF.Ln,
                                            bias=1.0))
                b_bmt.wait(s_, B["bmt"] + 1)
                b_bev.sig(s_.copy(bmT, p_tr[:, 0:64]))
                for tt_ in range(4):
                    b_dut.wait(s_, B["dut"] + tt_ + 1)
                    b_due.sig(s_.copy(b_duT[:, tt_ * Ts:(tt_ + 1) * Ts], p_tr))
                for nn in range(16):
                    par = nn % 2
                    if nn >= 1:
                        parp = (nn - 1) % 2
                        for pt in range(4):
                            b_sc.wait(s_, B["sc"] + 4 * (nn - 1) + pt + 1)
                            b_stc.sig(s_.copy(st_sb[:, pt * 16 + nn - 1: pt * 16 + nn],
                                              b_h[parp][:, (pt + 1) * Ts - 1:(pt + 1) * Ts]))
                    if nn >= 2:
                        b_sc.wait(s_, B["sc"] + 4 * (nn - 1))
                    for pt in range(4):
                        b_da.sig(s_.activation(b_dA[par][:, pt * Ts:(pt + 1) * Ts],
                                               b_dt[:, pt * Ts:(pt + 1) * Ts], AF.Exp,
                                               scale=A_sb[:, pt * 16 + nn: pt * 16 + nn + 1]))
                    if nn >= 2:
                        b_db.wait(s_, B["db"] + 4 * (nn - 1))
                    for tb in range(4):
                        inst = s_.activation(b_dg[par][:, tb * 128:(tb + 1) * 128], eye, AF.Copy,
                                             scale=bmT[:, tb * 16 + nn: tb * 16 + nn + 1])
                    b_dgc.sig(inst)
                    b_cmc.wait(s_, B["cm"] + nn + 1)
                    if nn >= 2:
                        b_ytc.wait(s_, B["yt"] + 4 * (nn - 1))
                    b_ce.sig(s_.copy(b_crep[par], p_cr))
                for pt in range(4):
                    b_sc.wait(s_, B["sc"] + 60 + pt + 1)
                    b_stc.sig(s_.copy(st_sb[:, pt * 16 + 15: pt * 16 + 16],
                                      b_h[1][:, (pt + 1) * Ts - 1:(pt + 1) * Ts]))

            @blk.vector
            def _(v, B=B):
                ld.wait(v, B["ld"] + 112)
                b_xb.sig(v.tensor_scalar(attn[0:16, 0:512], attn[0:16, 0:512],
                                         bxpr[:, 0:1], None, ALU.add))
                b_xb.sig(v.tensor_scalar(b_bm, b_bm, bxpr[:, 1:2], None, ALU.add))
                b_xb.sig(v.tensor_scalar(b_cm, b_cm, bxpr[:, 2:3], None, ALU.add))
                for pt in range(4):
                    b_spa.wait(v, B["spa"] + 3 * pt + 3)
                    b_dta.sig(v.tensor_add(b_dt[:, pt * Ts:(pt + 1) * Ts], spA, spB))
                for pt in range(4):
                    b_duc.sig(v.tensor_mul(b_du[:, pt * Ts:(pt + 1) * Ts],
                                           b_dt[:, pt * Ts:(pt + 1) * Ts],
                                           b_u[:, pt * Ts:(pt + 1) * Ts]))
                for nn in range(16):
                    par = nn % 2
                    for pt in range(4):
                        b_da.wait(v, B["da"] + 4 * nn + pt + 1)
                        b_db.wait(v, B["db"] + 4 * nn + pt + 1)
                        if nn >= 2:
                            b_ytc.wait(v, B["yt"] + 4 * (nn - 1))
                            b_stc.wait(v, B["stc"] + 4 * (nn - 2) + pt + 1)
                        b_sc.sig(v.tensor_tensor_scan(b_h[par][:, pt * Ts:(pt + 1) * Ts],
                                                      b_dA[par][:, pt * Ts:(pt + 1) * Ts],
                                                      p_db[pt % 2],
                                                      st_sb[:, pt * 16 + nn: pt * 16 + nn + 1],
                                                      ALU.mult, ALU.add))
                ld.wait(v, B["ld"] + 176)
                b_ya.wait(v, B["ya"] + 64)
                b_ytc.wait(v, B["yt"] + 64)
                for pt in range(4):
                    v.tensor_mul(b_gtmp, p_y[pt], b_szc[:, pt * Ts:(pt + 1) * Ts])
                    b_gm.sig(v.tensor_scalar(b_gst[:, pt * Ts:(pt + 1) * Ts], b_gtmp,
                                             mb0, None, ALU.mult))
                    b_gm.sig(v.tensor_scalar(b_gst[:, 2048 + pt * Ts:2048 + (pt + 1) * Ts],
                                             b_gtmp, mb1, None, ALU.mult))

            @blk.gpsimd
            def _(g, B=B):
                for nn in range(16):
                    par = nn % 2
                    for pt in range(4):
                        b_sc.wait(g, B["sc"] + 4 * nn + pt + 1)
                        b_ce.wait(g, B["ce"] + nn + 1)
                        if nn >= 2:
                            b_ya.wait(g, B["ya"] + 4 * (nn - 1))
                        b_ytc.sig(g.tensor_tensor(b_yt[par][:, pt * Ts:(pt + 1) * Ts],
                                                  b_h[par][:, pt * Ts:(pt + 1) * Ts],
                                                  b_crep[par], ALU.mult))

    def wout_reg(dc):
        if dc < 4:
            return hT[:, dc * 1024:(dc + 1) * 1024]
        return big_kv[:, (dc - 4) * 1024:(dc - 3) * 1024]

    with nc.Block("RSg") as blk:
        @blk.gpsimd
        def _(g):
            st.wait(g, st.n)
            coll.sig(g.collective_compute("ReduceScatter", ALU.add, replica_groups=GROUPS,
                                          ins=[rs_gin.opt()], outs=[g_full.opt()]), 1)

        @blk.sync
        def _(s):
            st.wait(s, st.n)
            cst.wait(s, 32)
            for dc in range(8):
                ldc.sig(s.dma_start(wout_reg(dc), wout_bf[:, dc * 1024:(dc + 1) * 1024]), 16)

    for k in range(8):
        t0 = k * Tc
        B = dict(ld=ld.n, st=st.n, mm=c_mm.n, ev=c_ev.n, v=c_v.n, a=c_a.n,
                 x1=c_x1c.n, tl=c_tl.n)
        with nc.Block(f"C{k}") as blk:
            @blk.sync
            def _(s, k=k, B=B, t0=t0):
                if k == 0:
                    coll.wait(s, 2)
                for dc in range(8):
                    ld.sig(s.dma_start(c_gsb[:, dc * Tc:(dc + 1) * Tc],
                                       g_full[dc * 128:(dc + 1) * 128, t0:t0 + Tc]), 16)
                for m in range(4):
                    ld.sig(s.dma_start(x_sb[:, m * D:(m + 1) * D],
                                       x_own[t0 + m * 128: t0 + (m + 1) * 128, :]), 16)
                for m in range(4):
                    c_x1c.wait(s, B["x1"] + m + 1)
                    st.sig(s.dma_start(x1own_h[t0 + m * 128: t0 + (m + 1) * 128, :],
                                       c_x1[:, m * D:(m + 1) * D]), 16)
                    if k >= 6:
                        rr = t0 + m * 128 - 3072
                        c_tl.wait(s, B["tl"] + 4 * m + 4)
                        st.sig(s.dma_start(rs_t_in[0, rr:rr + 128, 0:512], tt1), 16)
                        st.sig(s.dma_start(rs_t_in[0, rr:rr + 128, 512:1024], tt2), 16)
                        st.sig(s.dma_start(rs_t_in[1, rr:rr + 128, 0:512], x3_sb[:, 0:512]), 16)
                        st.sig(s.dma_start(rs_t_in[1, rr:rr + 128, 512:1024], x3_sb[:, 512:1024]), 16)
                st.wait(s, st.n)

            @blk.tensor
            def _(t, B=B, k=k):
                if k == 0:
                    ldc.wait(t, 128)
                ld.wait(t, B["ld"] + 192)
                for m in range(4):
                    for oh in range(2):
                        ps = pM[2 * (m % 2) + oh]
                        if m >= 2:
                            c_ev.wait(t, B["ev"] + 2 * (m - 1) + oh + 1)
                        for dc in range(8):
                            t.matmul(ps, c_gsb[:, dc * Tc + m * 128: dc * Tc + m * 128 + 128],
                                     wout_at(dc, oh), start=(dc == 0), stop=False)
                        inst = t.matmul(ps, ones1, bout_sb[:, oh * 512:(oh + 1) * 512],
                                        start=False, stop=True)
                        c_mm.sig(inst)

            @blk.vector
            def _(v, B=B, k=k):
                ld.wait(v, B["ld"] + 192)
                for m in range(4):
                    for oh in range(2):
                        c_mm.wait(v, B["mm"] + 2 * m + oh + 1)
                        c_ev.sig(v.tensor_add(x2_sb[:, m * D + oh * 512: m * D + (oh + 1) * 512],
                                              pM[2 * (m % 2) + oh],
                                              x_sb[:, m * D + oh * 512: m * D + (oh + 1) * 512]))
                    c_ev.wait(v, c_ev.n)
                    c_v.sig(v.bn_stats(bns[:, 0:6], x2_sb[:, m * D: m * D + 512]))
                    c_v.sig(v.bn_stats(bns[:, 6:12], x2_sb[:, m * D + 512: m * D + 1024]))
                    c_v.wait(v, c_v.n)
                    c_v.sig(v.bn_aggr(mv, bns))
                    c_a.wait(v, B["a"] + m + 1)
                    c_v.sig(v.reciprocal(rs_t, sd))
                    c_v.wait(v, c_v.n)
                    c_x1c.sig(v.tensor_scalar(c_x1[:, m * D:(m + 1) * D],
                                              x2_sb[:, m * D:(m + 1) * D],
                                              mv[:, 0:1], rs_t, ALU.subtract, ALU.mult))
                    if k >= 6:
                        st.wait(v, B["st"] + 80 * m)
                        c_tl.sig(v.tensor_scalar(tt1, c_x1[:, m * D: m * D + 512],
                                                 mt0, None, ALU.mult))
                        c_tl.sig(v.tensor_scalar(tt2, c_x1[:, m * D + 512: m * D + 1024],
                                                 mt0, None, ALU.mult))
                        c_tl.sig(v.tensor_scalar(x3_sb[:, 0:512], c_x1[:, m * D: m * D + 512],
                                                 mt1, None, ALU.mult))
                        c_tl.sig(v.tensor_scalar(x3_sb[:, 512:1024],
                                                 c_x1[:, m * D + 512: m * D + 1024],
                                                 mt1, None, ALU.mult))

            @blk.scalar
            def _(s, B=B):
                for m in range(4):
                    c_v.wait(s, B["v"] + 4 * m + 3)
                    c_a.sig(s.activation(sd, mv[:, 1:2], AF.Sqrt, bias=eps))

    with nc.Block("RSx1") as blk:
        @blk.gpsimd
        def _(g):
            st.wait(g, st.n)
            coll.sig(g.collective_compute("ReduceScatter", ALU.add, replica_groups=GROUPS,
                                          ins=[rs_t_in.opt()], outs=[halo_h.opt()]), 1)

    em = Em(nc)
    for qn, kind, inc in [("pe", "tensor", 1), ("act", "scalar", 1), ("dve", "vector", 1),
                          ("gp", "gpsimd", 1), ("ld", "sync", 16), ("ldw", "sync", 16),
                          ("st", "sync", 16)]:
        em.queue(qn, kind, inc)

    V = dict(eye=eye, eyebf=eyebf, ones1=ones1, eps=eps, e12=e12, halfpi=halfpi,
             maskL=maskL, maskR=maskR, shA=shA, shB=shB, wqkv_sb=wqkv_sb, wo_sb=wo_sb,
             w2s_sb=w2s_sb, w1a_sb=w1a_sb, w1g_sb=w1g_sb, b1a_sb=b1a_sb, b1g_sb=b1g_sb,
             b2_sb=b2_sb, gam_rep=gam_rep, bet_rep=bet_rep, lngrow_sb=lngrow_sb,
             lnbrow_sb=lnbrow_sb, qsrow_sb=qsrow_sb, ksrow_sb=ksrow_sb, qscrep=qscrep,
             kscrep=kscrep, pos_sb=pos_sb, mf_sb=mf_sb, shm_sb=shm_sb, invf=invf,
             lnbase=lnbase, frq=frq, rrf=rrf, ang=ang, angc=angc, rri=rri, snt=snt,
             cstt=cstt, pwv=pwv, sclog=sclog, scl=scl, iscl=iscl, csx=csx, ssx=ssx,
             cski=cski, sski=sski, csq2=csq2, nsq2=nsq2, sq2=sq2, csk2=csk2, nsk2=nsk2,
             sk2=sk2, x_sb=x_sb, x1T=x1T, q_sb=q_sb, k_sb=k_sb, qss=qss, kss=kss,
             rq=rq, rk=rk, qr2=qr2, kr2=kr2, tt1=tt1, tt2=tt2, qT=qT, kring=kring,
             vring=vring, s8=s8, nm=nm, nmz=nmz, P_sb=P_sb, PT_sb=PT_sb, sume=sume,
             rsume=rsume, attn=attn, attnT=attnT, x2_sb=x2_sb, x2tail=x2tail, bns=bns,
             mv=mv, sd=sd, rs=rs_t, hh_sb=hh_sb, hT=hT, a_sb=a_sb, gg_sb=gg_sb,
             gate=gate, x3_sb=x3_sb, mres_sb=mres_sb, mt0=mt0, mt1=mt1,
             p1=p1, p2=p2, p3=p3, pS=pS, pT=pT, pVa=pVa, pVb=pVb,
             collsem=coll.sem, cstw=(cst.sem, CSTN))

    R = Res
    S = dict(r_eye=R(), r_masks=R(), r_sh=R(), r_invf=R(), r_lnb=R(),
             r_gam=R(), r_bet=R(), r_qsc=R(), r_ksc=R(),
             r_x=[R() for _ in range(4)], r_x1T=[R() for _ in range(8)],
             r_q=R(), r_k=R(), r_qss=R(), r_kss=R(), r_rq=R(), r_rk=R(),
             r_qr2=R(), r_kr2=R(), r_tt1=R(), r_tt2=R(), r_qT=R(),
             r_kring=[R() for _ in range(NRING)], r_vring=[R() for _ in range(NRING)],
             r_s8=R(), r_nm=R(), r_nmz=R(), r_P=R(), r_PT=R(), r_sume=R(), r_rsume=R(),
             r_attn=R(), r_attnT=R(), r_x2=[R() for _ in range(4)], r_x2tail=R(),
             r_hh=[R() for _ in range(4)], r_hT=R(), r_a=R(), r_gg=R(),
             r_gate=[R() for _ in range(NHh)], r_x3=R(),
             r_w1a=[R(), R()], r_w1g=[R(), R()],
             r_p1=R(), r_p2=R(), r_p3=R(), r_pS=R(), r_pT=R(), r_pV=[R(), R()],
             r_tabs=R(), r_lns=R(), r_wq=R(), r_wo=R(), r_w2s=[R(), R()],
             r_b1a=R(), r_b1g=R(), r_b2=R(), r_lng=R(), r_lnbr=R(), r_qsr=R(),
             r_ksr=R(), r_mf=R())

    xrow0 = [[] for _ in range(8)]
    tails0 = []
    W0 = dict(FWB[0])
    for kk in ("b1a_r", "b1g_r", "b2_r", "lng_row", "lnb_row", "qs_row", "ks_row"):
        W0[kk] = FWE[0][kk]
    W0["mf_r"] = mf_r
    W0["rs_t2"] = rs_t2_in

    def x_at0(r):
        if r < H4:
            return halo_h[r:r + 128, :]
        return x1own_h[r - H4:r - H4 + 128, :]

    emit_ffn(nc, em, V, S, 0, W0, x_at0, x2own_h, None, xrow0, tails0, None, False)

    coll2 = em.t("gp", lambda g: g.collective_compute(
        "ReduceScatter", ALU.add, replica_groups=GROUPS,
        ins=[rs_t2_in.opt()], outs=[halo2_h.opt()]), tails0)

    W1 = dict(FWB[1])
    for kk in ("b1a_r", "b1g_r", "b2_r", "lng_row", "lnb_row", "qs_row", "ks_row"):
        W1[kk] = FWE[1][kk]
    W1["mres_ap"] = x1own_h

    def x_at1(r):
        if r < H4:
            return halo2_h[r:r + 128, :]
        return x2own_h[r - H4:r - H4 + 128, :]

    emit_ffn(nc, em, V, S, 1, W1, x_at1, xo, xrow0, None, [], coll2, True)

    with nc.Block("emmain") as blk:
        em.flush(blk)
    with nc.Block("fin") as blk:
        @blk.sync
        def _(s):
            s.wait_ge(em.qs["st"]["ch"].sem, em.qs["st"]["ch"].n)
    return nc


def _ffn_arrays(ins, li):
    HID = 2816
    NHh = 22
    wqkv = ins["L0_ffn_wqkv"][li]; wo_ = ins["L0_ffn_wo"][li]
    w1 = ins["L0_ffn_w1"][li]; b1 = ins["L0_ffn_b1"][li]
    w2 = ins["L0_ffn_w2"][li]; b2 = ins["L0_ffn_b2"][li]
    lng = ins["L0_ffn_ln_g"][li]; lnb = ins["L0_ffn_ln_b"][li]
    qs = ins["L0_ffn_q_scale"][li]; ks = ins["L0_ffn_k_scale"][li]
    w1a = np.zeros((1024, HID), np.float32); w1a[:, :2730] = w1[:, :2730]
    w1g = np.zeros((1024, HID), np.float32); w1g[:, :2730] = w1[:, 2730:]
    b1a = np.zeros(HID, np.float32); b1a[:2730] = b1[:2730]
    b1g = np.zeros(HID, np.float32); b1g[:2730] = b1[2730:]
    w2p = np.zeros((HID, 1024), np.float32); w2p[:2730] = w2
    rsh = lambda w: np.ascontiguousarray(
        w.reshape(w.shape[0] // 128, 128, w.shape[1]).transpose(1, 0, 2)
        .reshape(128, (w.shape[0] // 128) * w.shape[1]), dtype=np.float32)
    rshh = lambda w: np.ascontiguousarray(
        w.reshape(8, 128, NHh, 128).transpose(1, 2, 0, 3)
        .reshape(128, NHh * 1024), dtype=np.float32)
    return {f"wqkv_r{li}": rsh(wqkv), f"wo_r{li}": rsh(wo_),
            f"w1a_r{li}": rshh(w1a), f"w1g_r{li}": rshh(w1g), f"w2_r{li}": rsh(w2p),
            f"b1a_r{li}": np.ascontiguousarray(b1a.reshape(NHh, 128).T),
            f"b1g_r{li}": np.ascontiguousarray(b1g.reshape(NHh, 128).T),
            f"b2_r{li}": b2.reshape(1, 1024).astype(np.float32),
            f"lng_row{li}": lng.reshape(1, 1024).astype(np.float32),
            f"lnb_row{li}": lnb.reshape(1, 1024).astype(np.float32),
            f"qs_row{li}": qs.reshape(1, 64).astype(np.float32),
            f"ks_row{li}": ks.reshape(1, 64).astype(np.float32)}


def stage_fused(ins):
    import ml_dtypes
    x = ins["L0_attn_x"]
    W_in = ins["L0_attn_W_in"]; b_in = ins["L0_attn_b_in"]
    W_conv = ins["L0_attn_W_conv"]; b_conv = ins["L0_attn_b_conv"]
    W_xp = ins["L0_attn_W_xp"]; b_xp = ins["L0_attn_b_xp"]
    W_dt = ins["L0_attn_W_dt"]; b_dt = ins["L0_attn_b_dt"]
    A_log = ins["L0_attn_A_log"]
    W_out = ins["L0_attn_W_out"]; b_out = ins["L0_attn_b_out"]
    wout_r = np.ascontiguousarray(
        W_out.reshape(8, 128, 1024).transpose(1, 0, 2).reshape(128, 8 * 1024), dtype=np.float32)
    bout_r = np.ascontiguousarray(b_out.reshape(1, 1024), dtype=np.float32)
    ffn = [_ffn_arrays(ins, 0), _ffn_arrays(ins, 1)]
    r4 = lambda v: np.ascontiguousarray(v.reshape(4, 128).T, dtype=np.float32)
    maps = []
    for c in range(NCORE):
        b, h = c // 2, c % 2
        dh = h
        s0 = h * NT
        d0 = dh * 512
        wcat = np.concatenate([W_in[:, d0:d0 + 512], W_in[:, 1024 + d0:1024 + d0 + 512]], axis=1)
        w_in_h = np.ascontiguousarray(
            wcat.reshape(8, 128, 1024).transpose(1, 0, 2).reshape(128, 8192), dtype=np.float32)
        mp = {"x_a": np.ascontiguousarray(x[b], dtype=np.float32),
              "x_own": np.ascontiguousarray(x[b, s0:s0 + NT, :], dtype=np.float32),
              "w_in_h": w_in_h,
              "bin_uh": r4(b_in[d0:d0 + 512]),
              "bin_zh": r4(b_in[1024 + d0:1024 + d0 + 512]),
              "bconvh": r4(b_conv[d0:d0 + 512]),
              "wconvh": np.ascontiguousarray(
                  W_conv[d0:d0 + 512, 0, :].reshape(4, 128, 4).transpose(1, 0, 2)
                  .reshape(128, 16), dtype=np.float32),
              "wxp_h": np.ascontiguousarray(
                  W_xp[d0:d0 + 512].reshape(4, 128, 48).transpose(1, 0, 2)
                  .reshape(128, 192), dtype=np.float32),
              "bxp_r": np.ascontiguousarray(b_xp.reshape(3, 16).T, dtype=np.float32),
              "wdt_h": np.ascontiguousarray(W_dt[:, d0:d0 + 512], dtype=np.float32),
              "bdth": r4(b_dt[d0:d0 + 512]),
              "alog_h": np.ascontiguousarray(
                  A_log[d0:d0 + 512].reshape(4, 128, 16).transpose(1, 0, 2)
                  .reshape(128, 64), dtype=np.float32),
              "mb0": np.full((128, 1), 1.0 if dh == 0 else 0.0, np.float32),
              "mb1": np.full((128, 1), 1.0 if dh == 1 else 0.0, np.float32),
              "mt0": np.full((128, 1), 1.0 if h == 1 else 0.0, np.float32),
              "mt1": np.full((128, 1), 1.0 if h == 0 else 0.0, np.float32),
              "wout_r": wout_r, "bout_r": bout_r}
        pos = np.zeros((128, 10 * 4), np.float32)
        for ti in range(10 * 4):
            pos[:, ti] = s0 - H4 + ti * 128 + np.arange(128)
        mf = np.zeros((128, 4 * 640), ml_dtypes.bfloat16)
        if h == 0:
            for m in range(4):
                cut = 512 - m * 128
                mf[:, m * 640: m * 640 + cut] = NEG
        shm = (np.ones((128, 1), np.float32) if h == 1
               else np.zeros((128, 1), np.float32))
        mp["pos_r"] = pos; mp["mf_r"] = mf; mp["shm_r"] = shm
        mp.update(ffn[0]); mp.update(ffn[1])
        maps.append(mp)
    return maps


def kernel(**inputs):
    nc = build_fused()
    maps = stage_fused(inputs)
    res = run_bass_kernel_spmd(nc, maps, core_ids=list(range(NCORE)))
    out = np.empty((4, 2 * NT, D), np.float32)
    for c in range(NCORE):
        b, h = c // 2, c % 2
        out[b, h * NT:(h + 1) * NT] = res.results[c]["xo_t"]
    return out


# version 06
# speedup vs baseline: 1.0793x; 1.0793x over previous
import numpy as np
import concourse.bass as bass
import concourse.mybir as mybir
from concourse.bass_utils import run_bass_kernel_spmd

F32 = mybir.dt.float32
BF16 = mybir.dt.bfloat16
I32 = mybir.dt.int32
AF = mybir.ActivationFunctionType
ALU = mybir.AluOpType
AX = mybir.AxisListType

NCORE = 8
D = 1024
NT = 4096
HALO1 = 512
NEG = -1.0e30


class Ch:

    def __init__(self, nc, name):
        self.sem = nc.alloc_semaphore(name)
        self.n = 0

    def sig(self, inst, k=1):
        inst.then_inc(self.sem, k)
        self.n += k
        return self.n

    def wait(self, eng, n):
        eng.wait_ge(self.sem, n)


def make_eye(nc, blk, eye, scr, ch):
    @blk.vector
    def _(v):
        ch.sig(v.memset(scr, 1.0))

    @blk.gpsimd
    def _(g):
        ch.wait(g, ch.n)
        ch.sig(g.affine_select(eye, scr, pattern=[[1, 128]], compare_op=ALU.is_equal,
                               fill=0.0, base=0, channel_multiplier=-1))


def build_l1():
    nc = bass.Bass("TRN2", target_bir_lowering=False)
    T = NT + HALO1
    Tc = 512
    NCH = T // Tc

    x_in = nc.dram_tensor("x_l1", [T, D], F32, kind="ExternalInput").ap()
    w_in_r = nc.dram_tensor("w_in_r", [128, 8 * 2048], F32, kind="ExternalInput").ap()
    bin_u = nc.dram_tensor("bin_u", [128, 8], F32, kind="ExternalInput").ap()
    bin_z = nc.dram_tensor("bin_z", [128, 8], F32, kind="ExternalInput").ap()
    bconv = nc.dram_tensor("bconv", [128, 8], F32, kind="ExternalInput").ap()
    wconv = nc.dram_tensor("wconv", [128, 32], F32, kind="ExternalInput").ap()
    wxp_r = nc.dram_tensor("wxp_r", [128, 8 * 48], F32, kind="ExternalInput").ap()
    bxp = nc.dram_tensor("bxp", [48, 1], F32, kind="ExternalInput").ap()
    wdt_r = nc.dram_tensor("wdt_r", [16, 1024], F32, kind="ExternalInput").ap()
    bdt = nc.dram_tensor("bdt", [128, 8], F32, kind="ExternalInput").ap()
    umask = nc.dram_tensor("umask", [128, Tc], F32, kind="ExternalInput").ap()

    u_out = nc.dram_tensor("u_out", [D, NT], F32, kind="ExternalOutput").ap()
    dt_out = nc.dram_tensor("dt_out", [D, NT], F32, kind="ExternalOutput").ap()
    sz_out = nc.dram_tensor("sz_out", [D, NT], F32, kind="ExternalOutput").ap()
    bc_out = nc.dram_tensor("bc_out", [32, NT], F32, kind="ExternalOutput").ap()

    w_in_bf_d = nc.dram_tensor("w_in_bf_d", [128, 8 * 2048], BF16).ap()

    sb = lambda n, sh, dt=F32: nc.alloc_sbuf_tensor(n, sh, dt).ap()
    w_in_sb = sb("w_in_sb", [128, 8 * 2048], BF16)
    wxp_sb = sb("wxp_sb", [128, 8 * 48])
    wdt_sb = sb("wdt_sb", [16, 1024])
    binu_sb = sb("binu_sb", [128, 8]); binz_sb = sb("binz_sb", [128, 8])
    bconv_sb = sb("bconv_sb", [128, 8]); wconv_sb = sb("wconv_sb", [128, 32])
    bxp_sb = sb("bxp_sb", [48, 1]); bdt_sb = sb("bdt_sb", [128, 8])
    umask_sb = sb("umask_sb", [128, Tc])
    eye = sb("eye", [128, 128]); scr128 = sb("scr128", [128, 128])
    diagc = sb("diagc", [128, 32 * 128])
    xtok = sb("xtok", [128, 4 * D])
    xT = sb("xT", [128, 8 * Tc], BF16)
    uraw = sb("uraw", [128, 8 * (Tc + 3)])
    u_sb = sb("u_sb", [128, 8 * Tc])
    sz_sb = sb("sz_sb", [128, 8 * Tc])
    xdb = sb("xdb", [48, Tc])
    dt_sb = sb("dt_sb", [128, 8 * Tc])
    sp1 = sb("sp1", [128, Tc]); sp2 = sb("sp2", [128, Tc]); sp3 = sb("sp3", [128, Tc])

    pA = nc.alloc_psum_tensor("pA", [128, 512], F32).ap()
    pB = nc.alloc_psum_tensor("pB", [128, 512], F32).ap()
    pC = nc.alloc_psum_tensor("pC", [128, 512], F32).ap()
    pD = nc.alloc_psum_tensor("pD", [48, 512], F32).ap()

    ld = Ch(nc, "ld"); st = Ch(nc, "st"); cst = Ch(nc, "cst")
    c_tr = Ch(nc, "c_tr")
    c_mm = Ch(nc, "c_mm")
    c_xt = Ch(nc, "c_xt")
    c_ur = Ch(nc, "c_ur")
    c_sz = Ch(nc, "c_sz")
    c_uc = Ch(nc, "c_uc")
    c_xdb = Ch(nc, "c_xdb")
    c_sp = Ch(nc, "c_sp")
    c_msk = Ch(nc, "c_msk")
    c_dta = Ch(nc, "c_dta")
    c_tail = Ch(nc, "c_tail")

    with nc.Block("prep0") as blk:
        @blk.gpsimd
        def _(g):
            cst.sig(g.dma_start(out=w_in_bf_d, in_=w_in_r), 16)

        @blk.sync
        def _(s):
            for dst, src_ in [(wxp_sb, wxp_r), (wdt_sb, wdt_r), (binu_sb, bin_u),
                              (binz_sb, bin_z), (bconv_sb, bconv), (wconv_sb, wconv),
                              (bxp_sb, bxp), (bdt_sb, bdt), (umask_sb, umask)]:
                ld.sig(s.dma_start(dst, src_), 16)
            ld.wait(s, ld.n)
            cst.wait(s, cst.n)
            ld.sig(s.dma_start(w_in_sb, w_in_bf_d), 16)
            ld.wait(s, ld.n)

    with nc.Block("prep1") as blk:
        make_eye(nc, blk, eye, scr128, c_msk)

        @blk.vector
        def _(v):
            for dc in range(8):
                v.memset(uraw[:, dc * (Tc + 3): dc * (Tc + 3) + 3], 0.0)

        @blk.scalar
        def _(s):
            c_msk.wait(s, c_msk.n)
            for dc in range(8):
                for j in range(4):
                    i = dc * 4 + j
                    s.activation(diagc[:, i * 128:(i + 1) * 128], eye, AF.Copy,
                                 scale=wconv_sb[:, i:i + 1])

    for k in range(NCH):
        own = k >= 1
        oc0 = (k - 1) * Tc
        B = dict(ld=ld.n, tr=c_tr.n, mm=c_mm.n, xt=c_xt.n, ur=c_ur.n, sz=c_sz.n,
                 uc=c_uc.n, xdb=c_xdb.n, sp=c_sp.n, msk=c_msk.n, dta=c_dta.n)
        with nc.Block(f"l1c{k}") as blk:
            @blk.sync
            def _(s, k=k, B=B, oc0=oc0, own=own):
                for ti in range(4):
                    ld.sig(s.dma_start(xtok[:, ti * D:(ti + 1) * D],
                                       x_in[k * Tc + ti * 128: k * Tc + (ti + 1) * 128, :]), 16)
                if own:
                    c_sz.wait(s, B["sz"] + 8)
                    for dc in range(8):
                        st.sig(s.dma_start(sz_out[dc * 128:(dc + 1) * 128, oc0:oc0 + Tc],
                                           sz_sb[:, dc * Tc:(dc + 1) * Tc]), 16)
                    c_uc.wait(s, B["uc"] + 8)
                    for dc in range(8):
                        st.sig(s.dma_start(u_out[dc * 128:(dc + 1) * 128, oc0:oc0 + Tc],
                                           u_sb[:, dc * Tc:(dc + 1) * Tc]), 16)
                    c_xdb.wait(s, B["xdb"] + 1)
                    st.sig(s.dma_start(bc_out[:, oc0:oc0 + Tc], xdb[16:48, :]), 16)
                    for dc in range(8):
                        c_dta.wait(s, B["dta"] + dc + 1)
                        st.sig(s.dma_start(dt_out[dc * 128:(dc + 1) * 128, oc0:oc0 + Tc],
                                           dt_sb[:, dc * Tc:(dc + 1) * Tc]), 16)
                st.wait(s, st.n)

            @blk.tensor
            def _(t, k=k, B=B):
                ld.wait(t, B["ld"] + 64)
                for dj in range(8):
                    if dj > 0:
                        c_xt.wait(t, B["xt"] + dj)
                    for ti in range(4):
                        inst = t.transpose(pA[:, ti * 128:(ti + 1) * 128],
                                           xtok[:, ti * D + dj * 128: ti * D + (dj + 1) * 128], eye)
                    c_tr.sig(inst)
                c_xt.wait(t, B["xt"] + 8)
                for oc in range(16):
                    ps = pB if (oc % 2 == 0) else pC
                    if oc >= 2:
                        if oc - 2 < 8:
                            c_ur.wait(t, B["ur"] + oc - 1)
                        else:
                            c_sz.wait(t, B["sz"] + oc - 9)
                    for dc in range(8):
                        inst = t.matmul(ps, w_in_sb[:, dc * 2048 + oc * 128: dc * 2048 + (oc + 1) * 128],
                                        xT[:, dc * Tc:(dc + 1) * Tc],
                                        start=(dc == 0), stop=(dc == 7))
                    c_mm.sig(inst)
                c_ur.wait(t, B["ur"] + 8)
                if k == 0:
                    c_msk.wait(t, B["msk"] + 8)
                for dc in range(8):
                    ps = pB if (dc % 2 == 0) else pC
                    if dc < 2:
                        c_sz.wait(t, B["sz"] + 7 + dc)
                    else:
                        c_uc.wait(t, B["uc"] + dc - 1)
                    for j in range(4):
                        i = dc * 4 + j
                        inst = t.matmul(ps, diagc[:, i * 128:(i + 1) * 128],
                                        uraw[:, dc * (Tc + 3) + j: dc * (Tc + 3) + j + Tc],
                                        start=(j == 0), stop=(j == 3))
                    c_mm.sig(inst)
                c_uc.wait(t, B["uc"] + 8)
                for dc in range(8):
                    inst = t.matmul(pD, wxp_sb[:, dc * 48:(dc + 1) * 48],
                                    u_sb[:, dc * Tc:(dc + 1) * Tc],
                                    start=(dc == 0), stop=(dc == 7))
                c_mm.sig(inst)
                c_xdb.wait(t, B["xdb"] + 1)
                for dc in range(8):
                    ps = pB if (dc % 2 == 0) else pC
                    if dc < 2:
                        c_uc.wait(t, B["uc"] + 7 + dc)
                    else:
                        c_sp.wait(t, B["sp"] + 4 * (dc - 2) + 2)
                    c_mm.sig(t.matmul(ps, wdt_sb[:, dc * 128:(dc + 1) * 128], xdb[0:16, :],
                                      start=True, stop=True))

            @blk.scalar
            def _(s, k=k, B=B):
                for dj in range(8):
                    c_tr.wait(s, B["tr"] + dj + 1)
                    c_xt.sig(s.copy(xT[:, dj * Tc:(dj + 1) * Tc], pA))
                for oc in range(8, 16):
                    ps = pB if (oc % 2 == 0) else pC
                    c_mm.wait(s, B["mm"] + oc + 1)
                    c_sz.sig(s.activation(sz_sb[:, (oc - 8) * Tc:(oc - 7) * Tc],
                                          ps, AF.Silu, bias=binz_sb[:, oc - 8:oc - 7]))
                for dc in range(8):
                    ps = pB if (dc % 2 == 0) else pC
                    c_mm.wait(s, B["mm"] + 16 + dc + 1)
                    c_uc.sig(s.activation(u_sb[:, dc * Tc:(dc + 1) * Tc], ps, AF.Silu,
                                          bias=bconv_sb[:, dc:dc + 1]))
                for dc in range(8):
                    ps = pB if (dc % 2 == 0) else pC
                    c_mm.wait(s, B["mm"] + 25 + dc + 1)
                    if dc >= 1:
                        c_dta.wait(s, B["dta"] + dc)
                    c_sp.sig(s.activation(sp1, ps, AF.Abs, bias=bdt_sb[:, dc:dc + 1]))
                    c_sp.sig(s.activation(sp3, ps, AF.Relu, bias=bdt_sb[:, dc:dc + 1]))
                    c_sp.wait(s, c_sp.n)
                    c_sp.sig(s.activation(sp2, sp1, AF.Exp, scale=-1.0))
                    c_sp.wait(s, c_sp.n)
                    c_sp.sig(s.activation(sp1, sp2, AF.Ln, bias=1.0))

            @blk.vector
            def _(v, k=k, B=B):
                for oc in range(8):
                    ps = pB if (oc % 2 == 0) else pC
                    c_mm.wait(v, B["mm"] + oc + 1)
                    c_ur.sig(v.tensor_scalar(uraw[:, oc * (Tc + 3) + 3: oc * (Tc + 3) + 3 + Tc],
                                             ps, binu_sb[:, oc:oc + 1], None, ALU.add))
                if k == 0:
                    for dc in range(8):
                        c_ur.wait(v, B["ur"] + dc + 1)
                        c_msk.sig(v.tensor_mul(uraw[:, dc * (Tc + 3) + 3: dc * (Tc + 3) + 3 + Tc],
                                               uraw[:, dc * (Tc + 3) + 3: dc * (Tc + 3) + 3 + Tc],
                                               umask_sb))
                c_mm.wait(v, B["mm"] + 25)
                c_xdb.sig(v.tensor_scalar(xdb, pD[0:48, :], bxp_sb, None, ALU.add))
                for dc in range(8):
                    c_sp.wait(v, B["sp"] + 4 * (dc + 1))
                    c_dta.sig(v.tensor_add(dt_sb[:, dc * Tc:(dc + 1) * Tc], sp1, sp3))

            @blk.gpsimd
            def _(g, B=B):
                for dc in range(8):
                    c_mm.wait(g, B["mm"] + 16 + dc + 1)
                    c_tail.sig(g.tensor_copy(uraw[:, dc * (Tc + 3): dc * (Tc + 3) + 3],
                                             uraw[:, dc * (Tc + 3) + Tc: dc * (Tc + 3) + Tc + 3]))

    with nc.Block("fin") as blk:
        @blk.sync
        def _(s):
            st.wait(s, st.n)
    return nc


def stage_l1(ins):
    x = ins["L0_attn_x"]
    W_in = ins["L0_attn_W_in"]; b_in = ins["L0_attn_b_in"]
    W_conv = ins["L0_attn_W_conv"]; b_conv = ins["L0_attn_b_conv"]
    W_xp = ins["L0_attn_W_xp"]; b_xp = ins["L0_attn_b_xp"]
    W_dt = ins["L0_attn_W_dt"]; b_dt = ins["L0_attn_b_dt"]

    w_in_r = np.ascontiguousarray(
        W_in.reshape(1024, 16, 128).transpose(2, 0, 1), np.float32)
    w_in_r = np.ascontiguousarray(
        W_in.reshape(8, 128, 2048).transpose(1, 0, 2).reshape(128, 8 * 2048), dtype=np.float32)
    wxp_r = np.ascontiguousarray(
        W_xp.reshape(8, 128, 48).transpose(1, 0, 2).reshape(128, 8 * 48), dtype=np.float32)
    bin_u = np.ascontiguousarray(b_in[:1024].reshape(8, 128).T, dtype=np.float32)
    bin_z = np.ascontiguousarray(b_in[1024:].reshape(8, 128).T, dtype=np.float32)
    bconv_r = np.ascontiguousarray(b_conv.reshape(8, 128).T, dtype=np.float32)
    wconv_r = np.ascontiguousarray(
        W_conv[:, 0, :].reshape(8, 128, 4).transpose(1, 0, 2).reshape(128, 32), dtype=np.float32)
    bdt_r = np.ascontiguousarray(b_dt.reshape(8, 128).T, dtype=np.float32)
    bxp_r = np.ascontiguousarray(b_xp.reshape(48, 1), dtype=np.float32)
    wdt_r = np.ascontiguousarray(W_dt, dtype=np.float32)

    maps = []
    for c in range(NCORE):
        b, half = c // 2, c % 2
        s0 = half * NT
        xs = np.zeros((NT + HALO1, D), np.float32)
        lo = s0 - HALO1
        src_lo = max(lo, 0)
        xs[src_lo - lo:, :] = x[b, src_lo: s0 + NT, :]
        um = (np.zeros((128, 512), np.float32) if half == 0
              else np.ones((128, 512), np.float32))
        maps.append({"x_l1": xs, "w_in_r": w_in_r, "bin_u": bin_u, "bin_z": bin_z,
                     "bconv": bconv_r, "wconv": wconv_r, "wxp_r": wxp_r, "bxp": bxp_r,
                     "wdt_r": wdt_r, "bdt": bdt_r, "umask": um})
    return maps


def build_l2(debug=False):
    nc = bass.Bass("TRN2", target_bir_lowering=False)
    DH = 512
    L = 8192
    Ts = 512
    NCH = L // Ts

    u_in = nc.dram_tensor("u_h", [DH, L], F32, kind="ExternalInput").ap()
    dt_in = nc.dram_tensor("dt_h", [DH, L], F32, kind="ExternalInput").ap()
    bc_in = nc.dram_tensor("bc_h", [32, L], F32, kind="ExternalInput").ap()
    alog_in = nc.dram_tensor("alog_h", [128, 4 * 16], F32, kind="ExternalInput").ap()
    y_out = nc.dram_tensor("y_h", [DH, L], F32, kind="ExternalOutput").ap()
    dbg_st = nc.dram_tensor("dbg_st", [128, 64], F32, kind="ExternalOutput").ap()
    dbg_da = nc.dram_tensor("dbg_da", [128, 512], F32, kind="ExternalOutput").ap()
    dbg_db = nc.dram_tensor("dbg_db", [128, 512], F32, kind="ExternalOutput").ap()
    dbg_h = nc.dram_tensor("dbg_h", [128, 512], F32, kind="ExternalOutput").ap()
    dbg_cr = nc.dram_tensor("dbg_cr", [128, 512], F32, kind="ExternalOutput").ap()
    dbg_db_sb = nc.alloc_sbuf_tensor("dbg_db_sb", [128, 512], F32).ap()

    sb = lambda n, sh, dt=F32: nc.alloc_sbuf_tensor(n, sh, dt).ap()
    eye = sb("eye", [128, 128]); scr128 = sb("scr128", [128, 128])
    ones1 = sb("ones1", [1, 128])
    A_sb = sb("A_sb", [128, 64])
    alog_sb = sb("alog_sb", [128, 64])
    u_sb = sb("u_sb", [128, 4 * Ts])
    dt_sb = sb("dt_sb", [128, 4 * Ts])
    bm_sb = sb("bm_sb", [16, Ts])
    cm_sb = sb("cm_sb", [16, Ts])
    sel_sb = sb("sel_sb", [16, 16 * 128])
    ones16 = sb("ones16", [16, 128])
    du_sb = sb("du_sb", [128, 4 * Ts])
    duT = sb("duT", [128, 4 * Ts])
    bmT = sb("bmT", [128, 64])
    dA = [sb(f"dA{p}", [128, 4 * Ts]) for p in range(2)]
    dg = [sb(f"dg{p}", [128, 4 * 128]) for p in range(2)]
    crep = [sb(f"crep{p}", [128, Ts]) for p in range(2)]
    h_sb = [sb(f"h{p}", [128, 4 * Ts]) for p in range(2)]
    yt_sb = [sb(f"yt{p}", [128, 4 * Ts]) for p in range(2)]
    st_sb = sb("st_sb", [128, 64])
    y_sb = sb("y_sb", [128, 4 * Ts])

    p_y = [nc.alloc_psum_tensor(f"p_y{i}", [128, 512], F32).ap() for i in range(4)]
    p_db = [nc.alloc_psum_tensor(f"p_db{i}", [128, 512], F32).ap() for i in range(2)]
    p_cr = nc.alloc_psum_tensor("p_cr", [128, 512], F32).ap()
    p_tr = nc.alloc_psum_tensor("p_tr", [128, 512], F32).ap()

    ld = Ch(nc, "ld"); st = Ch(nc, "st")
    c_bmt = Ch(nc, "c_bmt"); c_bev = Ch(nc, "c_bev")
    c_du = Ch(nc, "c_du"); c_dut = Ch(nc, "c_dut"); c_due = Ch(nc, "c_due")
    c_da = Ch(nc, "c_da"); c_dg = Ch(nc, "c_dg")
    c_cm = Ch(nc, "c_cm"); c_ce = Ch(nc, "c_ce")
    c_db = Ch(nc, "c_db"); c_sc = Ch(nc, "c_sc")
    c_yt = Ch(nc, "c_yt"); c_ya = Ch(nc, "c_ya"); c_ye = Ch(nc, "c_ye")
    c_stc = Ch(nc, "c_stc")

    with nc.Block("prep0") as blk:
        @blk.sync
        def _(s):
            ld.sig(s.dma_start(alog_sb, alog_in), 16)
            ld.wait(s, ld.n)

        @blk.vector
        def _(v):
            c_du.sig(v.memset(scr128, 1.0))
            v.memset(ones1, 1.0)
            v.memset(st_sb, 0.0)
            c_du.sig(v.memset(ones16, 1.0))

        @blk.gpsimd
        def _(g):
            c_du.wait(g, c_du.n)
            c_du.sig(g.affine_select(eye, scr128, pattern=[[1, 128]], compare_op=ALU.is_equal,
                                     fill=0.0, base=0, channel_multiplier=-1))
            for nn in range(16):
                g.affine_select(sel_sb[:, nn * 128:(nn + 1) * 128], ones16,
                                pattern=[[0, 128]], compare_op=ALU.is_equal,
                                fill=0.0, base=-nn, channel_multiplier=1)

    with nc.Block("prep1") as blk:
        @blk.scalar
        def _(s):
            ld.wait(s, ld.n)
            c_da.sig(s.activation(A_sb, alog_sb, AF.Exp))

        @blk.vector
        def _(v):
            c_da.wait(v, c_da.n)
            v.tensor_scalar(A_sb, A_sb, -1.0, None, ALU.mult)

    for k in range(NCH):
        t0 = k * Ts
        B = dict(ld=ld.n, bmt=c_bmt.n, bev=c_bev.n, du=c_du.n, dut=c_dut.n,
                 due=c_due.n, da=c_da.n, dg=c_dg.n, cm=c_cm.n, ce=c_ce.n,
                 db=c_db.n, sc=c_sc.n, yt=c_yt.n, ya=c_ya.n, ye=c_ye.n, st=st.n,
                 stc=c_stc.n)
        with nc.Block(f"l2c{k}") as blk:
            @blk.sync
            def _(s, k=k, B=B, t0=t0):
                if debug and k == 1:
                    st.sig(s.dma_start(dbg_st, st_sb), 16)
                for pt in range(4):
                    ld.sig(s.dma_start(u_sb[:, pt * Ts:(pt + 1) * Ts],
                                       u_in[pt * 128:(pt + 1) * 128, t0:t0 + Ts]), 16)
                    ld.sig(s.dma_start(dt_sb[:, pt * Ts:(pt + 1) * Ts],
                                       dt_in[pt * 128:(pt + 1) * 128, t0:t0 + Ts]), 16)
                ld.sig(s.dma_start(bm_sb, bc_in[0:16, t0:t0 + Ts]), 16)
                ld.sig(s.dma_start(cm_sb, bc_in[16:32, t0:t0 + Ts]), 16)
                if debug and k == 1:
                    c_da.wait(s, B["da"] + 1)
                    st.sig(s.dma_start(dbg_da, dA[0][:, 0:Ts]), 16)
                    c_sc.wait(s, B["sc"] + 1)
                    st.sig(s.dma_start(dbg_h, h_sb[0][:, 0:Ts]), 16)
                    c_ce.wait(s, B["ce"] + 1)
                    st.sig(s.dma_start(dbg_cr, crep[0]), 16)
                c_ye.wait(s, B["ye"] + 4)
                for pt in range(4):
                    st.sig(s.dma_start(y_out[pt * 128:(pt + 1) * 128, t0:t0 + Ts],
                                       y_sb[:, pt * Ts:(pt + 1) * Ts]), 16)
                st.wait(s, st.n)

            @blk.tensor
            def _(t, B=B):
                ld.wait(t, B["ld"] + 160)
                for tb in range(4):
                    inst = t.transpose(p_tr[:, tb * 16:(tb + 1) * 16],
                                       bm_sb[0:16, tb * 128:(tb + 1) * 128], eye[0:16, 0:16])
                c_bmt.sig(inst)
                c_bev.wait(t, B["bev"] + 1)
                for tt in range(4):
                    if tt > 0:
                        c_due.wait(t, B["due"] + tt)
                    c_du.wait(t, B["du"] + 4)
                    for pt in range(4):
                        inst = t.transpose(p_tr[:, pt * 128:(pt + 1) * 128],
                                           du_sb[:, pt * Ts + tt * 128: pt * Ts + (tt + 1) * 128], eye)
                    c_dut.sig(inst)
                for nn in range(16):
                    par = nn % 2
                    c_ce.wait(t, B["ce"] + nn)
                    if nn >= 2:
                        c_yt.wait(t, B["yt"] + 4 * (nn - 1))
                    inst = t.matmul(p_cr, sel_sb[:, nn * 128:(nn + 1) * 128], cm_sb,
                                    start=True, stop=True)
                    c_cm.sig(inst)
                    c_dg.wait(t, B["dg"] + nn + 1)
                    c_due.wait(t, B["due"] + 4)
                    for pt in range(4):
                        if nn > 0 or pt >= 2:
                            c_sc.wait(t, B["sc"] + 4 * nn + pt - 1)
                        for tb in range(4):
                            inst = t.matmul(p_db[pt % 2][:, tb * 128:(tb + 1) * 128],
                                            duT[:, tb * Ts + pt * 128: tb * Ts + (pt + 1) * 128],
                                            dg[par][:, tb * 128:(tb + 1) * 128],
                                            start=True, stop=True)
                        c_db.sig(inst)
                    if nn > 0:
                        parp = (nn - 1) % 2
                        for pt in range(4):
                            c_yt.wait(t, B["yt"] + 4 * (nn - 1) + pt + 1)
                            c_ya.sig(t.matmul(p_y[pt], eye, yt_sb[parp][:, pt * Ts:(pt + 1) * Ts],
                                              start=(nn - 1 == 0), stop=False))
                for pt in range(4):
                    c_yt.wait(t, B["yt"] + 60 + pt + 1)
                    c_ya.sig(t.matmul(p_y[pt], eye, yt_sb[15 % 2][:, pt * Ts:(pt + 1) * Ts],
                                      start=False, stop=True))

            @blk.scalar
            def _(s, B=B):
                c_bmt.wait(s, B["bmt"] + 1)
                c_bev.sig(s.copy(bmT, p_tr[:, 0:64]))
                for tt in range(4):
                    c_dut.wait(s, B["dut"] + tt + 1)
                    c_due.sig(s.copy(duT[:, tt * Ts:(tt + 1) * Ts], p_tr))
                for nn in range(16):
                    par = nn % 2
                    if nn >= 2:
                        c_sc.wait(s, B["sc"] + 4 * (nn - 1))
                    for pt in range(4):
                        c_da.sig(s.activation(dA[par][:, pt * Ts:(pt + 1) * Ts],
                                              dt_sb[:, pt * Ts:(pt + 1) * Ts], AF.Exp,
                                              scale=A_sb[:, pt * 16 + nn: pt * 16 + nn + 1]))
                    if nn >= 2:
                        c_db.wait(s, B["db"] + 4 * (nn - 1))
                    for tb in range(4):
                        inst = s.activation(dg[par][:, tb * 128:(tb + 1) * 128], eye, AF.Copy,
                                            scale=bmT[:, tb * 16 + nn: tb * 16 + nn + 1])
                    c_dg.sig(inst)
                    c_cm.wait(s, B["cm"] + nn + 1)
                    if nn >= 2:
                        c_yt.wait(s, B["yt"] + 4 * (nn - 1))
                    c_ce.sig(s.copy(crep[par], p_cr))
                for pt in range(4):
                    c_ya.wait(s, B["ya"] + 64)
                    c_ye.sig(s.copy(y_sb[:, pt * Ts:(pt + 1) * Ts], p_y[pt]))

            @blk.vector
            def _(v, B=B):
                ld.wait(v, B["ld"] + 160)
                for pt in range(4):
                    c_du.sig(v.tensor_mul(du_sb[:, pt * Ts:(pt + 1) * Ts],
                                          dt_sb[:, pt * Ts:(pt + 1) * Ts],
                                          u_sb[:, pt * Ts:(pt + 1) * Ts]))
                for nn in range(16):
                    par = nn % 2
                    for pt in range(4):
                        c_da.wait(v, B["da"] + 4 * nn + pt + 1)
                        c_db.wait(v, B["db"] + 4 * nn + pt + 1)
                        if nn >= 2:
                            c_yt.wait(v, B["yt"] + 4 * (nn - 1))
                        c_stc.wait(v, B["stc"])
                        c_sc.sig(v.tensor_tensor_scan(h_sb[par][:, pt * Ts:(pt + 1) * Ts],
                                                      dA[par][:, pt * Ts:(pt + 1) * Ts],
                                                      p_db[pt % 2],
                                                      st_sb[:, pt * 16 + nn: pt * 16 + nn + 1],
                                                      ALU.mult, ALU.add))
                        c_sc.wait(v, c_sc.n)
                        c_stc.sig(v.tensor_copy(st_sb[:, pt * 16 + nn: pt * 16 + nn + 1],
                                                h_sb[par][:, (pt + 1) * Ts - 1:(pt + 1) * Ts]))

            @blk.gpsimd
            def _(g, B=B):
                for nn in range(16):
                    par = nn % 2
                    for pt in range(4):
                        c_sc.wait(g, B["sc"] + 4 * nn + pt + 1)
                        c_ce.wait(g, B["ce"] + nn + 1)
                        if nn >= 2:
                            c_ya.wait(g, B["ya"] + 4 * (nn - 1))
                        c_yt.sig(g.tensor_tensor(yt_sb[par][:, pt * Ts:(pt + 1) * Ts],
                                                 h_sb[par][:, pt * Ts:(pt + 1) * Ts],
                                                 crep[par], ALU.mult))

    with nc.Block("fin") as blk:
        @blk.sync
        def _(s):
            st.wait(s, st.n)
    return nc


def stage_l2(ins, l1_outs):
    A_log = ins["L0_attn_A_log"]
    maps = []
    for c in range(NCORE):
        b, dh = c // 2, c % 2
        u_full = np.concatenate([l1_outs[2 * b]["u_out"], l1_outs[2 * b + 1]["u_out"]], axis=1)
        dt_full = np.concatenate([l1_outs[2 * b]["dt_out"], l1_outs[2 * b + 1]["dt_out"]], axis=1)
        bc_full = np.concatenate([l1_outs[2 * b]["bc_out"], l1_outs[2 * b + 1]["bc_out"]], axis=1)
        r0 = dh * 512
        alog_r = np.ascontiguousarray(
            A_log[r0:r0 + 512].reshape(4, 128, 16).transpose(1, 0, 2).reshape(128, 64), dtype=np.float32)
        maps.append({"u_h": np.ascontiguousarray(u_full[r0:r0 + 512]),
                     "dt_h": np.ascontiguousarray(dt_full[r0:r0 + 512]),
                     "bc_h": np.ascontiguousarray(bc_full),
                     "alog_h": alog_r})
    return maps


def build_l3():
    nc = bass.Bass("TRN2", target_bir_lowering=False)
    Tc = 512
    NCH = NT // Tc

    y_in = nc.dram_tensor("y_t", [D, NT], F32, kind="ExternalInput").ap()
    sz_in = nc.dram_tensor("sz_t", [D, NT], F32, kind="ExternalInput").ap()
    x_in = nc.dram_tensor("x_t", [NT, D], F32, kind="ExternalInput").ap()
    wout_r = nc.dram_tensor("wout_r", [128, 8 * 1024], F32, kind="ExternalInput").ap()
    bout_r = nc.dram_tensor("bout_r", [1, 1024], F32, kind="ExternalInput").ap()
    x1_out = nc.dram_tensor("x1_t", [NT, D], F32, kind="ExternalOutput").ap()
    wout_bf_d = nc.dram_tensor("wout_bf_d", [128, 8 * 1024], BF16).ap()

    sb = lambda n, sh, dt=F32: nc.alloc_sbuf_tensor(n, sh, dt).ap()
    wout_sb = sb("wout_sb", [128, 8 * 1024], BF16)
    bout_sb = sb("bout_sb", [1, 1024])
    ones1 = sb("ones1", [1, 128])
    y_sb = sb("y_sb", [128, 8 * Tc])
    szt = sb("szt", [128, 8 * Tc])
    g_sb = sb("g_sb", [128, 8 * Tc], BF16)
    x_sb = sb("x_sb", [128, 4 * D])
    xs_sb = sb("xs_sb", [128, 4 * D])
    x1_sb = sb("x1_sb", [128, 4 * D])
    bns = sb("bns", [128, 12])
    mv = sb("mv", [128, 2])
    sd = sb("sd", [128, 1])
    rs = sb("rs", [128, 1])
    eps = sb("eps", [128, 1])

    pM = [nc.alloc_psum_tensor(f"pM{i}", [128, 512], F32).ap() for i in range(4)]

    ld = Ch(nc, "ld"); st = Ch(nc, "st"); cst = Ch(nc, "cst")
    c_g = Ch(nc, "c_g")
    c_mm = Ch(nc, "c_mm")
    c_ev = Ch(nc, "c_ev")
    c_v = Ch(nc, "c_v")
    c_a = Ch(nc, "c_a")
    c_x1 = Ch(nc, "c_x1")

    with nc.Block("prep0") as blk:
        @blk.gpsimd
        def _(g):
            cst.sig(g.dma_start(out=wout_bf_d, in_=wout_r), 16)

        @blk.sync
        def _(s):
            ld.sig(s.dma_start(bout_sb, bout_r), 16)
            ld.wait(s, ld.n)
            cst.wait(s, cst.n)
            ld.sig(s.dma_start(wout_sb, wout_bf_d), 16)
            ld.wait(s, ld.n)

        @blk.vector
        def _(v):
            v.memset(ones1, 1.0)
            v.memset(eps, 1e-5)

    for k in range(NCH):
        t0 = k * Tc
        B = dict(ld=ld.n, g=c_g.n, mm=c_mm.n, ev=c_ev.n, v=c_v.n, a=c_a.n,
                 x1=c_x1.n, st=st.n)
        with nc.Block(f"l3c{k}") as blk:
            @blk.sync
            def _(s, k=k, B=B, t0=t0):
                for dc in range(8):
                    ld.sig(s.dma_start(y_sb[:, dc * Tc:(dc + 1) * Tc],
                                       y_in[dc * 128:(dc + 1) * 128, t0:t0 + Tc]), 16)
                    ld.sig(s.dma_start(szt[:, dc * Tc:(dc + 1) * Tc],
                                       sz_in[dc * 128:(dc + 1) * 128, t0:t0 + Tc]), 16)
                for m in range(4):
                    ld.sig(s.dma_start(x_sb[:, m * D:(m + 1) * D],
                                       x_in[t0 + m * 128: t0 + (m + 1) * 128, :]), 16)
                for m in range(4):
                    c_x1.wait(s, B["x1"] + m + 1)
                    st.sig(s.dma_start(x1_out[t0 + m * 128: t0 + (m + 1) * 128, :],
                                       x1_sb[:, m * D:(m + 1) * D]), 16)
                st.wait(s, st.n)

            @blk.vector
            def _(v, B=B):
                ld.wait(v, B["ld"] + 20 * 16)
                for dc in range(8):
                    c_g.sig(v.tensor_mul(g_sb[:, dc * Tc:(dc + 1) * Tc],
                                         y_sb[:, dc * Tc:(dc + 1) * Tc],
                                         szt[:, dc * Tc:(dc + 1) * Tc]))
                for m in range(4):
                    for oh in range(2):
                        c_mm.wait(v, B["mm"] + 2 * m + oh + 1)
                        c_ev.sig(v.tensor_add(xs_sb[:, m * D + oh * 512: m * D + (oh + 1) * 512],
                                              pM[2 * (m % 2) + oh],
                                              x_sb[:, m * D + oh * 512: m * D + (oh + 1) * 512]))
                    c_ev.wait(v, c_ev.n)
                    c_v.sig(v.bn_stats(bns[:, 0:6], xs_sb[:, m * D: m * D + 512]))
                    c_v.sig(v.bn_stats(bns[:, 6:12], xs_sb[:, m * D + 512: m * D + 1024]))
                    c_v.wait(v, c_v.n)
                    c_v.sig(v.bn_aggr(mv, bns))
                    c_a.wait(v, B["a"] + m + 1)
                    c_v.sig(v.reciprocal(rs, sd))
                    c_v.wait(v, c_v.n)
                    c_x1.sig(v.tensor_scalar(x1_sb[:, m * D:(m + 1) * D],
                                             xs_sb[:, m * D:(m + 1) * D],
                                             mv[:, 0:1], rs, ALU.subtract, ALU.mult))

            @blk.tensor
            def _(t, B=B):
                for m in range(4):
                    for oh in range(2):
                        ps = pM[2 * (m % 2) + oh]
                        if m >= 2:
                            c_ev.wait(t, B["ev"] + 2 * (m - 1) + oh + 1)
                        for dc in range(8):
                            c_g.wait(t, B["g"] + dc + 1)
                            t.matmul(ps, g_sb[:, dc * Tc + m * 128: dc * Tc + m * 128 + 128],
                                     wout_sb[:, dc * 1024 + oh * 512: dc * 1024 + (oh + 1) * 512],
                                     start=(dc == 0), stop=False)
                        inst = t.matmul(ps, ones1, bout_sb[:, oh * 512:(oh + 1) * 512],
                                        start=False, stop=True)
                        c_mm.sig(inst)

            @blk.scalar
            def _(s, B=B):
                for m in range(4):
                    c_v.wait(s, B["v"] + 4 * m + 3)
                    c_a.sig(s.activation(sd, mv[:, 1:2], AF.Sqrt, bias=eps))

    with nc.Block("fin") as blk:
        @blk.sync
        def _(s):
            st.wait(s, st.n)
    return nc






class Em:
    KINDS = ("tensor", "scalar", "vector", "gpsimd", "sync")

    def __init__(self, nc):
        self.nc = nc
        self.qs = {}
        self.tasks = []
        self.wm = {k: {} for k in self.KINDS}

    def queue(self, name, kind, inc=1):
        q = dict(ch=Ch(self.nc, "q_" + name), kind=kind, inc=inc, created=0)
        if inc != 1:
            q["grp"] = Ch(self.nc, "g_" + name)
            q["open"] = 0
            q["gno"] = 0
        self.qs[name] = q
        return name

    def close_group(self, qn):
        q = self.qs[qn]
        if q.get("open", 0) == 0:
            return
        q["open"] = 0
        q["gno"] += 1
        total = q["created"] * q["inc"]

        def fn(eng, q=q, total=total):
            eng.wait_ge(q["ch"].sem, total)
            return eng.sem_inc(q["grp"].sem, 1)
        self.tasks.append((qn, fn, [], True))

    def t(self, qn, fn, deps=()):
        q = self.qs[qn]
        rdeps = []
        for d in deps:
            if d is None:
                continue
            dq, dk = d
            dqq = self.qs[dq]
            if dqq["inc"] != 1:
                self.close_group(dq)
                rdeps.append((dq + "~g", dqq["gno"]))
            else:
                rdeps.append((dq, dk))
        q["created"] += 1
        if q["inc"] != 1:
            q["open"] += 1
        tok = (qn, q["created"])
        self.tasks.append((qn, fn, rdeps, False))
        return tok

    def flush(self, blk):
        for qn, q in self.qs.items():
            if q.get("open", 0):
                self.close_group(qn)
        per = {k: [] for k in self.KINDS}
        for qn, fn, rdeps, isclose in self.tasks:
            per[self.qs[qn]["kind"]].append((qn, fn, rdeps, isclose))
        self.tasks = []

        def body(kind):
            items = per[kind]

            def run(eng):
                wm = self.wm[kind]
                for qn, fn, rdeps, isclose in items:
                    for dq, val in rdeps:
                        if wm.get(dq, 0) < val:
                            if dq.endswith("~g"):
                                eng.wait_ge(self.qs[dq[:-2]]["grp"].sem, val)
                            else:
                                eng.wait_ge(self.qs[dq]["ch"].sem, val)
                            wm[dq] = val
                    if isclose:
                        fn(eng)
                        continue
                    inst = fn(eng)
                    q = self.qs[qn]
                    inst.then_inc(q["ch"].sem, q["inc"])
                    q["ch"].n += q["inc"]
            return run

        for kind in self.KINDS:
            if per[kind]:
                getattr(blk, kind)(body(kind))

    def done(self, qn):
        return (qn, self.qs[qn]["created"])


class Res:

    def __init__(self, tok=None):
        self.w = tok
        self.rs = []

    def wdeps(self):
        d = list(self.rs)
        if self.w is not None:
            d.append(self.w)
        return d

    def rdep(self):
        return [self.w] if self.w is not None else []

    def write(self, tok):
        self.w = tok
        self.rs = []
        return tok

    def read(self, tok):
        self.rs.append(tok)
        return tok


H4 = 1024


def build_l4(final=False, ncap=None, trunc=99):
    nc = bass.Bass("TRN2", target_bir_lowering=False)
    Tc = 512
    NCH = (NT + H4) // Tc
    if ncap is not None:
        NCH = ncap
    HID = 2816
    NHh = 22

    x_in = nc.dram_tensor("xl_h", [NT + H4, D], F32, kind="ExternalInput").ap()
    wqkv_r = nc.dram_tensor("wqkv_r", [128, 8 * 1536], F32, kind="ExternalInput").ap()
    wo_r = nc.dram_tensor("wo_r", [128, 4 * 1024], F32, kind="ExternalInput").ap()
    w1a_r = nc.dram_tensor("w1a_r", [128, 8 * HID], F32, kind="ExternalInput").ap()
    w1g_r = nc.dram_tensor("w1g_r", [128, 8 * HID], F32, kind="ExternalInput").ap()
    w2_r = nc.dram_tensor("w2_r", [128, NHh * 1024], F32, kind="ExternalInput").ap()
    b1a_r = nc.dram_tensor("b1a_r", [128, NHh], F32, kind="ExternalInput").ap()
    b1g_r = nc.dram_tensor("b1g_r", [128, NHh], F32, kind="ExternalInput").ap()
    b2_r = nc.dram_tensor("b2_r", [1, 1024], F32, kind="ExternalInput").ap()
    lng_row = nc.dram_tensor("lng_row", [1, 1024], F32, kind="ExternalInput").ap()
    lnb_row = nc.dram_tensor("lnb_row", [1, 1024], F32, kind="ExternalInput").ap()
    qs_row = nc.dram_tensor("qs_row", [1, 64], F32, kind="ExternalInput").ap()
    ks_row = nc.dram_tensor("ks_row", [1, 64], F32, kind="ExternalInput").ap()
    pos_r = nc.dram_tensor("pos_r", [128, NCH * 4], F32, kind="ExternalInput").ap()
    mf_r = nc.dram_tensor("mf_r", [128, 4 * 640], BF16, kind="ExternalInput").ap()
    shm_r = nc.dram_tensor("shm_r", [128, 1], F32, kind="ExternalInput").ap()
    if final:
        mres_in = nc.dram_tensor("mres", [NT, D], F32, kind="ExternalInput").ap()
    xo = nc.dram_tensor("xo_t", [NT, D], F32, kind="ExternalOutput").ap()

    wqkv_bf = nc.dram_tensor("wqkv_bf", [128, 8 * 1536], BF16).ap()
    wo_bf = nc.dram_tensor("wo_bf", [128, 4 * 1024], BF16).ap()
    w1a_bf = nc.dram_tensor("w1a_bf", [128, 8 * HID], BF16).ap()
    w1g_bf = nc.dram_tensor("w1g_bf", [128, 8 * HID], BF16).ap()
    w2_bf = nc.dram_tensor("w2_bf", [128, NHh * 1024], BF16).ap()

    sb = lambda n, sh, dt=F32: nc.alloc_sbuf_tensor(n, sh, dt).ap()
    eye = sb("eye", [128, 128]); eyebf = sb("eyebf", [128, 128], BF16)
    ones1 = sb("ones1", [1, 128]); eps = sb("eps", [128, 1]); e12 = sb("e12", [128, 8])
    maskL = sb("maskL", [128, 128], BF16); maskR = sb("maskR", [128, 128], BF16)
    shA = sb("shA", [128, 128]); shB = sb("shB", [128, 128])
    wqkv_sb = sb("wqkv_sb", [128, 8 * 1536], BF16)
    wo_sb = sb("wo_sb", [128, 4 * 1024], BF16)
    w2s_sb = [sb(f"w2s_sb{i}", [128, 1024], BF16) for i in range(2)]
    w1a_sb = [sb(f"w1a_sb{i}", [128, 8 * 128], BF16) for i in range(2)]
    w1g_sb = [sb(f"w1g_sb{i}", [128, 8 * 128], BF16) for i in range(2)]
    b1a_sb = sb("b1a_sb", [128, NHh]); b1g_sb = sb("b1g_sb", [128, NHh])
    b2_sb = sb("b2_sb", [1, 1024])
    gam_rep = sb("gam_rep", [128, 1024]); bet_rep = sb("bet_rep", [128, 1024])
    lngrow_sb = sb("lngrow_sb", [1, 1024]); lnbrow_sb = sb("lnbrow_sb", [1, 1024])
    qsrow_sb = sb("qsrow_sb", [1, 64]); ksrow_sb = sb("ksrow_sb", [1, 64])
    qscrep = sb("qscrep", [128, 64]); kscrep = sb("kscrep", [128, 64])
    pos_sb = sb("pos_sb", [128, NCH * 4])
    mf_sb = sb("mf_sb", [128, 4 * 640], BF16)
    shm_sb = sb("shm_sb", [128, 1])
    invf = sb("invf", [128, 32]); lnbase = sb("lnbase", [128, 32])
    frq = sb("frq", [128, 4 * 32])
    rrf = sb("rrf", [128, 4 * 32]); ang = sb("ang", [128, 4 * 32]); angc = sb("angc", [128, 4 * 32])
    rri = nc.alloc_sbuf_tensor("rri", [128, 4 * 32], I32).ap()
    halfpi = sb("halfpi", [128, 1])
    snt = sb("snt", [128, 4 * 32]); cstt = sb("cstt", [128, 4 * 32])
    pwv = sb("pwv", [128, 4]); sclog = sb("sclog", [128, 4 * 32])
    scl = sb("scl", [128, 4 * 32]); iscl = sb("iscl", [128, 4 * 32])
    csx = sb("csx", [128, 4 * 32]); ssx = sb("ssx", [128, 4 * 32])
    cski = sb("cski", [128, 4 * 32]); sski = sb("sski", [128, 4 * 32])
    csq2 = sb("csq2", [128, 4 * 64]); nsq2 = sb("nsq2", [128, 4 * 32]); sq2 = sb("sq2", [128, 4 * 32])
    csk2 = sb("csk2", [128, 4 * 64]); nsk2 = sb("nsk2", [128, 4 * 32]); sk2 = sb("sk2", [128, 4 * 32])
    x_sb = sb("x_sb", [128, 4 * D])
    x1T = sb("x1T", [128, 8 * (Tc + 1)], BF16)
    q_sb = sb("q_sb", [128, 512]); k_sb = sb("k_sb", [128, 512])
    qss = sb("qss", [128, 8]); kss = sb("kss", [128, 8])
    rq = sb("rq", [128, 8]); rk = sb("rk", [128, 8])
    qr2 = sb("qr2", [128, 512]); kr2 = sb("kr2", [128, 512])
    tt1 = sb("tt1", [128, 512]); tt2 = sb("tt2", [128, 512])
    qT = sb("qT", [128, 512], BF16)
    NRING = 6
    kring = [sb(f"kring{i}", [128, 512], BF16) for i in range(NRING)]
    vring = [sb(f"vring{i}", [128, 512], BF16) for i in range(NRING)]
    s8 = sb("s8", [128, 8]); nm = sb("nm", [128, 1]); nmz = sb("nmz", [128, 1])
    P_sb = sb("P_sb", [128, 640])
    P_bf = sb("P_bf", [128, 640], BF16)
    PT_sb = sb("PT_sb", [128, 5 * 128], BF16)
    sume = sb("sume", [128, 1]); rsume = sb("rsume", [128, 1])
    attn = sb("attn", [128, 512])
    attnT = sb("attnT", [128, 512], BF16)
    x2_sb = sb("x2_sb", [128, 4 * D])
    x2tail = sb("x2tail", [128, D])
    bns = sb("bns", [128, 12]); mv = sb("mv", [128, 2]); sd = sb("sd", [128, 1]); rs = sb("rs", [128, 1])
    hh_sb = sb("hh_sb", [128, 4 * D], BF16)
    hT = sb("hT", [128, 8 * 512], BF16)
    a_sb = sb("a_sb", [128, 512]); gg_sb = sb("gg_sb", [128, 512])
    gate = sb("gate", [128, NHh * 512], BF16)
    x3_sb = sb("x3_sb", [128, D])
    mres_sb = x_sb[:, 0:D]

    p1 = nc.alloc_psum_tensor("p1", [128, 512], F32).ap()
    p2 = nc.alloc_psum_tensor("p2", [128, 512], F32).ap()
    p3 = nc.alloc_psum_tensor("p3", [128, 512], F32).ap()
    pS = nc.alloc_psum_tensor("pS", [128, 1024], F32).ap()
    pT = nc.alloc_psum_tensor("pT", [128, 512], F32).ap()
    pVa = nc.alloc_psum_tensor("pVa", [128, 512], F32).ap()
    pVb = nc.alloc_psum_tensor("pVb", [128, 512], F32).ap()

    em = Em(nc)
    for qn, kind, inc in [("pe", "tensor", 1), ("act", "scalar", 1), ("dve", "vector", 1),
                          ("gp", "gpsimd", 1), ("ld", "sync", 16), ("ldw", "sync", 16),
                          ("st", "sync", 16), ("gc", "gpsimd", 16)]:
        em.queue(qn, kind, inc)

    R = lambda: Res()
    r_eye = R(); r_masks = R(); r_sh = R(); r_invf = R(); r_lnb = R()
    r_gam = R(); r_bet = R(); r_qsc = R(); r_ksc = R()
    r_x = [R() for _ in range(4)]
    r_x1T = [R() for _ in range(8)]
    r_q = R(); r_k = R(); r_qss = R(); r_kss = R(); r_rq = R(); r_rk = R()
    r_qr2 = R(); r_kr2 = R(); r_tt1 = R(); r_tt2 = R()
    r_qT = R(); r_kring = [R() for _ in range(NRING)]; r_vring = [R() for _ in range(NRING)]
    r_s8 = R(); r_nm = R(); r_nmz = R(); r_P = R(); r_PT = R(); r_sume = R(); r_rsume = R()
    r_attn = R(); r_attnT = R()
    r_x2 = [R() for _ in range(4)]; r_x2tail = R()
    r_bns = R(); r_mv = R(); r_sd = R(); r_rs = R()
    r_hh = [R() for _ in range(4)]; r_hT = R()
    r_a = R(); r_gg = R(); r_gate = [R() for _ in range(NHh)]
    r_x3 = R(); r_xf = R(); r_mres = R()
    r_w1a = [R(), R()]; r_w1g = [R(), R()]
    r_p1 = R(); r_p2 = R(); r_p3 = R(); r_pS = R(); r_pT = R(); r_pV = [R(), R()]
    r_tabs = R()
    r_lns = R()

    tcast = []
    for dst, src_ in [(wqkv_bf, wqkv_r), (wo_bf, wo_r), (w1a_bf, w1a_r),
                      (w1g_bf, w1g_r), (w2_bf, w2_r)]:
        tcast.append(em.t("gc", lambda g, d=dst, s_=src_: g.dma_start(out=d, in_=s_)))
    tld = []
    for dst, src_ in [(b1a_sb, b1a_r), (b1g_sb, b1g_r), (b2_sb, b2_r),
                      (pos_sb, pos_r), (mf_sb, mf_r), (shm_sb, shm_r),
                      (lngrow_sb, lng_row), (lnbrow_sb, lnb_row),
                      (qsrow_sb, qs_row), (ksrow_sb, ks_row)]:
        tld.append(em.t("ld", lambda s, d=dst, s_=src_: s.dma_start(d, s_)))
    t_wq = em.t("ld", lambda s: s.dma_start(wqkv_sb, wqkv_bf), [tcast[0]])
    t_wo = em.t("ld", lambda s: s.dma_start(wo_sb, wo_bf), [tcast[1]])
    r_wq = Res(t_wq); r_wo = Res(t_wo)
    r_w2s = [Res(), Res()]

    t0_ = em.t("dve", lambda v: v.memset(ones1, 1.0))
    em.t("dve", lambda v: v.memset(eps, 1e-5))
    em.t("dve", lambda v: v.memset(e12, 1e-12))
    t_hp = em.t("dve", lambda v: v.memset(halfpi, float(np.pi / 2)))
    r_x2tail.write(em.t("dve", lambda v: v.memset(x2tail, 0.0)))
    for i in range(8):
        r_x1T[i].write(em.t("dve", lambda v, i=i: v.memset(x1T[:, i * (Tc + 1):i * (Tc + 1) + 1], 0.0)))
    t_scr = em.t("dve", lambda v: v.memset(attn[:, 0:128], 1.0))
    t_mL0 = em.t("dve", lambda v: v.memset(maskL, 0.0))
    t_mR0 = em.t("dve", lambda v: v.memset(maskR, 0.0))
    r_eye.write(em.t("gp", lambda g: g.affine_select(eye, attn[:, 0:128], pattern=[[1, 128]],
                                                     compare_op=ALU.is_equal, fill=0.0,
                                                     base=0, channel_multiplier=-1), [t_scr]))
    r_sh.write(em.t("gp", lambda g: g.affine_select(shA, attn[:, 0:128], pattern=[[1, 128]],
                                                    compare_op=ALU.is_equal, fill=0.0,
                                                    base=-1, channel_multiplier=-1), [t_scr]))
    t_shb0 = em.t("gp", lambda g: g.memset(shB, 0.0))
    r_sh.write(em.t("gp", lambda g: g.affine_select(shB[:, 0:1], attn[:, 0:1],
                                                    pattern=[[1, 1]], compare_op=ALU.is_equal,
                                                    fill=0.0, base=-127, channel_multiplier=1),
               [t_scr, t_shb0]))
    r_masks.write(em.t("gp", lambda g: g.affine_select(maskL, maskL, pattern=[[1, 128]],
                                                       compare_op=ALU.is_ge, fill=NEG,
                                                       base=0, channel_multiplier=-1), [t_mL0]))
    r_masks.write(em.t("gp", lambda g: g.affine_select(maskR, maskR, pattern=[[-1, 128]],
                                                       compare_op=ALU.is_ge, fill=NEG,
                                                       base=0, channel_multiplier=1), [t_mR0]))
    r_attn.write(t_scr)
    t_i1 = em.t("gp", lambda g: g.iota(invf, pattern=[[1, 32]], base=0, channel_multiplier=0,
                                       allow_small_or_imprecise_dtypes=True))
    t_i2 = em.t("gp", lambda g: g.iota(lnbase, pattern=[[1, 32]], base=0, channel_multiplier=0,
                                       allow_small_or_imprecise_dtypes=True))
    r_invf.write(em.t("act", lambda s: s.activation(invf, invf, AF.Exp,
                                                    scale=float(-np.log(10000.0) / 32.0)), [t_i1]))
    r_eye.write(em.t("act", lambda s: s.copy(eyebf, eye), [r_eye.w]))
    t_lb0 = em.t("dve", lambda v: v.tensor_scalar(lnbase, lnbase, 2.0, 25.6, ALU.mult, ALU.add), [t_i2])
    t_lb = em.t("act", lambda s: s.activation(lnbase, lnbase, AF.Ln), [t_lb0])
    r_lnb.write(em.t("dve", lambda v: v.tensor_scalar(lnbase, lnbase, float(np.log(89.6)),
                                                      None, ALU.subtract), [t_lb]))
    for (dst, src_, res) in [(gam_rep, lngrow_sb, r_gam), (bet_rep, lnbrow_sb, r_bet)]:
        for oh in range(2):
            tm = em.t("pe", lambda t, d=dst, s_=src_, oh=oh: t.matmul(
                pT, ones1, s_[:, oh * 512:(oh + 1) * 512], start=True, stop=True),
                [t0_, tld[6], tld[7]] + r_pT.wdeps())
            r_pT.write(tm)
            te = em.t("act", lambda s, d=dst, oh=oh: s.copy(d[:, oh * 512:(oh + 1) * 512], pT), [tm])
            r_pT.read(te)
            res.write(te)
    for (dst, src_, res) in [(qscrep, qsrow_sb, r_qsc), (kscrep, ksrow_sb, r_ksc)]:
        tm = em.t("pe", lambda t, d=dst, s_=src_: t.matmul(pT[:, 0:64], ones1, s_,
                                                           start=True, stop=True),
                  [t0_, tld[8], tld[9]] + r_pT.wdeps())
        r_pT.write(tm)
        te = em.t("act", lambda s, d=dst: s.copy(d, pT[:, 0:64]), [tm])
        r_pT.read(te)
        res.write(te)

    TWO_PI = float(2 * np.pi)

    for ci in range(NCH):
        own = ci >= 2
        attn_tiles = [3] if ci == 1 else ([0, 1, 2, 3] if own else [])
        t0 = ci * Tc
        oc0 = (ci - 2) * Tc

        tlx = []
        for m in range(4):
            tk = em.t("ld", lambda s, m=m, t0=t0: s.dma_start(
                x_sb[:, m * D:(m + 1) * D], x_in[t0 + m * 128: t0 + (m + 1) * 128, :]),
                r_x[m].wdeps())
            r_x[m].write(tk)
            tlx.append(tk)

        for dj in range(8):
            deps = r_pT.wdeps() + [r_x[m].w for m in range(4)] + r_eye.rdep()
            tm = None
            def mk_trx(dj):
                def f(t):
                    for m in range(4):
                        inst = t.transpose(pT[:, m * 128:(m + 1) * 128],
                                           x_sb[:, m * D + dj * 128: m * D + (dj + 1) * 128], eye)
                    return inst
                return f
            tm = em.t("pe", mk_trx(dj), deps)
            r_pT.write(tm)
            for m in range(4):
                r_x[m].read(tm)
            te = em.t("act", lambda s, dj=dj: s.copy(
                x1T[:, dj * (Tc + 1) + 1: dj * (Tc + 1) + 1 + Tc], pT),
                [tm] + r_x1T[dj].wdeps())
            r_pT.read(te)
            r_x1T[dj].write(te)

        last = (ci == NCH - 1)
        if last and trunc < 1:
            continue
        if last and trunc == 11:
            continue
        pslice = pos_sb[:, ci * 4:(ci + 1) * 4]
        bc4 = lambda t_, n=32: t_.unsqueeze(1).broadcast_to([128, 4, n])
        v4 = lambda t_, n=32: t_.rearrange("p (a f) -> p a f", a=4)
        if last and trunc == 12:
            continue
        tF = em.t("dve", lambda v, ps_=pslice: v.tensor_tensor(v4(frq), bc4(invf),
                                                   ps_.unsqueeze(2).broadcast_to([128, 4, 32]),
                                                   ALU.mult),
                  r_invf.rdep() + [tld[3]] + r_tabs.wdeps())
        C1H, C1M, C1L = 6.283203125, -1.7642974853515625e-05, -1.7484555314695172e-07
        t1 = em.t("dve", lambda v: v.tensor_scalar(rrf, frq, float(1.0 / TWO_PI), 0.5,
                                                   ALU.mult, ALU.add), [tF])
        t2 = em.t("dve", lambda v: v.tensor_copy(rri, rrf), [t1])
        t3 = em.t("dve", lambda v: v.tensor_copy(rrf, rri), [t2])
        t4a = em.t("dve", lambda v: v.scalar_tensor_tensor(ang, rrf, -C1H, frq, ALU.mult, ALU.add), [t3])
        t4b = em.t("dve", lambda v: v.scalar_tensor_tensor(angc, rrf, -C1M, ang, ALU.mult, ALU.add), [t4a])
        t4c = em.t("dve", lambda v: v.scalar_tensor_tensor(ang, rrf, -C1L, angc, ALU.mult, ALU.add), [t4b])
        t4d = em.t("dve", lambda v: v.tensor_scalar(rrf, ang, float(np.pi), None, ALU.is_gt), [t4c])
        t4e = em.t("dve", lambda v: v.scalar_tensor_tensor(ang, rrf, -TWO_PI, ang, ALU.mult, ALU.add), [t4d])
        t4f = em.t("dve", lambda v: v.tensor_scalar(rrf, ang, float(-np.pi), None, ALU.is_lt), [t4e])
        t4 = em.t("dve", lambda v: v.scalar_tensor_tensor(ang, rrf, TWO_PI, ang, ALU.mult, ALU.add), [t4f])
        tS = em.t("act", lambda s: s.activation(snt, ang, AF.Sin), [t4])
        tCa = em.t("act", lambda s: s.activation(angc, ang, AF.Abs), [t4])
        tC = em.t("act", lambda s: s.activation(cstt, angc, AF.Sin, scale=-1.0, bias=halfpi), [tCa, t_hp])
        tPW = em.t("dve", lambda v, ps_=pslice: v.tensor_scalar(pwv, ps_, 4096.0, float(1.0 / 256.0),
                                                    ALU.subtract, ALU.mult), [tld[3]] + r_tabs.wdeps())
        tSL = em.t("dve", lambda v: v.tensor_tensor(v4(sclog), bc4(lnbase),
                                                    pwv.unsqueeze(2).broadcast_to([128, 4, 32]),
                                                    ALU.mult), [tPW] + r_lnb.rdep())
        tE1 = em.t("act", lambda s: s.activation(scl, sclog, AF.Exp), [tSL])
        tE2 = em.t("act", lambda s: s.activation(iscl, sclog, AF.Exp, scale=-1.0), [tSL])
        tcs = em.t("dve", lambda v: v.tensor_mul(csx, cstt, scl), [tC, tE1])
        tss = em.t("dve", lambda v: v.tensor_mul(ssx, snt, scl), [tS, tE1])
        tck = em.t("dve", lambda v: v.tensor_mul(cski, cstt, iscl), [tC, tE2])
        tsk = em.t("dve", lambda v: v.tensor_mul(sski, snt, iscl), [tS, tE2])
        qv = r_qsc.rdep(); kv = r_ksc.rdep()
        tq1 = em.t("dve", lambda v: v.tensor_tensor(csq2.rearrange("p (a f) -> p a f", a=4),
                                                    bc4(qscrep, 64).rearrange("p a f -> p a f") if False else csq2.rearrange("p (a f) -> p a f", a=4),
                                                    csq2.rearrange("p (a f) -> p a f", a=4), ALU.mult), []) if False else None
        tq1 = em.t("dve", lambda v: v.tensor_tensor(
            csq2.rearrange("p (a b f) -> p a b f", a=4, b=2),
            csx.rearrange("p (a f) -> p a f", a=4).unsqueeze(2).broadcast_to([128, 4, 2, 32]),
            qscrep.rearrange("p (b f) -> p b f", b=2).unsqueeze(1).broadcast_to([128, 4, 2, 32]),
            ALU.mult), [tcs] + qv + r_tabs.wdeps())
        tq2 = em.t("dve", lambda v: v.scalar_tensor_tensor(
            nsq2.rearrange("p (a f) -> p a f", a=4),
            ssx.rearrange("p (a f) -> p a f", a=4), -1.0,
            qscrep[:, 32:64].unsqueeze(1).broadcast_to([128, 4, 32]),
            ALU.mult, ALU.mult), [tss] + qv)
        tq3 = em.t("dve", lambda v: v.tensor_tensor(
            sq2.rearrange("p (a f) -> p a f", a=4),
            ssx.rearrange("p (a f) -> p a f", a=4),
            qscrep[:, 0:32].unsqueeze(1).broadcast_to([128, 4, 32]),
            ALU.mult), [tss] + qv)
        tk1 = em.t("dve", lambda v: v.tensor_tensor(
            csk2.rearrange("p (a b f) -> p a b f", a=4, b=2),
            cski.rearrange("p (a f) -> p a f", a=4).unsqueeze(2).broadcast_to([128, 4, 2, 32]),
            kscrep.rearrange("p (b f) -> p b f", b=2).unsqueeze(1).broadcast_to([128, 4, 2, 32]),
            ALU.mult), [tck] + kv)
        tk2 = em.t("dve", lambda v: v.scalar_tensor_tensor(
            nsk2.rearrange("p (a f) -> p a f", a=4),
            sski.rearrange("p (a f) -> p a f", a=4), -1.0,
            kscrep[:, 32:64].unsqueeze(1).broadcast_to([128, 4, 32]),
            ALU.mult, ALU.mult), [tsk] + kv)
        tk3 = em.t("dve", lambda v: v.tensor_tensor(
            sk2.rearrange("p (a f) -> p a f", a=4),
            sski.rearrange("p (a f) -> p a f", a=4),
            kscrep[:, 0:32].unsqueeze(1).broadcast_to([128, 4, 32]),
            ALU.mult), [tsk] + kv)
        t_tabs = [tq1, tq2, tq3, tk1, tk2, tk3]
        r_tabs.write(t_tabs[-1])
        for tt in t_tabs[:-1]:
            r_tabs.read(tt)

        if last and trunc < 2:
            continue
        for m in range(4):
            g_abs = ci * 4 + m
            slot = g_abs % NRING
            pqs = [r_p1, r_p2, r_p3]
            pps = [p1, p2, p3]
            tqkv = []
            for v3 in range(3):
                deps = ([r_x1T[dj].w for dj in range(8)] + pqs[v3].wdeps()
                        + r_wq.rdep())
                def mk_qkv(m=m, v3=v3):
                    def f(t):
                        for dc in range(8):
                            col = dc * (Tc + 1) + (1 if dc < 4 else 0) + m * 128
                            inst = t.matmul(pps[v3], x1T[:, col: col + 128],
                                            wqkv_sb[:, dc * 1536 + v3 * 512: dc * 1536 + (v3 + 1) * 512],
                                            start=(dc == 0), stop=(dc == 7))
                        return inst
                    return f
                tk_ = em.t("pe", mk_qkv(), deps)
                pqs[v3].write(tk_)
                for dj in range(8):
                    r_x1T[dj].read(tk_)
                r_wq.read(tk_)
                tqkv.append(tk_)
            te_q = em.t("act", lambda s: s.copy(q_sb, p1), [tqkv[0]] + r_q.wdeps())
            r_p1.read(te_q); r_q.write(te_q)
            te_k = em.t("act", lambda s: s.copy(k_sb, p2), [tqkv[1]] + r_k.wdeps())
            r_p2.read(te_k); r_k.write(te_k)
            te_v = em.t("act", lambda s, slot=slot: s.copy(vring[slot], p3),
                        [tqkv[2]] + r_vring[slot].wdeps())
            r_p3.read(te_v); r_vring[slot].write(te_v)

            tsq = em.t("act", lambda s: s.activation(tt1, q_sb, AF.Square),
                       [te_q] + r_tt1.wdeps())
            r_tt1.write(tsq); r_q.read(tsq)
            trd = em.t("dve", lambda v: v.reduce_sum(qss, tt1.rearrange("p (h c) -> p h c", h=8),
                                                     axis=AX.X), [tsq] + r_qss.wdeps())
            r_tt1.read(trd); r_qss.write(trd)
            tsq2 = em.t("act", lambda s: s.activation(tt2, k_sb, AF.Square),
                        [te_k] + r_tt2.wdeps())
            r_tt2.write(tsq2); r_k.read(tsq2)
            trd2 = em.t("dve", lambda v: v.reduce_sum(kss, tt2.rearrange("p (h c) -> p h c", h=8),
                                                      axis=AX.X), [tsq2] + r_kss.wdeps())
            r_tt2.read(trd2); r_kss.write(trd2)
            tsr = em.t("act", lambda s: s.activation(qss, qss, AF.Sqrt), [trd])
            r_qss.write(tsr)
            tmx = em.t("dve", lambda v: v.tensor_tensor(qss, qss, e12, ALU.max), [tsr])
            r_qss.write(tmx)
            trq = em.t("dve", lambda v: v.reciprocal(rq, qss), [tmx] + r_rq.wdeps())
            r_qss.read(trq); r_rq.write(trq)
            tsr2 = em.t("act", lambda s: s.activation(kss, kss, AF.Sqrt), [trd2])
            r_kss.write(tsr2)
            tmx2 = em.t("dve", lambda v: v.tensor_tensor(kss, kss, e12, ALU.max), [tsr2])
            r_kss.write(tmx2)
            trk = em.t("dve", lambda v: v.reciprocal(rk, kss), [tmx2] + r_rk.wdeps())
            r_kss.read(trk); r_rk.write(trk)

            vq = lambda t_: t_.rearrange("p (h b c) -> p h b c", h=8, b=2)
            tb = r_tabs.w
            ta1 = em.t("dve", lambda v, m=m: v.tensor_tensor(
                vq(tt1), vq(q_sb),
                csq2[:, m * 64:(m + 1) * 64].rearrange("p (b c) -> p b c", b=2)
                .unsqueeze(1).broadcast_to([128, 8, 2, 32]), ALU.mult),
                [te_q, tb] + r_tt1.wdeps())
            r_tt1.write(ta1); r_q.read(ta1)
            ta2 = em.t("dve", lambda v, m=m: v.tensor_tensor(
                tt2.rearrange("p (h b c) -> p h b c", h=8, b=2)[:, :, 0, :],
                q_sb.rearrange("p (h b c) -> p h b c", h=8, b=2)[:, :, 1, :],
                nsq2[:, m * 32:(m + 1) * 32].unsqueeze(1).broadcast_to([128, 8, 32]),
                ALU.mult), [te_q, tb] + r_tt2.wdeps())
            ta3 = em.t("dve", lambda v, m=m: v.tensor_tensor(
                tt2.rearrange("p (h b c) -> p h b c", h=8, b=2)[:, :, 1, :],
                q_sb.rearrange("p (h b c) -> p h b c", h=8, b=2)[:, :, 0, :],
                sq2[:, m * 32:(m + 1) * 32].unsqueeze(1).broadcast_to([128, 8, 32]),
                ALU.mult), [te_q, tb, ta2])
            r_tt2.write(ta3); r_q.read(ta3)
            ta4 = em.t("dve", lambda v: v.tensor_add(qr2, tt1, tt2), [ta1, ta3] + r_qr2.wdeps())
            r_tt1.read(ta4); r_tt2.read(ta4); r_qr2.write(ta4)
            tb1 = em.t("dve", lambda v, m=m: v.tensor_tensor(
                vq(tt1), vq(k_sb),
                csk2[:, m * 64:(m + 1) * 64].rearrange("p (b c) -> p b c", b=2)
                .unsqueeze(1).broadcast_to([128, 8, 2, 32]), ALU.mult),
                [te_k, tb, ta4] + r_tt1.wdeps())
            r_tt1.write(tb1); r_k.read(tb1)
            tb2 = em.t("dve", lambda v, m=m: v.tensor_tensor(
                tt2.rearrange("p (h b c) -> p h b c", h=8, b=2)[:, :, 0, :],
                k_sb.rearrange("p (h b c) -> p h b c", h=8, b=2)[:, :, 1, :],
                nsk2[:, m * 32:(m + 1) * 32].unsqueeze(1).broadcast_to([128, 8, 32]),
                ALU.mult), [te_k, tb, ta4] + r_tt2.wdeps())
            tb3 = em.t("dve", lambda v, m=m: v.tensor_tensor(
                tt2.rearrange("p (h b c) -> p h b c", h=8, b=2)[:, :, 1, :],
                k_sb.rearrange("p (h b c) -> p h b c", h=8, b=2)[:, :, 0, :],
                sk2[:, m * 32:(m + 1) * 32].unsqueeze(1).broadcast_to([128, 8, 32]),
                ALU.mult), [te_k, tb, tb2])
            r_tt2.write(tb3); r_k.read(tb3)
            tb4 = em.t("dve", lambda v: v.tensor_add(kr2, tt1, tt2), [tb1, tb3] + r_kr2.wdeps())
            r_tt1.read(tb4); r_tt2.read(tb4); r_kr2.write(tb4)
            tb5 = em.t("dve", lambda v: v.tensor_tensor(
                kr2.rearrange("p (h c) -> p h c", h=8),
                kr2.rearrange("p (h c) -> p h c", h=8),
                rk.unsqueeze(2).broadcast_to([128, 8, 64]), ALU.mult),
                [tb4, trk])
            r_rk.read(tb5); r_kr2.write(tb5)

            def mk_tr(src, m=m):
                def f(t):
                    for p4 in range(4):
                        inst = t.transpose(pT[:, p4 * 128:(p4 + 1) * 128],
                                           src[:, p4 * 128:(p4 + 1) * 128], eye)
                    return inst
                return f
            tqt = em.t("pe", mk_tr(qr2), [ta4] + r_pT.wdeps() + r_eye.rdep())
            r_pT.write(tqt); r_qr2.read(tqt)
            tqe = em.t("act", lambda s: s.copy(qT, pT), [tqt] + r_qT.wdeps())
            r_pT.read(tqe); r_qT.write(tqe)
            tkt = em.t("pe", mk_tr(kr2), [tb5] + r_pT.wdeps())
            r_pT.write(tkt); r_kr2.read(tkt)
            tke = em.t("act", lambda s, slot=slot: s.copy(kring[slot], pT),
                       [tkt] + r_kring[slot].wdeps())
            r_pT.read(tke); r_kring[slot].write(tke)

            if m in attn_tiles and not (last and trunc < 3):
                for h in range(8):
                    jj, pp = h % 2, h // 2
                    deps = (r_pS.wdeps() + [tqe] +
                            [r_kring[(g_abs - 4 + kc) % NRING].w for kc in range(5)] +
                            r_masks.rdep() + r_eye.rdep())
                    def mk_qk(g_abs=g_abs, jj=jj, pp=pp):
                        def f(t):
                            for kc in range(5):
                                slot2 = (g_abs - 4 + kc) % NRING
                                inst = t.matmul(pS[:, kc * 128:(kc + 1) * 128],
                                                qT[jj * 64:(jj + 1) * 64, pp * 128:(pp + 1) * 128],
                                                kring[slot2][jj * 64:(jj + 1) * 64, pp * 128:(pp + 1) * 128],
                                                start=True, stop=(kc not in (0, 4)))
                                if kc == 0:
                                    inst = t.matmul(pS[:, 0:128], eyebf, maskL, start=False, stop=True)
                                elif kc == 4:
                                    inst = t.matmul(pS[:, 512:640], eyebf, maskR, start=False, stop=True)
                            return inst
                        return f
                    tsc = em.t("pe", mk_qk(), deps)
                    r_pS.write(tsc)
                    r_qT.read(tsc)
                    for kc in range(5):
                        r_kring[(g_abs - 4 + kc) % NRING].read(tsc)
                    if ci == 2:
                        tmf = em.t("dve", lambda v, m=m: v.tensor_tensor(
                            pS.flatten()[0:0] if False else pS[:, 0:640],
                            pS[:, 0:640], mf_sb[:, m * 640:(m + 1) * 640], ALU.add),
                            [tsc, tld[4]])
                        r_pS.write(tmf)
                        tsc = tmf
                    tmx_ = em.t("dve", lambda v: v.reduce_max(nm, pS[:, 0:640], axis=AX.X,
                                                              negate=True),
                                [tsc] + r_nm.wdeps())
                    r_pS.read(tmx_); r_nm.write(tmx_)
                    tnm = em.t("dve", lambda v, h=h: v.tensor_tensor(nmz, nm, s8[:, h:h + 1],
                                                                     ALU.mult),
                               [tmx_, ts8] + r_nmz.wdeps())
                    r_nm.read(tnm); r_s8.read(tnm); r_nmz.write(tnm)
                    tex = em.t("act", lambda s, h=h: s.activation(P_sb, pS[:, 0:640], AF.Exp,
                                                             bias=nmz, scale=s8[:, h:h + 1],
                                                             accum_out=sume),
                               [tnm] + r_P.wdeps() + r_sume.wdeps())
                    r_pS.read(tex); r_nmz.read(tex); r_s8.read(tex)
                    r_P.write(tex); r_sume.write(tex)
                    trs = em.t("dve", lambda v: v.reciprocal(rsume, sume),
                               [tex] + r_rsume.wdeps())
                    r_sume.read(trs); r_rsume.write(trs)
                    def mk_ptr1():
                        def f(t):
                            for kc in range(4):
                                inst = t.transpose(pT[:, kc * 128:(kc + 1) * 128],
                                                   P_sb[:, kc * 128:(kc + 1) * 128], eye)
                            return inst
                        return f
                    tp1 = em.t("pe", mk_ptr1(), [tex] + r_pT.wdeps())
                    r_pT.write(tp1); r_P.read(tp1)
                    tpe1 = em.t("act", lambda s: s.copy(PT_sb[:, 0:512], pT),
                                [tp1] + r_PT.wdeps())
                    r_pT.read(tpe1); r_PT.write(tpe1)
                    tp2 = em.t("pe", lambda t: t.transpose(pT[:, 0:128], P_sb[:, 512:640], eye),
                               [tpe1] + r_pT.wdeps())
                    r_pT.write(tp2); r_P.read(tp2)
                    tpe2 = em.t("act", lambda s: s.copy(PT_sb[:, 512:640], pT[:, 0:128]), [tp2])
                    r_pT.read(tpe2); r_PT.write(tpe2)
                    pv = pVa if h % 2 == 0 else pVb
                    rpv = r_pV[h % 2]
                    def mk_pv(g_abs=g_abs, h=h, pv=pv):
                        def f(t):
                            for kc in range(5):
                                slot2 = (g_abs - 4 + kc) % NRING
                                inst = t.matmul(pv[:, 0:64], PT_sb[:, kc * 128:(kc + 1) * 128],
                                                vring[slot2][:, h * 64:(h + 1) * 64],
                                                start=(kc == 0), stop=(kc == 4))
                            return inst
                        return f
                    tpv = em.t("pe", mk_pv(), [tpe1, tpe2] + rpv.wdeps() +
                               [r_vring[(g_abs - 4 + kc) % NRING].w for kc in range(5)])
                    rpv.write(tpv); r_PT.read(tpv)
                    for kc in range(5):
                        r_vring[(g_abs - 4 + kc) % NRING].read(tpv)
                    tae = em.t("dve", lambda v, h=h, pv=pv: v.tensor_scalar(
                        attn[:, h * 64:(h + 1) * 64], pv[:, 0:64], rsume, None, ALU.mult),
                        [tpv, trs] + (r_attn.wdeps() if h == 0 else []))
                    rpv.read(tae); r_rsume.read(tae)
                    if h == 0:
                        r_attn.write(tae)
                    else:
                        r_attn.write(tae)
                tat = em.t("pe", mk_tr(attn), [r_attn.w] + r_pT.wdeps())
                r_pT.write(tat); r_attn.read(tat)
                tate = em.t("act", lambda s: s.copy(attnT, pT), [tat] + r_attnT.wdeps())
                r_pT.read(tate); r_attnT.write(tate)
                for oh in range(2):
                    pw_ = p1 if oh == 0 else p2
                    rw = r_p1 if oh == 0 else r_p2
                    def mk_wo(oh=oh, pw_=pw_):
                        def f(t):
                            for p4 in range(4):
                                inst = t.matmul(pw_, attnT[:, p4 * 128:(p4 + 1) * 128],
                                                wo_sb[:, p4 * 1024 + oh * 512: p4 * 1024 + (oh + 1) * 512],
                                                start=(p4 == 0), stop=(p4 == 3))
                            return inst
                        return f
                    two = em.t("pe", mk_wo(), [tate] + rw.wdeps() + r_wo.rdep())
                    rw.write(two); r_attnT.read(two); r_wo.read(two)
                    if ci == 1:
                        dst = x2tail
                        rd = r_x2tail
                    else:
                        dst = x2_sb
                        rd = r_x2[m]
                    off = (0 if ci == 1 else m * D) + oh * 512
                    tx2 = em.t("dve", lambda v, dst=dst, off=off, pw_=pw_, m=m, oh=oh: v.tensor_add(
                        dst[:, off:off + 512], pw_[:, 0:512],
                        x_sb[:, m * D + oh * 512: m * D + (oh + 1) * 512]),
                        [two, r_x[m].w] + rd.wdeps())
                    rw.read(tx2); r_x[m].read(tx2); rd.write(tx2)
                if ci == 1:
                    tshm = em.t("dve", lambda v: v.tensor_scalar(x2tail, x2tail, shm_sb[:, 0:1],
                                                                 None, ALU.mult),
                                [r_x2tail.w, tld[5]])
                    r_x2tail.write(tshm)

        if last and trunc == 13:
            continue
        for dj in range(8):
            tc_ = em.t("dve", lambda v, dj=dj: v.tensor_copy(
                x1T[:, dj * (Tc + 1): dj * (Tc + 1) + 1],
                x1T[:, dj * (Tc + 1) + Tc: dj * (Tc + 1) + Tc + 1]),
                r_x1T[dj].wdeps() + [r_x1T[dj].w])
            r_x1T[dj].write(tc_)

        if not own or (last and trunc < 4):
            continue

        for m in range(4):
            def mk_sh(m=m):
                def f(t):
                    inst = t.matmul(pS[:, 0:512], eye,
                                    x2_sb[:, m * D: m * D + 512],
                                    start=True, stop=True)
                    prev = (x2_sb[:, (m - 1) * D + 512:(m - 1) * D + 1024]
                            if m > 0 else x2tail[:, 512:1024])
                    inst = t.matmul(pS[:, 512:1024], shA,
                                    x2_sb[:, m * D + 512: m * D + 1024],
                                    start=True, stop=False)
                    inst = t.matmul(pS[:, 512:1024], shB, prev,
                                    start=False, stop=True)
                    return inst
                return f
            prev_res = r_x2[m - 1] if m > 0 else r_x2tail
            tsh = em.t("pe", mk_sh(), r_pS.wdeps() + [r_x2[m].w, prev_res.w] + r_sh.rdep())
            r_pS.write(tsh); r_x2[m].read(tsh); prev_res.read(tsh)
            tb1_ = em.t("dve", lambda v: v.bn_stats(bns[:, 0:6], pS[:, 0:512]),
                        [tsh] + r_lns.wdeps())
            tb2_ = em.t("dve", lambda v: v.bn_stats(bns[:, 6:12], pS[:, 512:1024]), [tsh, tb1_])
            tag = em.t("dve", lambda v: v.bn_aggr(mv, bns), [tb2_])
            tsd = em.t("act", lambda s: s.activation(sd, mv[:, 1:2], AF.Sqrt, bias=eps), [tag])
            trs_ = em.t("dve", lambda v: v.reciprocal(rs, sd), [tsd])
            tap = em.t("dve", lambda v, m=m: v.tensor_scalar(hh_sb[:, m * D:(m + 1) * D],
                                                             pS[:, 0:1024], mv[:, 0:1], rs,
                                                             ALU.subtract, ALU.mult),
                       [trs_] + r_hh[m].wdeps())
            r_pS.read(tap)
            tgm = em.t("dve", lambda v, m=m: v.tensor_mul(hh_sb[:, m * D:(m + 1) * D],
                                                          hh_sb[:, m * D:(m + 1) * D], gam_rep),
                       [tap] + r_gam.rdep())
            tbt = em.t("dve", lambda v, m=m: v.tensor_add(hh_sb[:, m * D:(m + 1) * D],
                                                          hh_sb[:, m * D:(m + 1) * D], bet_rep),
                       [tgm] + r_bet.rdep())
            r_lns.write(tbt)
            r_hh[m].write(tbt)
        ttail = em.t("dve", lambda v: v.tensor_copy(x2tail, x2_sb[:, 3 * D:4 * D]),
                     [r_x2[3].w] + r_x2tail.wdeps())
        r_x2[3].read(ttail)
        r_x2tail.write(ttail)
        pT_bf = pT.bitcast(BF16)
        for dj in range(8):
            def mk_ht(dj=dj):
                def f(t):
                    for m in range(4):
                        inst = t.transpose(pT_bf[:, m * 128:(m + 1) * 128],
                                           hh_sb[:, m * D + dj * 128: m * D + (dj + 1) * 128], eyebf)
                    return inst
                return f
            tht = em.t("pe", mk_ht(), [r_hh[m].w for m in range(4)] + r_pT.wdeps())
            r_pT.write(tht)
            for m in range(4):
                r_hh[m].read(tht)
            the = em.t("act", lambda s, dj=dj: s.copy(hT[:, dj * 512:(dj + 1) * 512],
                                                      pT_bf[:, 0:512]),
                       [tht] + ([] if dj > 0 else r_hT.wdeps()))
            r_pT.read(the)
            r_hT.write(the)
        for hh in range(NHh):
            pb = hh % 2
            tw1l = []
            for dc in range(8):
                tw1l.append(em.t("ldw", lambda s, dc=dc, hh=hh, pb=pb: s.dma_start(
                    w1a_sb[pb][:, dc * 128:(dc + 1) * 128],
                    w1a_bf[:, dc * HID + hh * 128: dc * HID + (hh + 1) * 128]),
                    r_w1a[pb].wdeps() + [tcast[2]]))
                tw1l.append(em.t("ldw", lambda s, dc=dc, hh=hh, pb=pb: s.dma_start(
                    w1g_sb[pb][:, dc * 128:(dc + 1) * 128],
                    w1g_bf[:, dc * HID + hh * 128: dc * HID + (hh + 1) * 128]),
                    r_w1g[pb].wdeps() + [tcast[3]]))
            r_w1a[pb].write(tw1l[-2]); r_w1g[pb].write(tw1l[-1])
            def mk_w1(wsb, pp_, pb=pb):
                def f(t):
                    for dc in range(8):
                        inst = t.matmul(pp_, wsb[pb][:, dc * 128:(dc + 1) * 128],
                                        hT[:, dc * 512:(dc + 1) * 512],
                                        start=(dc == 0), stop=(dc == 7))
                    return inst
                return f
            ta_ = em.t("pe", mk_w1(w1a_sb, p1), [tw1a, r_hT.w] + r_p1.wdeps())
            r_p1.write(ta_); r_w1a[pb].read(ta_); r_hT.read(ta_)
            tg_ = em.t("pe", mk_w1(w1g_sb, p2), [tw1g, r_hT.w] + r_p2.wdeps())
            r_p2.write(tg_); r_w1g[pb].read(tg_)
            tae_ = em.t("dve", lambda v, hh=hh: v.tensor_scalar(a_sb, p1, b1a_sb[:, hh:hh + 1],
                                                                None, ALU.add),
                        [ta_, tld[0]] + r_a.wdeps())
            r_p1.read(tae_); r_a.write(tae_)
            tge_ = em.t("act", lambda s, hh=hh: s.activation(gg_sb, p2, AF.Gelu,
                                                             bias=b1g_sb[:, hh:hh + 1]),
                        [tg_, tld[1]] + r_gg.wdeps())
            r_p2.read(tge_); r_gg.write(tge_)
            tgt_ = em.t("dve", lambda v, hh=hh: v.tensor_mul(gate[:, hh * 512:(hh + 1) * 512],
                                                             a_sb, gg_sb),
                        [tae_, tge_] + r_gate[hh].wdeps())
            r_a.read(tgt_); r_gg.read(tgt_); r_gate[hh].write(tgt_)
        def do_store(m):
            if final:
                tmr = em.t("ld", lambda s, m=m, oc0=oc0: s.dma_start(
                    mres_sb, mres_in[oc0 + m * 128: oc0 + (m + 1) * 128, :]),
                    r_x[0].wdeps())
                r_x[0].write(tmr)
                tfa = em.t("dve", lambda v: v.tensor_add(x3_sb, x3_sb, mres_sb),
                           [r_x3.w, tmr])
                r_x[0].read(tfa); r_x3.write(tfa)
                tb1f = em.t("dve", lambda v: v.bn_stats(bns[:, 0:6], x3_sb[:, 0:512]),
                            [tfa] + r_lns.wdeps())
                tb2f = em.t("dve", lambda v: v.bn_stats(bns[:, 6:12], x3_sb[:, 512:1024]),
                            [tfa, tb1f])
                tagf = em.t("dve", lambda v: v.bn_aggr(mv, bns), [tb2f])
                tsdf = em.t("act", lambda s: s.activation(sd, mv[:, 1:2], AF.Sqrt, bias=eps), [tagf])
                trsf = em.t("dve", lambda v: v.reciprocal(rs, sd), [tsdf])
                tapf = em.t("dve", lambda v: v.tensor_scalar(x3_sb, x3_sb, mv[:, 0:1], rs,
                                                             ALU.subtract, ALU.mult),
                            [trsf, r_x3.w])
                r_lns.write(tapf)
                r_x3.write(tapf)
                out_t = x3_sb
                out_r = r_x3
            else:
                out_t = x3_sb
                out_r = r_x3
            tst = em.t("st", lambda s, m=m, oc0=oc0, out_t=out_t: s.dma_start(
                xo[oc0 + m * 128: oc0 + (m + 1) * 128, :], out_t), [out_r.w])
            out_r.read(tst)

        for mp in range(2):
            pws = [p1, p2, p3, pVa]
            rws = [r_p1, r_p2, r_p3, r_pV[0]]
            tw2g = [None] * 4
            for hh in range(NHh):
                pb2 = hh % 2
                tl2 = em.t("ldw", lambda s, hh=hh, pb2=pb2: s.dma_start(
                    w2s_sb[pb2], w2_bf[:, hh * 1024:(hh + 1) * 1024]),
                    r_w2s[pb2].wdeps() + [tcast[4]])
                r_w2s[pb2].write(tl2)
                for mi2 in range(2):
                    m = 2 * mp + mi2
                    for oh in range(2):
                        sl = mi2 * 2 + oh
                        def mk_w2p(m=m, oh=oh, sl=sl, hh=hh, pb2=pb2):
                            def f(t):
                                return t.matmul(pws[sl][:, 0:512],
                                                gate[:, hh * 512 + m * 128: hh * 512 + (m + 1) * 128],
                                                w2s_sb[pb2][:, oh * 512:(oh + 1) * 512],
                                                start=(hh == 0), stop=False)
                            return f
                        deps = [tl2, r_gate[hh].w]
                        if hh == 0:
                            deps += rws[sl].wdeps()
                        tk2 = em.t("pe", mk_w2p(), deps)
                        if hh == 0:
                            rws[sl].write(tk2)
                        tw2g[sl] = tk2
                        r_gate[hh].read(tk2)
                        r_w2s[pb2].read(tk2)
            for sl in range(4):
                oh = sl % 2
                def mk_b2(sl=sl, oh=oh):
                    def f(t):
                        return t.matmul(pws[sl][:, 0:512], ones1,
                                        b2_sb[:, oh * 512:(oh + 1) * 512],
                                        start=False, stop=True)
                    return f
                tkb = em.t("pe", mk_b2(), [tw2g[sl], tld[2], t0_])
                rws[sl].write(tkb)
                tw2g[sl] = tkb
            for mi2 in range(2):
                m = 2 * mp + mi2
                for oh in range(2):
                    sl = mi2 * 2 + oh
                    tx3 = em.t("dve", lambda v, m=m, oh=oh, sl=sl: v.tensor_add(
                        x3_sb[:, oh * 512:(oh + 1) * 512], pws[sl][:, 0:512],
                        x2_sb[:, m * D + oh * 512: m * D + (oh + 1) * 512]),
                        [tw2g[sl], r_x2[m].w] + (r_x3.wdeps() if oh == 0 else [r_x3.w]))
                    rws[sl].read(tx3); r_x2[m].read(tx3); r_x3.write(tx3)
                do_store(m)

    with nc.Block("main") as blk:
        em.flush(blk)
    with nc.Block("fin") as blk:
        @blk.sync
        def _(s):
            s.wait_ge(em.qs["st"]["ch"].sem, em.qs["st"]["ch"].n)
    return nc


def stage_l4(ins, x_full, li, final=False, mres_full=None):
    wqkv = ins["L0_ffn_wqkv"][li]; wo_ = ins["L0_ffn_wo"][li]
    w1 = ins["L0_ffn_w1"][li]; b1 = ins["L0_ffn_b1"][li]
    w2 = ins["L0_ffn_w2"][li]; b2 = ins["L0_ffn_b2"][li]
    lng = ins["L0_ffn_ln_g"][li]; lnb = ins["L0_ffn_ln_b"][li]
    qs = ins["L0_ffn_q_scale"][li]; ks = ins["L0_ffn_k_scale"][li]
    HID = 2816; NHh = 22
    w1a = np.zeros((1024, HID), np.float32); w1a[:, :2730] = w1[:, :2730]
    w1g = np.zeros((1024, HID), np.float32); w1g[:, :2730] = w1[:, 2730:]
    b1a = np.zeros(HID, np.float32); b1a[:2730] = b1[:2730]
    b1g = np.zeros(HID, np.float32); b1g[:2730] = b1[2730:]
    w2p = np.zeros((HID, 1024), np.float32); w2p[:2730] = w2
    rsh = lambda w, r: np.ascontiguousarray(
        w.reshape(w.shape[0] // 128, 128, w.shape[1]).transpose(1, 0, 2)
        .reshape(128, (w.shape[0] // 128) * w.shape[1]), dtype=np.float32)
    wqkv_r = rsh(wqkv, 8)
    wo_r = rsh(wo_, 4)
    w1a_r = rsh(w1a, 8); w1g_r = rsh(w1g, 8)
    w2_r = rsh(w2p, NHh)
    b1a_r = np.ascontiguousarray(b1a.reshape(NHh, 128).T)
    b1g_r = np.ascontiguousarray(b1g.reshape(NHh, 128).T)
    b2_r = b2.reshape(1, 1024).astype(np.float32)
    lng_row = lng.reshape(1, 1024).astype(np.float32)
    lnb_row = lnb.reshape(1, 1024).astype(np.float32)
    qs_row = qs.reshape(1, 64).astype(np.float32)
    ks_row = ks.reshape(1, 64).astype(np.float32)
    NCH = (NT + H4) // 512
    maps = []
    for c in range(NCORE):
        b, half = c // 2, c % 2
        s0 = half * NT
        xs = np.zeros((NT + H4, D), np.float32)
        lo = s0 - H4
        src_lo = max(lo, 0)
        xs[src_lo - lo:, :] = x_full[b, src_lo: s0 + NT, :]
        pos = np.zeros((128, NCH * 4), np.float32)
        for ti in range(NCH * 4):
            pos[:, ti] = s0 - H4 + ti * 128 + np.arange(128)
        import ml_dtypes
        mf = np.zeros((128, 4 * 640), ml_dtypes.bfloat16)
        if half == 0:
            for m in range(4):
                cut = 512 - m * 128
                mf[:, m * 640: m * 640 + cut] = NEG
        shm = (np.ones((128, 1), np.float32) if half == 1
               else np.zeros((128, 1), np.float32))
        mp = {"xl_h": xs, "wqkv_r": wqkv_r, "wo_r": wo_r, "w1a_r": w1a_r,
              "w1g_r": w1g_r, "w2_r": w2_r, "b1a_r": b1a_r, "b1g_r": b1g_r,
              "b2_r": b2_r, "lng_row": lng_row, "lnb_row": lnb_row,
              "qs_row": qs_row, "ks_row": ks_row, "pos_r": pos, "mf_r": mf,
              "shm_r": shm}
        if final:
            mp["mres"] = np.ascontiguousarray(mres_full[b, s0:s0 + NT, :])
        maps.append(mp)
    return maps


def _stage_l3(ins, l1_outs, l2_outs):
    W_out = ins["L0_attn_W_out"]; b_out = ins["L0_attn_b_out"]
    x = ins["L0_attn_x"]
    wout_r = np.ascontiguousarray(
        W_out.reshape(8, 128, 1024).transpose(1, 0, 2).reshape(128, 8 * 1024), dtype=np.float32)
    bout_r = np.ascontiguousarray(b_out.reshape(1, 1024), dtype=np.float32)
    maps = []
    for c in range(NCORE):
        b, half = c // 2, c % 2
        s0 = half * NT
        y_full = np.concatenate([l2_outs[2 * b]["y_h"], l2_outs[2 * b + 1]["y_h"]], axis=0)
        maps.append({"y_t": np.ascontiguousarray(y_full[:, s0:s0 + NT]),
                     "sz_t": l1_outs[c]["sz_out"],
                     "x_t": np.ascontiguousarray(x[b, s0:s0 + NT, :]),
                     "wout_r": wout_r, "bout_r": bout_r})
    return maps


def kernel(**inputs):
    ins = inputs
    nc = build_l1()
    res1 = run_bass_kernel_spmd(nc, stage_l1(ins), core_ids=list(range(NCORE)))
    l1_outs = [res1.results[c] for c in range(NCORE)]
    nc = build_l2()
    res2 = run_bass_kernel_spmd(nc, stage_l2(ins, l1_outs), core_ids=list(range(NCORE)))
    l2_outs = [res2.results[c] for c in range(NCORE)]
    nc = build_l3()
    res3 = run_bass_kernel_spmd(nc, _stage_l3(ins, l1_outs, l2_outs),
                                core_ids=list(range(NCORE)))
    x1_full = np.empty((4, 2 * NT, D), np.float32)
    for c in range(NCORE):
        b, half = c // 2, c % 2
        x1_full[b, half * NT:(half + 1) * NT] = res3.results[c]["x1_t"]
    nc = build_l4()
    res4 = run_bass_kernel_spmd(nc, stage_l4(ins, x1_full, 0), core_ids=list(range(NCORE)))
    x2_full = np.empty((4, 2 * NT, D), np.float32)
    for c in range(NCORE):
        b, half = c // 2, c % 2
        x2_full[b, half * NT:(half + 1) * NT] = res4.results[c]["xo_t"]
    nc = build_l4(final=True)
    res5 = run_bass_kernel_spmd(nc, stage_l4(ins, x2_full, 1, final=True, mres_full=x1_full),
                                core_ids=list(range(NCORE)))
    out = np.empty((4, 2 * NT, D), np.float32)
    for c in range(NCORE):
        b, half = c // 2, c % 2
        out[b, half * NT:(half + 1) * NT] = res5.results[c]["xo_t"]
    return out



def emit_ffn(nc, em, V, S, li, W, x_at, out_dst, xrow_prev, xrow_new, tail_toks,
             coll2_tok, final):
    Tc = 512
    NCH = 10
    HID = 2816
    NHh = 22
    NRING = 6
    eye = V["eye"]; eyebf = V["eyebf"]; ones1 = V["ones1"]; eps = V["eps"]
    e12 = V["e12"]; halfpi = V["halfpi"]; maskL = V["maskL"]; maskR = V["maskR"]
    shA = V["shA"]; shB = V["shB"]
    wqkv_sb = V["wqkv_sb"]; wo_sb = V["wo_sb"]; w2s_sb = V["w2s_sb"]
    w1a_sb = V["w1a_sb"]; w1g_sb = V["w1g_sb"]
    b1a_sb = V["b1a_sb"]; b1g_sb = V["b1g_sb"]; b2_sb = V["b2_sb"]
    gam_rep = V["gam_rep"]; bet_rep = V["bet_rep"]
    lngrow_sb = V["lngrow_sb"]; lnbrow_sb = V["lnbrow_sb"]
    qsrow_sb = V["qsrow_sb"]; ksrow_sb = V["ksrow_sb"]
    qscrep = V["qscrep"]; kscrep = V["kscrep"]
    pos_sb = V["pos_sb"]; mf_sb = V["mf_sb"]; shm_sb = V["shm_sb"]
    invf = V["invf"]; lnbase = V["lnbase"]
    frq = V["frq"]; rrf = V["rrf"]; ang = V["ang"]; angc = V["angc"]; rri = V["rri"]
    snt = V["snt"]; cstt = V["cstt"]; pwv = V["pwv"]; sclog = V["sclog"]
    scl = V["scl"]; iscl = V["iscl"]; csx = V["csx"]; ssx = V["ssx"]
    cski = V["cski"]; sski = V["sski"]
    csq2 = V["csq2"]; nsq2 = V["nsq2"]; sq2 = V["sq2"]
    csk2 = V["csk2"]; nsk2 = V["nsk2"]; sk2 = V["sk2"]
    x_sb = V["x_sb"]; x1T = V["x1T"]
    q_sb = V["q_sb"]; k_sb = V["k_sb"]; qss = V["qss"]; kss = V["kss"]
    rq = V["rq"]; rk = V["rk"]; qr2 = V["qr2"]; kr2 = V["kr2"]
    tt1 = V["tt1"]; tt2 = V["tt2"]; qT = V["qT"]
    kring = V["kring"]; vring = V["vring"]
    s8 = V["s8"]; nm = V["nm"]; nmz = V["nmz"]
    P_sb = V["P_sb"]; P_bf = V["P_bf"]; PT_sb = V["PT_sb"]; sume = V["sume"]; rsume = V["rsume"]
    attn = V["attn"]; attnT = V["attnT"]
    x2_sb = V["x2_sb"]; x2tail = V["x2tail"]
    bns = V["bns"]; mv = V["mv"]; sd = V["sd"]; rs = V["rs"]
    hh_sb = V["hh_sb"]; hT = V["hT"]; a_sb = V["a_sb"]; gg_sb = V["gg_sb"]
    gate = V["gate"]; x3_sb = V["x3_sb"]; mres_sb = V["mres_sb"]
    mt0 = V["mt0"]; mt1 = V["mt1"]
    p1 = V["p1"]; p2 = V["p2"]; p3 = V["p3"]; pS = V["pS"]; pT = V["pT"]
    pVa = V["pVa"]; pVb = V["pVb"]
    collsem = V["collsem"]
    cstw = V["cstw"]

    r_eye = S["r_eye"]; r_masks = S["r_masks"]; r_sh = S["r_sh"]
    r_invf = S["r_invf"]; r_lnb = S["r_lnb"]
    r_gam = S["r_gam"]; r_bet = S["r_bet"]; r_qsc = S["r_qsc"]; r_ksc = S["r_ksc"]
    r_x = S["r_x"]; r_x1T = S["r_x1T"]
    r_q = S["r_q"]; r_k = S["r_k"]; r_qss = S["r_qss"]; r_kss = S["r_kss"]
    r_rq = S["r_rq"]; r_rk = S["r_rk"]
    r_qr2 = S["r_qr2"]; r_kr2 = S["r_kr2"]; r_tt1 = S["r_tt1"]; r_tt2 = S["r_tt2"]
    r_qT = S["r_qT"]; r_kring = S["r_kring"]; r_vring = S["r_vring"]
    r_s8 = S["r_s8"]; r_nm = S["r_nm"]; r_nmz = S["r_nmz"]
    r_P = S["r_P"]; r_Pb = S["r_Pb"]; r_PT = S["r_PT"]; r_sume = S["r_sume"]; r_rsume = S["r_rsume"]
    r_attn = S["r_attn"]; r_attnT = S["r_attnT"]
    r_x2 = S["r_x2"]; r_x2tail = S["r_x2tail"]
    r_hh = S["r_hh"]; r_hT = S["r_hT"]
    r_a = S["r_a"]; r_gg = S["r_gg"]; r_gate = S["r_gate"]
    r_x3 = S["r_x3"]
    r_w1a = S["r_w1a"]; r_w1g = S["r_w1g"]
    r_p1 = S["r_p1"]; r_p2 = S["r_p2"]; r_p3 = S["r_p3"]
    r_pS = S["r_pS"]; r_pT = S["r_pT"]; r_pV = S["r_pV"]
    r_tabs = S["r_tabs"]; r_lns = S["r_lns"]
    r_wq = S["r_wq"]; r_wo = S["r_wo"]; r_w2s = S["r_w2s"]
    r_b1a = S["r_b1a"]; r_b1g = S["r_b1g"]; r_b2 = S["r_b2"]
    r_lng = S["r_lng"]; r_lnbr = S["r_lnbr"]; r_qsr = S["r_qsr"]; r_ksr = S["r_ksr"]
    r_mf = S["r_mf"]

    def cwait(mk):
        def f(s):
            s.wait_ge(*cstw)
            return mk(s)
        return f
    t_wq = em.t("ld", cwait(lambda s: s.dma_start(wqkv_sb, W["wqkv_bf"])), r_wq.wdeps())
    r_wq.write(t_wq)
    t_wo = em.t("ld", lambda s: s.dma_start(wo_sb, W["wo_bf"]), r_wo.wdeps())
    r_wo.write(t_wo)
    t_b1a = em.t("ld", lambda s: s.dma_start(b1a_sb, W["b1a_r"]), r_b1a.wdeps())
    r_b1a.write(t_b1a)
    t_b1g = em.t("ld", lambda s: s.dma_start(b1g_sb, W["b1g_r"]), r_b1g.wdeps())
    r_b1g.write(t_b1g)
    t_b2 = em.t("ld", lambda s: s.dma_start(b2_sb, W["b2_r"]), r_b2.wdeps())
    r_b2.write(t_b2)
    t_lng = em.t("ld", lambda s: s.dma_start(lngrow_sb, W["lng_row"]), r_lng.wdeps())
    r_lng.write(t_lng)
    t_lnb = em.t("ld", lambda s: s.dma_start(lnbrow_sb, W["lnb_row"]), r_lnbr.wdeps())
    r_lnbr.write(t_lnb)
    t_qs = em.t("ld", lambda s: s.dma_start(qsrow_sb, W["qs_row"]), r_qsr.wdeps())
    r_qsr.write(t_qs)
    t_ks = em.t("ld", lambda s: s.dma_start(ksrow_sb, W["ks_row"]), r_ksr.wdeps())
    r_ksr.write(t_ks)
    if li == 0:
        t_mf = em.t("ld", lambda s: s.dma_start(mf_sb, W["mf_r"]), r_mf.wdeps())
        r_mf.write(t_mf)
    else:
        t_mf = r_mf.w
    tld = [t_b1a, t_b1g, t_b2, None, t_mf, None, t_lng, t_lnb, t_qs, t_ks]

    t_x2z = em.t("dve", lambda v: v.memset(x2tail, 0.0), r_x2tail.wdeps())
    r_x2tail.write(t_x2z)
    for i in range(8):
        tz = em.t("dve", lambda v, i=i: v.memset(
            x1T[:, i * (Tc + 1):i * (Tc + 1) + 1], 0.0), r_x1T[i].wdeps())
        r_x1T[i].write(tz)

    for (dst, src_, res, tl) in [(gam_rep, lngrow_sb, r_gam, t_lng),
                                 (bet_rep, lnbrow_sb, r_bet, t_lnb)]:
        for oh in range(2):
            tm = em.t("pe", lambda t, d=dst, s_=src_, oh=oh: t.matmul(
                pT, ones1, s_[:, oh * 512:(oh + 1) * 512], start=True, stop=True),
                [tl] + r_pT.wdeps() + res.wdeps())
            r_pT.write(tm)
            if res is r_lng or True:
                pass
            (r_lng if res is r_gam else r_lnbr).read(tm)
            te = em.t("act", lambda s, d=dst, oh=oh: s.copy(d[:, oh * 512:(oh + 1) * 512], pT), [tm])
            r_pT.read(te)
            res.write(te)
    for (dst, src_, res, tl, rsr) in [(qscrep, qsrow_sb, r_qsc, t_qs, r_qsr),
                                      (kscrep, ksrow_sb, r_ksc, t_ks, r_ksr)]:
        tm = em.t("pe", lambda t, d=dst, s_=src_: t.matmul(pT[:, 0:64], ones1, s_,
                                                           start=True, stop=True),
                  [tl] + r_pT.wdeps() + res.wdeps())
        r_pT.write(tm)
        rsr.read(tm)
        te = em.t("act", lambda s, d=dst: s.copy(d, pT[:, 0:64]), [tm])
        r_pT.read(te)
        res.write(te)

    TWO_PI = float(2 * np.pi)

    for ci in range(NCH):
        own = ci >= 2
        attn_tiles = [3] if ci == 1 else ([0, 1, 2, 3] if own else [])
        t0 = ci * Tc
        oc0 = (ci - 2) * Tc

        tlx = []
        for m in range(4):
            srcap = x_at(t0 + m * 128)
            deps = r_x[m].wdeps()
            if li == 1:
                if ci < 2:
                    deps = deps + [coll2_tok]
                else:
                    deps = deps + xrow_prev[ci - 2]
            if li == 0 and ci == 0:
                def mkx(dst, src):
                    def f(s):
                        s.wait_ge(collsem, 3)
                        return s.dma_start(dst, src)
                    return f
                tk = em.t("ld", mkx(x_sb[:, m * D:(m + 1) * D], srcap), deps)
            else:
                tk = em.t("ld", lambda s, m=m, srcap=srcap: s.dma_start(
                    x_sb[:, m * D:(m + 1) * D], srcap), deps)
            r_x[m].write(tk)
            tlx.append(tk)

        for dj in range(8):
            deps = r_pT.wdeps() + [r_x[m].w for m in range(4)] + r_eye.rdep()
            def mk_trx(dj):
                def f(t):
                    for m in range(4):
                        inst = t.transpose(pT[:, m * 128:(m + 1) * 128],
                                           x_sb[:, m * D + dj * 128: m * D + (dj + 1) * 128], eye)
                    return inst
                return f
            tm = em.t("pe", mk_trx(dj), deps)
            r_pT.write(tm)
            for m in range(4):
                r_x[m].read(tm)
            te = em.t("act", lambda s, dj=dj: s.copy(
                x1T[:, dj * (Tc + 1) + 1: dj * (Tc + 1) + 1 + Tc], pT),
                [tm] + r_x1T[dj].wdeps())
            r_pT.read(te)
            r_x1T[dj].write(te)

        pslice = pos_sb[:, ci * 4:(ci + 1) * 4]
        bc4 = lambda t_, n=32: t_.unsqueeze(1).broadcast_to([128, 4, n])
        v4 = lambda t_, n=32: t_.rearrange("p (a f) -> p a f", a=4)
        tF = em.t("dve", lambda v, ps_=pslice: v.tensor_tensor(v4(frq), bc4(invf),
                                                   ps_.unsqueeze(2).broadcast_to([128, 4, 32]),
                                                   ALU.mult),
                  r_invf.rdep() + r_tabs.wdeps())
        C1H, C1M, C1L = 6.283203125, -1.7642974853515625e-05, -1.7484555314695172e-07
        t1 = em.t("dve", lambda v: v.tensor_scalar(rrf, frq, float(1.0 / TWO_PI), 0.5,
                                                   ALU.mult, ALU.add), [tF])
        t2 = em.t("dve", lambda v: v.tensor_copy(rri, rrf), [t1])
        t3 = em.t("dve", lambda v: v.tensor_copy(rrf, rri), [t2])
        t4a = em.t("dve", lambda v: v.scalar_tensor_tensor(ang, rrf, -C1H, frq, ALU.mult, ALU.add), [t3])
        t4b = em.t("dve", lambda v: v.scalar_tensor_tensor(angc, rrf, -C1M, ang, ALU.mult, ALU.add), [t4a])
        t4c = em.t("dve", lambda v: v.scalar_tensor_tensor(ang, rrf, -C1L, angc, ALU.mult, ALU.add), [t4b])
        t4d = em.t("dve", lambda v: v.tensor_scalar(rrf, ang, float(np.pi), None, ALU.is_gt), [t4c])
        t4e = em.t("dve", lambda v: v.scalar_tensor_tensor(ang, rrf, -TWO_PI, ang, ALU.mult, ALU.add), [t4d])
        t4f = em.t("dve", lambda v: v.tensor_scalar(rrf, ang, float(-np.pi), None, ALU.is_lt), [t4e])
        t4 = em.t("dve", lambda v: v.scalar_tensor_tensor(ang, rrf, TWO_PI, ang, ALU.mult, ALU.add), [t4f])
        tS = em.t("act", lambda s: s.activation(snt, ang, AF.Sin), [t4])
        tCa = em.t("act", lambda s: s.activation(angc, ang, AF.Abs), [t4])
        tC = em.t("act", lambda s: s.activation(cstt, angc, AF.Sin, scale=-1.0, bias=halfpi), [tCa])
        tPW = em.t("dve", lambda v, ps_=pslice: v.tensor_scalar(pwv, ps_, 4096.0, float(1.0 / 256.0),
                                                    ALU.subtract, ALU.mult), r_tabs.wdeps())
        tSL = em.t("dve", lambda v: v.tensor_tensor(v4(sclog), bc4(lnbase),
                                                    pwv.unsqueeze(2).broadcast_to([128, 4, 32]),
                                                    ALU.mult), [tPW] + r_lnb.rdep())
        tE1 = em.t("act", lambda s: s.activation(scl, sclog, AF.Exp), [tSL])
        tE2 = em.t("act", lambda s: s.activation(iscl, sclog, AF.Exp, scale=-1.0), [tSL])
        tcs = em.t("dve", lambda v: v.tensor_mul(csx, cstt, scl), [tC, tE1])
        tss = em.t("dve", lambda v: v.tensor_mul(ssx, snt, scl), [tS, tE1])
        tck = em.t("dve", lambda v: v.tensor_mul(cski, cstt, iscl), [tC, tE2])
        tsk = em.t("dve", lambda v: v.tensor_mul(sski, snt, iscl), [tS, tE2])
        qv = r_qsc.rdep(); kv = r_ksc.rdep()
        tq1 = em.t("dve", lambda v: v.tensor_tensor(
            csq2.rearrange("p (a b f) -> p a b f", a=4, b=2),
            csx.rearrange("p (a f) -> p a f", a=4).unsqueeze(2).broadcast_to([128, 4, 2, 32]),
            qscrep.rearrange("p (b f) -> p b f", b=2).unsqueeze(1).broadcast_to([128, 4, 2, 32]),
            ALU.mult), [tcs] + qv + r_tabs.wdeps())
        tq2 = em.t("dve", lambda v: v.scalar_tensor_tensor(
            nsq2.rearrange("p (a f) -> p a f", a=4),
            ssx.rearrange("p (a f) -> p a f", a=4), -1.0,
            qscrep[:, 32:64].unsqueeze(1).broadcast_to([128, 4, 32]),
            ALU.mult, ALU.mult), [tss] + qv)
        tq3 = em.t("dve", lambda v: v.tensor_tensor(
            sq2.rearrange("p (a f) -> p a f", a=4),
            ssx.rearrange("p (a f) -> p a f", a=4),
            qscrep[:, 0:32].unsqueeze(1).broadcast_to([128, 4, 32]),
            ALU.mult), [tss] + qv)
        tk1 = em.t("dve", lambda v: v.tensor_tensor(
            csk2.rearrange("p (a b f) -> p a b f", a=4, b=2),
            cski.rearrange("p (a f) -> p a f", a=4).unsqueeze(2).broadcast_to([128, 4, 2, 32]),
            kscrep.rearrange("p (b f) -> p b f", b=2).unsqueeze(1).broadcast_to([128, 4, 2, 32]),
            ALU.mult), [tck] + kv)
        tk2 = em.t("dve", lambda v: v.scalar_tensor_tensor(
            nsk2.rearrange("p (a f) -> p a f", a=4),
            sski.rearrange("p (a f) -> p a f", a=4), -1.0,
            kscrep[:, 32:64].unsqueeze(1).broadcast_to([128, 4, 32]),
            ALU.mult, ALU.mult), [tsk] + kv)
        tk3 = em.t("dve", lambda v: v.tensor_tensor(
            sk2.rearrange("p (a f) -> p a f", a=4),
            sski.rearrange("p (a f) -> p a f", a=4),
            kscrep[:, 0:32].unsqueeze(1).broadcast_to([128, 4, 32]),
            ALU.mult), [tsk] + kv)
        for tt in (tq1, tq2, tq3, tk1, tk2):
            r_qsc.read(tt) if tt in (tq1, tq2, tq3) else r_ksc.read(tt)
        r_ksc.read(tk3)
        t_tabs = [tq1, tq2, tq3, tk1, tk2, tk3]
        r_tabs.write(t_tabs[-1])
        for tt in t_tabs[:-1]:
            r_tabs.read(tt)

        for m in range(4):
            g_abs = ci * 4 + m
            slot = g_abs % NRING
            pqs = [r_p1, r_p2, r_p3]
            pps = [p1, p2, p3]
            tqkv = []
            for v3 in range(3):
                deps = ([r_x1T[dj].w for dj in range(8)] + pqs[v3].wdeps()
                        + r_wq.rdep())
                def mk_qkv(m=m, v3=v3):
                    def f(t):
                        for dc in range(8):
                            col = dc * (Tc + 1) + (1 if dc < 4 else 0) + m * 128
                            inst = t.matmul(pps[v3], x1T[:, col: col + 128],
                                            wqkv_sb[:, dc * 1536 + v3 * 512: dc * 1536 + (v3 + 1) * 512],
                                            start=(dc == 0), stop=(dc == 7))
                        return inst
                    return f
                tk_ = em.t("pe", mk_qkv(), deps)
                pqs[v3].write(tk_)
                for dj in range(8):
                    r_x1T[dj].read(tk_)
                r_wq.read(tk_)
                tqkv.append(tk_)
            te_q = em.t("act", lambda s: s.copy(q_sb, p1), [tqkv[0]] + r_q.wdeps())
            r_p1.read(te_q); r_q.write(te_q)
            te_k = em.t("act", lambda s: s.copy(k_sb, p2), [tqkv[1]] + r_k.wdeps())
            r_p2.read(te_k); r_k.write(te_k)
            te_v = em.t("act", lambda s, slot=slot: s.copy(vring[slot], p3),
                        [tqkv[2]] + r_vring[slot].wdeps())
            r_p3.read(te_v); r_vring[slot].write(te_v)

            tsq = em.t("act", lambda s: s.activation(tt1, q_sb, AF.Square),
                       [te_q] + r_tt1.wdeps())
            r_tt1.write(tsq); r_q.read(tsq)
            trd = em.t("dve", lambda v: v.reduce_sum(qss, tt1.rearrange("p (h c) -> p h c", h=8),
                                                     axis=AX.X), [tsq] + r_qss.wdeps())
            r_tt1.read(trd); r_qss.write(trd)
            tsq2 = em.t("act", lambda s: s.activation(tt2, k_sb, AF.Square),
                        [te_k] + r_tt2.wdeps())
            r_tt2.write(tsq2); r_k.read(tsq2)
            trd2 = em.t("dve", lambda v: v.reduce_sum(kss, tt2.rearrange("p (h c) -> p h c", h=8),
                                                      axis=AX.X), [tsq2] + r_kss.wdeps())
            r_tt2.read(trd2); r_kss.write(trd2)
            tsr = em.t("act", lambda s: s.activation(qss, qss, AF.Sqrt), [trd])
            r_qss.write(tsr)
            tmx = em.t("dve", lambda v: v.tensor_tensor(qss, qss, e12, ALU.max), [tsr])
            r_qss.write(tmx)
            trq = em.t("dve", lambda v: v.reciprocal(rq, qss), [tmx] + r_rq.wdeps())
            r_qss.read(trq); r_rq.write(trq)
            tsr2 = em.t("act", lambda s: s.activation(kss, kss, AF.Sqrt), [trd2])
            r_kss.write(tsr2)
            tmx2 = em.t("dve", lambda v: v.tensor_tensor(kss, kss, e12, ALU.max), [tsr2])
            r_kss.write(tmx2)
            trk = em.t("dve", lambda v: v.reciprocal(rk, kss), [tmx2] + r_rk.wdeps())
            r_kss.read(trk); r_rk.write(trk)

            vq = lambda t_: t_.rearrange("p (h b c) -> p h b c", h=8, b=2)
            tb = r_tabs.w
            ta1 = em.t("dve", lambda v, m=m: v.tensor_tensor(
                vq(tt1), vq(q_sb),
                csq2[:, m * 64:(m + 1) * 64].rearrange("p (b c) -> p b c", b=2)
                .unsqueeze(1).broadcast_to([128, 8, 2, 32]), ALU.mult),
                [te_q, tb] + r_tt1.wdeps())
            r_tt1.write(ta1); r_q.read(ta1)
            ta2 = em.t("dve", lambda v, m=m: v.tensor_tensor(
                tt2.rearrange("p (h b c) -> p h b c", h=8, b=2)[:, :, 0, :],
                q_sb.rearrange("p (h b c) -> p h b c", h=8, b=2)[:, :, 1, :],
                nsq2[:, m * 32:(m + 1) * 32].unsqueeze(1).broadcast_to([128, 8, 32]),
                ALU.mult), [te_q, tb] + r_tt2.wdeps())
            ta3 = em.t("dve", lambda v, m=m: v.tensor_tensor(
                tt2.rearrange("p (h b c) -> p h b c", h=8, b=2)[:, :, 1, :],
                q_sb.rearrange("p (h b c) -> p h b c", h=8, b=2)[:, :, 0, :],
                sq2[:, m * 32:(m + 1) * 32].unsqueeze(1).broadcast_to([128, 8, 32]),
                ALU.mult), [te_q, tb, ta2])
            r_tt2.write(ta3); r_q.read(ta3)
            ta4 = em.t("dve", lambda v: v.tensor_add(qr2, tt1, tt2), [ta1, ta3] + r_qr2.wdeps())
            r_tt1.read(ta4); r_tt2.read(ta4); r_qr2.write(ta4)
            tb1 = em.t("dve", lambda v, m=m: v.tensor_tensor(
                vq(tt1), vq(k_sb),
                csk2[:, m * 64:(m + 1) * 64].rearrange("p (b c) -> p b c", b=2)
                .unsqueeze(1).broadcast_to([128, 8, 2, 32]), ALU.mult),
                [te_k, tb, ta4] + r_tt1.wdeps())
            r_tt1.write(tb1); r_k.read(tb1)
            tb2 = em.t("dve", lambda v, m=m: v.tensor_tensor(
                tt2.rearrange("p (h b c) -> p h b c", h=8, b=2)[:, :, 0, :],
                k_sb.rearrange("p (h b c) -> p h b c", h=8, b=2)[:, :, 1, :],
                nsk2[:, m * 32:(m + 1) * 32].unsqueeze(1).broadcast_to([128, 8, 32]),
                ALU.mult), [te_k, tb, ta4] + r_tt2.wdeps())
            tb3 = em.t("dve", lambda v, m=m: v.tensor_tensor(
                tt2.rearrange("p (h b c) -> p h b c", h=8, b=2)[:, :, 1, :],
                k_sb.rearrange("p (h b c) -> p h b c", h=8, b=2)[:, :, 0, :],
                sk2[:, m * 32:(m + 1) * 32].unsqueeze(1).broadcast_to([128, 8, 32]),
                ALU.mult), [te_k, tb, tb2])
            r_tt2.write(tb3); r_k.read(tb3)
            tb4 = em.t("dve", lambda v: v.tensor_add(kr2, tt1, tt2), [tb1, tb3] + r_kr2.wdeps())
            r_tt1.read(tb4); r_tt2.read(tb4); r_kr2.write(tb4)
            tb5 = em.t("dve", lambda v: v.tensor_tensor(
                kr2.rearrange("p (h c) -> p h c", h=8),
                kr2.rearrange("p (h c) -> p h c", h=8),
                rk.unsqueeze(2).broadcast_to([128, 8, 64]), ALU.mult),
                [tb4, trk])
            r_rk.read(tb5); r_kr2.write(tb5)

            def mk_tr(src, m=m):
                def f(t):
                    for p4 in range(4):
                        inst = t.transpose(pT[:, p4 * 128:(p4 + 1) * 128],
                                           src[:, p4 * 128:(p4 + 1) * 128], eye)
                    return inst
                return f
            tqt = em.t("pe", mk_tr(qr2), [ta4] + r_pT.wdeps() + r_eye.rdep())
            r_pT.write(tqt); r_qr2.read(tqt)
            tqe = em.t("act", lambda s: s.copy(qT, pT), [tqt] + r_qT.wdeps())
            r_pT.read(tqe); r_qT.write(tqe)
            tkt = em.t("pe", mk_tr(kr2), [tb5] + r_pT.wdeps())
            r_pT.write(tkt); r_kr2.read(tkt)
            tke = em.t("act", lambda s, slot=slot: s.copy(kring[slot], pT),
                       [tkt] + r_kring[slot].wdeps())
            r_pT.read(tke); r_kring[slot].write(tke)

            if m in attn_tiles:
                ts8 = em.t("dve", lambda v: v.tensor_scalar(s8, rq, 8.0, None, ALU.mult),
                           [trq] + r_s8.wdeps())
                r_rq.read(ts8); r_s8.write(ts8)
                for h in range(8):
                    jj, pp = h % 2, h // 2
                    deps = (r_pS.wdeps() + [tqe] +
                            [r_kring[(g_abs - 4 + kc) % NRING].w for kc in range(5)] +
                            r_masks.rdep() + r_eye.rdep())
                    def mk_qk(g_abs=g_abs, jj=jj, pp=pp):
                        def f(t):
                            for kc in range(5):
                                slot2 = (g_abs - 4 + kc) % NRING
                                inst = t.matmul(pS[:, kc * 128:(kc + 1) * 128],
                                                qT[jj * 64:(jj + 1) * 64, pp * 128:(pp + 1) * 128],
                                                kring[slot2][jj * 64:(jj + 1) * 64, pp * 128:(pp + 1) * 128],
                                                start=True, stop=(kc not in (0, 4)))
                                if kc == 0:
                                    inst = t.matmul(pS[:, 0:128], eyebf, maskL, start=False, stop=True)
                                elif kc == 4:
                                    inst = t.matmul(pS[:, 512:640], eyebf, maskR, start=False, stop=True)
                            return inst
                        return f
                    tsc = em.t("pe", mk_qk(), deps)
                    r_pS.write(tsc)
                    r_qT.read(tsc)
                    for kc in range(5):
                        r_kring[(g_abs - 4 + kc) % NRING].read(tsc)
                    if ci == 2:
                        tmf = em.t("dve", lambda v, m=m: v.tensor_tensor(
                            pS[:, 0:640],
                            pS[:, 0:640], mf_sb[:, m * 640:(m + 1) * 640], ALU.add),
                            [tsc, tld[4]])
                        r_pS.write(tmf)
                        r_mf.read(tmf)
                        tsc = tmf
                    tev = em.t("act", lambda s: s.copy(P_sb, pS[:, 0:640]),
                               [tsc] + r_P.wdeps())
                    r_pS.read(tev); r_P.write(tev)
                    tmx_ = em.t("dve", lambda v: v.reduce_max(nm, P_sb, axis=AX.X,
                                                              negate=True),
                                [tev] + r_nm.wdeps())
                    r_P.read(tmx_); r_nm.write(tmx_)
                    tnm = em.t("dve", lambda v, h=h: v.tensor_tensor(nmz, nm, s8[:, h:h + 1],
                                                                     ALU.mult),
                               [tmx_, ts8] + r_nmz.wdeps())
                    r_nm.read(tnm); r_s8.read(tnm); r_nmz.write(tnm)
                    tex = em.t("act", lambda s, h=h: s.activation(P_bf, P_sb, AF.Exp,
                                                             bias=nmz, scale=s8[:, h:h + 1],
                                                             accum_out=sume),
                               [tnm] + r_Pb.wdeps() + r_sume.wdeps())
                    r_P.read(tex); r_nmz.read(tex); r_s8.read(tex)
                    r_Pb.write(tex); r_sume.write(tex)
                    trs = em.t("dve", lambda v: v.reciprocal(rsume, sume),
                               [tex] + r_rsume.wdeps())
                    r_sume.read(trs); r_rsume.write(trs)
                    pT_bfv = pT.bitcast(BF16)
                    def mk_ptr1():
                        def f(t):
                            for kc in range(4):
                                inst = t.transpose(pT_bfv[:, kc * 128:(kc + 1) * 128],
                                                   P_bf[:, kc * 128:(kc + 1) * 128], eyebf)
                            return inst
                        return f
                    tp1 = em.t("pe", mk_ptr1(), [tex] + r_pT.wdeps())
                    r_pT.write(tp1); r_Pb.read(tp1)
                    tpe1 = em.t("act", lambda s: s.copy(PT_sb[:, 0:512], pT_bfv[:, 0:512]),
                                [tp1] + r_PT.wdeps())
                    r_pT.read(tpe1); r_PT.write(tpe1)
                    tp2 = em.t("pe", lambda t: t.transpose(pT_bfv[:, 0:128], P_bf[:, 512:640], eyebf),
                               [tpe1] + r_pT.wdeps())
                    r_pT.write(tp2); r_Pb.read(tp2)
                    tpe2 = em.t("act", lambda s: s.copy(PT_sb[:, 512:640], pT_bfv[:, 0:128]), [tp2])
                    r_pT.read(tpe2); r_PT.write(tpe2)
                    pv = pVa if h % 2 == 0 else pVb
                    rpv = r_pV[h % 2]
                    def mk_pv(g_abs=g_abs, h=h, pv=pv):
                        def f(t):
                            for kc in range(5):
                                slot2 = (g_abs - 4 + kc) % NRING
                                inst = t.matmul(pv[:, 0:64], PT_sb[:, kc * 128:(kc + 1) * 128],
                                                vring[slot2][:, h * 64:(h + 1) * 64],
                                                start=(kc == 0), stop=(kc == 4))
                            return inst
                        return f
                    tpv = em.t("pe", mk_pv(), [tpe1, tpe2] + rpv.wdeps() +
                               [r_vring[(g_abs - 4 + kc) % NRING].w for kc in range(5)])
                    rpv.write(tpv); r_PT.read(tpv)
                    for kc in range(5):
                        r_vring[(g_abs - 4 + kc) % NRING].read(tpv)
                    tae = em.t("dve", lambda v, h=h, pv=pv: v.tensor_scalar(
                        attn[:, h * 64:(h + 1) * 64], pv[:, 0:64], rsume, None, ALU.mult),
                        [tpv, trs] + (r_attn.wdeps() if h == 0 else []))
                    rpv.read(tae); r_rsume.read(tae)
                    r_attn.write(tae)
                tat = em.t("pe", mk_tr(attn), [r_attn.w] + r_pT.wdeps())
                r_pT.write(tat); r_attn.read(tat)
                tate = em.t("act", lambda s: s.copy(attnT, pT), [tat] + r_attnT.wdeps())
                r_pT.read(tate); r_attnT.write(tate)
                for oh in range(2):
                    pw_ = p1 if oh == 0 else p2
                    rw = r_p1 if oh == 0 else r_p2
                    def mk_wo(oh=oh, pw_=pw_):
                        def f(t):
                            for p4 in range(4):
                                inst = t.matmul(pw_, attnT[:, p4 * 128:(p4 + 1) * 128],
                                                wo_sb[:, p4 * 1024 + oh * 512: p4 * 1024 + (oh + 1) * 512],
                                                start=(p4 == 0), stop=(p4 == 3))
                            return inst
                        return f
                    two = em.t("pe", mk_wo(), [tate] + rw.wdeps() + r_wo.rdep())
                    rw.write(two); r_attnT.read(two); r_wo.read(two)
                    if ci == 1:
                        dst = x2tail
                        rd = r_x2tail
                    else:
                        dst = x2_sb
                        rd = r_x2[m]
                    off = (0 if ci == 1 else m * D) + oh * 512
                    tx2 = em.t("dve", lambda v, dst=dst, off=off, pw_=pw_, m=m, oh=oh: v.tensor_add(
                        dst[:, off:off + 512], pw_[:, 0:512],
                        x_sb[:, m * D + oh * 512: m * D + (oh + 1) * 512]),
                        [two, r_x[m].w] + rd.wdeps())
                    rw.read(tx2); r_x[m].read(tx2); rd.write(tx2)
                if ci == 1:
                    tshm = em.t("dve", lambda v: v.tensor_scalar(x2tail, x2tail, shm_sb[:, 0:1],
                                                                 None, ALU.mult),
                                [r_x2tail.w])
                    r_x2tail.write(tshm)

        for dj in range(8):
            if ci == 1:
                tc_ = em.t("dve", lambda v, dj=dj: v.tensor_scalar(
                    x1T[:, dj * (Tc + 1): dj * (Tc + 1) + 1],
                    x1T[:, dj * (Tc + 1) + Tc: dj * (Tc + 1) + Tc + 1],
                    shm_sb[:, 0:1], None, ALU.mult),
                    r_x1T[dj].wdeps() + [r_x1T[dj].w])
            else:
                tc_ = em.t("dve", lambda v, dj=dj: v.tensor_copy(
                    x1T[:, dj * (Tc + 1): dj * (Tc + 1) + 1],
                    x1T[:, dj * (Tc + 1) + Tc: dj * (Tc + 1) + Tc + 1]),
                    r_x1T[dj].wdeps() + [r_x1T[dj].w])
            r_x1T[dj].write(tc_)

        if not own:
            continue

        for m in range(4):
            def mk_sh(m=m):
                def f(t):
                    inst = t.matmul(pS[:, 0:512], eye,
                                    x2_sb[:, m * D: m * D + 512],
                                    start=True, stop=True)
                    prev = (x2_sb[:, (m - 1) * D + 512:(m - 1) * D + 1024]
                            if m > 0 else x2tail[:, 512:1024])
                    inst = t.matmul(pS[:, 512:1024], shA,
                                    x2_sb[:, m * D + 512: m * D + 1024],
                                    start=True, stop=False)
                    inst = t.matmul(pS[:, 512:1024], shB, prev,
                                    start=False, stop=True)
                    return inst
                return f
            prev_res = r_x2[m - 1] if m > 0 else r_x2tail
            tsh = em.t("pe", mk_sh(), r_pS.wdeps() + [r_x2[m].w, prev_res.w] + r_sh.rdep())
            r_pS.write(tsh); r_x2[m].read(tsh); prev_res.read(tsh)
            tb1_ = em.t("dve", lambda v: v.bn_stats(bns[:, 0:6], pS[:, 0:512]),
                        [tsh] + r_lns.wdeps())
            tb2_ = em.t("dve", lambda v: v.bn_stats(bns[:, 6:12], pS[:, 512:1024]), [tsh, tb1_])
            tag = em.t("dve", lambda v: v.bn_aggr(mv, bns), [tb2_])
            tsd = em.t("act", lambda s: s.activation(sd, mv[:, 1:2], AF.Sqrt, bias=eps), [tag])
            trs_ = em.t("dve", lambda v: v.reciprocal(rs, sd), [tsd])
            tap = em.t("dve", lambda v, m=m: v.tensor_scalar(hh_sb[:, m * D:(m + 1) * D],
                                                             pS[:, 0:1024], mv[:, 0:1], rs,
                                                             ALU.subtract, ALU.mult),
                       [trs_] + r_hh[m].wdeps())
            r_pS.read(tap)
            tgm = em.t("dve", lambda v, m=m: v.tensor_mul(hh_sb[:, m * D:(m + 1) * D],
                                                          hh_sb[:, m * D:(m + 1) * D], gam_rep),
                       [tap] + r_gam.rdep())
            r_gam.read(tgm)
            tbt = em.t("dve", lambda v, m=m: v.tensor_add(hh_sb[:, m * D:(m + 1) * D],
                                                          hh_sb[:, m * D:(m + 1) * D], bet_rep),
                       [tgm] + r_bet.rdep())
            r_bet.read(tbt)
            r_lns.write(tbt)
            r_hh[m].write(tbt)
        ttail = em.t("dve", lambda v: v.tensor_copy(x2tail, x2_sb[:, 3 * D:4 * D]),
                     [r_x2[3].w] + r_x2tail.wdeps())
        r_x2[3].read(ttail)
        r_x2tail.write(ttail)
        pT_bf = pT.bitcast(BF16)
        for dj in range(8):
            def mk_ht(dj=dj):
                def f(t):
                    for m in range(4):
                        inst = t.transpose(pT_bf[:, m * 128:(m + 1) * 128],
                                           hh_sb[:, m * D + dj * 128: m * D + (dj + 1) * 128], eyebf)
                    return inst
                return f
            tht = em.t("pe", mk_ht(), [r_hh[m].w for m in range(4)] + r_pT.wdeps())
            r_pT.write(tht)
            for m in range(4):
                r_hh[m].read(tht)
            the = em.t("act", lambda s, dj=dj: s.copy(hT[:, dj * 512:(dj + 1) * 512],
                                                      pT_bf[:, 0:512]),
                       [tht] + ([] if dj > 0 else r_hT.wdeps()))
            r_pT.read(the)
            r_hT.write(the)
        for hh in range(NHh):
            pb = hh % 2
            tw1a = em.t("ldw", lambda s, hh=hh, pb=pb: s.dma_start(
                w1a_sb[pb], W["w1a_bf"][:, hh * 1024:(hh + 1) * 1024]),
                r_w1a[pb].wdeps())
            r_w1a[pb].write(tw1a)
            tw1g = em.t("ldw", lambda s, hh=hh, pb=pb: s.dma_start(
                w1g_sb[pb], W["w1g_bf"][:, hh * 1024:(hh + 1) * 1024]),
                r_w1g[pb].wdeps())
            r_w1g[pb].write(tw1g)
            def mk_w1(wsb, pp_, pb=pb):
                def f(t):
                    for dc in range(8):
                        inst = t.matmul(pp_, wsb[pb][:, dc * 128:(dc + 1) * 128],
                                        hT[:, dc * 512:(dc + 1) * 512],
                                        start=(dc == 0), stop=(dc == 7))
                    return inst
                return f
            ta_ = em.t("pe", mk_w1(w1a_sb, p1), [tw1a, r_hT.w] + r_p1.wdeps())
            r_p1.write(ta_); r_w1a[pb].read(ta_); r_hT.read(ta_)
            tg_ = em.t("pe", mk_w1(w1g_sb, p2), [tw1g, r_hT.w] + r_p2.wdeps())
            r_p2.write(tg_); r_w1g[pb].read(tg_)
            tae_ = em.t("dve", lambda v, hh=hh: v.tensor_scalar(a_sb, p1, b1a_sb[:, hh:hh + 1],
                                                                None, ALU.add),
                        [ta_, tld[0]] + r_a.wdeps())
            r_p1.read(tae_); r_a.write(tae_); r_b1a.read(tae_)
            tge_ = em.t("act", lambda s, hh=hh: s.activation(gg_sb, p2, AF.Gelu,
                                                             bias=b1g_sb[:, hh:hh + 1]),
                        [tg_, tld[1]] + r_gg.wdeps())
            r_p2.read(tge_); r_gg.write(tge_); r_b1g.read(tge_)
            tgt_ = em.t("dve", lambda v, hh=hh: v.tensor_mul(gate[:, hh * 512:(hh + 1) * 512],
                                                             a_sb, gg_sb),
                        [tae_, tge_] + r_gate[hh].wdeps())
            r_a.read(tgt_); r_gg.read(tgt_); r_gate[hh].write(tgt_)

        def do_store(m):
            if final:
                tmr = em.t("ld", lambda s, m=m, oc0=oc0: s.dma_start(
                    mres_sb, W["mres_ap"][oc0 + m * 128: oc0 + (m + 1) * 128, :]),
                    r_x[0].wdeps())
                r_x[0].write(tmr)
                tfa = em.t("dve", lambda v: v.tensor_add(x3_sb, x3_sb, mres_sb),
                           [r_x3.w, tmr])
                r_x[0].read(tfa); r_x3.write(tfa)
                tb1f = em.t("dve", lambda v: v.bn_stats(bns[:, 0:6], x3_sb[:, 0:512]),
                            [tfa] + r_lns.wdeps())
                tb2f = em.t("dve", lambda v: v.bn_stats(bns[:, 6:12], x3_sb[:, 512:1024]),
                            [tfa, tb1f])
                tagf = em.t("dve", lambda v: v.bn_aggr(mv, bns), [tb2f])
                tsdf = em.t("act", lambda s: s.activation(sd, mv[:, 1:2], AF.Sqrt, bias=eps), [tagf])
                trsf = em.t("dve", lambda v: v.reciprocal(rs, sd), [tsdf])
                tapf = em.t("dve", lambda v: v.tensor_scalar(x3_sb, x3_sb, mv[:, 0:1], rs,
                                                             ALU.subtract, ALU.mult),
                            [trsf, r_x3.w])
                r_lns.write(tapf)
                r_x3.write(tapf)
            tst = em.t("st", lambda s, m=m, oc0=oc0: s.dma_start(
                out_dst[oc0 + m * 128: oc0 + (m + 1) * 128, :], x3_sb), [r_x3.w])
            r_x3.read(tst)
            if xrow_new is not None:
                xrow_new[(oc0 + m * 128) // 512].append(tst)
            if (not final) and (oc0 + m * 128 >= NT - H4):
                rr = oc0 + m * 128 - (NT - H4)
                for si in range(2):
                    mts = mt0 if si == 0 else mt1
                    tc0 = em.t("dve", lambda v, mts=mts: v.tensor_scalar(
                        a_sb, x3_sb[:, 0:512], mts, None, ALU.mult),
                        [r_x3.w] + r_a.wdeps())
                    r_x3.read(tc0); r_a.write(tc0)
                    tc1 = em.t("dve", lambda v, mts=mts: v.tensor_scalar(
                        gg_sb, x3_sb[:, 512:1024], mts, None, ALU.mult),
                        [r_x3.w] + r_gg.wdeps())
                    r_x3.read(tc1); r_gg.write(tc1)
                    ts0 = em.t("st", lambda s, si=si, rr=rr: s.dma_start(
                        W["rs_t2"][si, rr:rr + 128, 0:512], a_sb), [tc0])
                    r_a.read(ts0)
                    ts1 = em.t("st", lambda s, si=si, rr=rr: s.dma_start(
                        W["rs_t2"][si, rr:rr + 128, 512:1024], gg_sb), [tc1])
                    r_gg.read(ts1)
                    tail_toks.append(ts0)
                    tail_toks.append(ts1)

        pws8 = [p1, p2, p3, pVa, pVb, pS[:, 0:512], pS[:, 512:1024], pT]
        rws8 = [r_p1, r_p2, r_p3, r_pV[0], r_pV[1], r_pS, r_pS, r_pT]
        tw2g = [None] * 8
        for hh in range(NHh):
            pb2 = hh % 2
            tl2 = em.t("ldw", lambda s, hh=hh, pb2=pb2: s.dma_start(
                w2s_sb[pb2], W["w2_bf"][:, hh * 1024:(hh + 1) * 1024]),
                r_w2s[pb2].wdeps())
            r_w2s[pb2].write(tl2)
            for m in range(4):
                for oh in range(2):
                    sl = m * 2 + oh
                    def mk_w2p(m=m, oh=oh, sl=sl, hh=hh, pb2=pb2):
                        def f(t):
                            return t.matmul(pws8[sl][:, 0:512],
                                            gate[:, hh * 512 + m * 128: hh * 512 + (m + 1) * 128],
                                            w2s_sb[pb2][:, oh * 512:(oh + 1) * 512],
                                            start=(hh == 0), stop=False)
                        return f
                    deps = [tl2, r_gate[hh].w]
                    if hh == 0:
                        deps += rws8[sl].wdeps()
                    tk2 = em.t("pe", mk_w2p(), deps)
                    if hh == 0:
                        rws8[sl].write(tk2)
                    tw2g[sl] = tk2
                    r_gate[hh].read(tk2)
                    r_w2s[pb2].read(tk2)
        for sl in range(8):
            oh = sl % 2
            def mk_b2(sl=sl, oh=oh):
                def f(t):
                    return t.matmul(pws8[sl][:, 0:512], ones1,
                                    b2_sb[:, oh * 512:(oh + 1) * 512],
                                    start=False, stop=True)
                return f
            tkb = em.t("pe", mk_b2(), [tw2g[sl], tld[2]])
            rws8[sl].write(tkb)
            r_b2.read(tkb)
            tw2g[sl] = tkb
        for m in range(4):
            for oh in range(2):
                sl = m * 2 + oh
                tx3 = em.t("dve", lambda v, m=m, oh=oh, sl=sl: v.tensor_add(
                    x3_sb[:, oh * 512:(oh + 1) * 512], pws8[sl][:, 0:512],
                    x2_sb[:, m * D + oh * 512: m * D + (oh + 1) * 512]),
                    [tw2g[sl], r_x2[m].w] + (r_x3.wdeps() if oh == 0 else [r_x3.w]))
                rws8[sl].read(tx3); r_x2[m].read(tx3); r_x3.write(tx3)
            do_store(m)


GROUPS = [[0, 1], [2, 3], [4, 5], [6, 7]]


def build_fused(dbg=False, skip_de=False):
    nc = bass.Bass("TRN2", target_bir_lowering=False)
    Tc = 512
    L = 2 * NT
    NCA = 16
    HID = 2816
    NHh = 22

    ein = lambda n, sh, dt=F32: nc.dram_tensor(n, sh, dt, kind="ExternalInput").ap()
    x_a = ein("x_a", [L, D])
    x_own = ein("x_own", [NT, D])
    w_in_r = ein("w_in_h", [128, 8 * 1024])
    bin_ur = ein("bin_uh", [128, 4]); bin_zr = ein("bin_zh", [128, 4])
    bconv_r = ein("bconvh", [128, 4]); wconv_r = ein("wconvh", [128, 16])
    wxp_r = ein("wxp_h", [128, 4 * 48]); bxp_re = ein("bxp_r", [16, 3])
    wdt_r = ein("wdt_h", [16, 512]); bdt_r = ein("bdth", [128, 4])
    alog_r = ein("alog_h", [128, 64])
    mb0_r = ein("mb0", [128, 1]); mb1_r = ein("mb1", [128, 1])
    mt0_r = ein("mt0", [128, 1]); mt1_r = ein("mt1", [128, 1])
    wout_r = ein("wout_r", [128, 8 * 1024]); bout_r = ein("bout_r", [1, 1024])
    pos_r = ein("pos_r", [128, 10 * 4])
    mf_r = ein("mf_r", [128, 4 * 640], BF16)
    shm_r = ein("shm_r", [128, 1])
    FWE = []
    for li in range(2):
        FWE.append(dict(
            wqkv_r=ein(f"wqkv_r{li}", [128, 8 * 1536]),
            wo_r=ein(f"wo_r{li}", [128, 4 * 1024]),
            w1a_r=ein(f"w1a_r{li}", [128, 8 * HID]),
            w1g_r=ein(f"w1g_r{li}", [128, 8 * HID]),
            w2_r=ein(f"w2_r{li}", [128, NHh * 1024]),
            b1a_r=ein(f"b1a_r{li}", [128, NHh]),
            b1g_r=ein(f"b1g_r{li}", [128, NHh]),
            b2_r=ein(f"b2_r{li}", [1, 1024]),
            lng_row=ein(f"lng_row{li}", [1, 1024]),
            lnb_row=ein(f"lnb_row{li}", [1, 1024]),
            qs_row=ein(f"qs_row{li}", [1, 64]),
            ks_row=ein(f"ks_row{li}", [1, 64]),
        ))
    xo = nc.dram_tensor("xo_t", [NT, D], F32, kind="ExternalOutput").ap()

    DBGOK = {"u_h", "sz_h", "x1own_h", "x2own_h"}
    din = lambda n, sh, dt=F32: nc.dram_tensor(
        n, sh, dt, kind=("ExternalOutput" if (dbg and n in DBGOK) else "Internal")).ap()
    u_h = din("u_h", [512, L]); sz_h = din("sz_h", [512, L])
    xdbp_h = din("xdbp_h", [48, L]); xdbs_h = din("xdbs_h", [48, L])
    rs_gin = din("rs_gin", [2, 1024, NT], BF16)
    g_full = din("g_full", [1024, NT], BF16)
    x1own_h = din("x1own_h", [NT, D])
    rs_t_in = din("rs_t_in", [2, H4, D])
    halo_h = din("halo_h", [H4, D])
    x2own_h = din("x2own_h", [NT, D])
    rs_t2_in = din("rs_t2_in", [2, H4, D])
    halo2_h = din("halo2_h", [H4, D])
    w_in_bf = din("w_in_bf", [128, 8 * 1024], BF16)
    wout_bf = din("wout_bf", [128, 8 * 1024], BF16)
    FWB = []
    for li in range(2):
        FWB.append(dict(
            wqkv_bf=din(f"wqkv_bf{li}", [128, 8 * 1536], BF16),
            wo_bf=din(f"wo_bf{li}", [128, 4 * 1024], BF16),
            w1a_bf=din(f"w1a_bf{li}", [128, 8 * HID], BF16),
            w1g_bf=din(f"w1g_bf{li}", [128, 8 * HID], BF16),
            w2_bf=din(f"w2_bf{li}", [128, NHh * 1024], BF16),
        ))

    sb = lambda n, sh, dt=F32: nc.alloc_sbuf_tensor(n, sh, dt).ap()
    eye = sb("eye", [128, 128]); eyebf = sb("eyebf", [128, 128], BF16)
    ones1 = sb("ones1", [1, 128]); eps = sb("eps", [128, 1]); e12 = sb("e12", [128, 8])
    halfpi = sb("halfpi", [128, 1])
    maskL = sb("maskL", [128, 128], BF16); maskR = sb("maskR", [128, 128], BF16)
    shA = sb("shA", [128, 128]); shB = sb("shB", [128, 128])
    wqkv_sb = sb("wqkv_sb", [128, 8 * 1536], BF16)
    wo_sb = sb("wo_sb", [128, 4 * 1024], BF16)
    w2s_sb = [sb(f"w2s_sb{i}", [128, 1024], BF16) for i in range(2)]
    w1a_sb = [sb(f"w1a_sb{i}", [128, 8 * 128], BF16) for i in range(2)]
    w1g_sb = [sb(f"w1g_sb{i}", [128, 8 * 128], BF16) for i in range(2)]
    b1a_sb = sb("b1a_sb", [128, NHh]); b1g_sb = sb("b1g_sb", [128, NHh])
    b2_sb = sb("b2_sb", [1, 1024])
    gambet = sb("gambet", [128, 2048])
    gam_rep = gambet[:, 0:1024]; bet_rep = gambet[:, 1024:2048]
    lngrow_sb = sb("lngrow_sb", [1, 1024]); lnbrow_sb = sb("lnbrow_sb", [1, 1024])
    qsrow_sb = sb("qsrow_sb", [1, 64]); ksrow_sb = sb("ksrow_sb", [1, 64])
    qscrep = sb("qscrep", [128, 64]); kscrep = sb("kscrep", [128, 64])
    pos_sb = sb("pos_sb", [128, 10 * 4])
    mf_sb = sb("mf_sb", [128, 4 * 640], BF16)
    shm_sb = sb("shm_sb", [128, 1])
    invf = sb("invf", [128, 32]); lnbase = sb("lnbase", [128, 32])
    frq = sb("frq", [128, 4 * 32])
    rrf = sb("rrf", [128, 4 * 32]); ang = sb("ang", [128, 4 * 32]); angc = sb("angc", [128, 4 * 32])
    rri = nc.alloc_sbuf_tensor("rri", [128, 4 * 32], I32).ap()
    snt = rri.bitcast(F32); cstt = sb("cstt", [128, 4 * 32])
    pwv = sb("pwv", [128, 4]); sclog = sb("sclog", [128, 4 * 32])
    scl = sb("scl", [128, 4 * 32]); iscl = sb("iscl", [128, 4 * 32])
    csx = sb("csx", [128, 4 * 32]); ssx = sb("ssx", [128, 4 * 32])
    cski = sb("cski", [128, 4 * 32]); sski = sb("sski", [128, 4 * 32])
    csq2 = sb("csq2", [128, 4 * 64]); nsq2 = sb("nsq2", [128, 4 * 32]); sq2 = sb("sq2", [128, 4 * 32])
    csk2 = sb("csk2", [128, 4 * 64]); nsk2 = sb("nsk2", [128, 4 * 32]); sk2 = sb("sk2", [128, 4 * 32])
    x_sb = sb("x_sb", [128, 4 * D])
    x1T = sb("x1T", [128, 8 * (Tc + 1)], BF16)
    q_sb = sb("q_sb", [128, 512]); k_sb = sb("k_sb", [128, 512])
    qss = sb("qss", [128, 8]); kss = sb("kss", [128, 8])
    rq = sb("rq", [128, 8]); rk = sb("rk", [128, 8])
    qr2 = sb("qr2", [128, 512]); kr2 = sb("kr2", [128, 512])
    tt1 = sb("tt1", [128, 512]); tt2 = sb("tt2", [128, 512])
    qT = sb("qT", [128, 512], BF16)
    NRING = 6
    big_kv = sb("big_kv", [128, 12 * 512], BF16)
    kring = [big_kv[:, i * 512:(i + 1) * 512] for i in range(NRING)]
    vring = [big_kv[:, (6 + i) * 512:(7 + i) * 512] for i in range(NRING)]
    s8 = sb("s8", [128, 8]); nm = sb("nm", [128, 1]); nmz = sb("nmz", [128, 1])
    P_sb = sb("P_sb", [128, 640])
    P_bf = sb("P_bf", [128, 640], BF16)
    PT_sb = sb("PT_sb", [128, 5 * 128], BF16)
    sume = sb("sume", [128, 1]); rsume = sb("rsume", [128, 1])
    attn = sb("attn", [128, 512])
    attnT = sb("attnT", [128, 512], BF16)
    x2_sb = sb("x2_sb", [128, 4 * D])
    x2tail = sb("x2tail", [128, D])
    bns = sb("bns", [128, 12]); mv = sb("mv", [128, 2]); sd = sb("sd", [128, 1]); rs_t = sb("rs", [128, 1])
    hh_sb = sb("hh_sb", [128, 4 * D], BF16)
    hT = sb("hT", [128, 8 * 512], BF16)
    a_sb = sb("a_sb", [128, 512]); gg_sb = sb("gg_sb", [128, 512])
    gate = sb("gate", [128, NHh * 512], BF16)
    x3_sb = sb("x3_sb", [128, D])
    masks4 = sb("masks4", [128, 4])
    mres_sb = x_sb[:, 0:D]

    ar1f = wqkv_sb.bitcast(F32)
    ar2f = gate.bitcast(F32)
    ar5f = hh_sb.bitcast(F32)
    ar6f = x1T.bitcast(F32)
    ar7f = wo_sb.bitcast(F32)
    bkf = big_kv.bitcast(F32)
    mff = mf_sb.bitcast(F32)
    a_win = wqkv_sb[:, 0:8192]
    a_xT = x1T[:, 0:4096]
    a_uraw = x2_sb[:, 0:2060]
    a_u = ar2f[:, 2048:4096]
    a_szt = ar5f[:, 0:2048]
    a_diag = ar2f[:, 0:2048]
    a_xdb_t = attn[0:48, 0:512]
    a_wxp = ar7f[:, 512:704]
    wdt_sb = ar7f[0:16, 0:512]
    binu = cstt[:, 16:20]; binz = cstt[:, 20:24]; bconvh = cstt[:, 24:28]
    bdth = cstt[:, 28:32]; wconvh = cstt[:, 0:16]
    bxp0 = cstt[0:48, 32:33]; bxpr = cstt[0:16, 33:36]
    mb0 = masks4[:, 0:1]; mb1 = masks4[:, 1:2]; mt0 = masks4[:, 2:3]; mt1 = masks4[:, 3:4]
    b_u = x_sb[:, 0:2048]; b_dt = x_sb[:, 2048:4096]
    b_du = x2_sb[:, 0:2048]; b_duT = x2_sb[:, 2048:4096]
    b_szc = x2_sb[:, 0:2048]
    b_dA = [ar1f[:, 0:2048], ar1f[:, 2048:4096]]
    b_h = [ar2f[:, 0:2048], ar2f[:, 2048:4096]]
    b_dg = [ar5f[:, 0:512], ar5f[:, 512:1024]]
    b_crep = [ar5f[:, 1024:1536], ar5f[:, 1536:2048]]
    b_gtmp = ar5f[:, 1024:1536]
    b_yt = [bkf[:, 0:2048], ar6f[:, 0:2048]]
    b_gst = hT
    A_sb = frq[:, 0:64]; bmT = rrf[:, 0:64]; st_sb = ang[:, 0:64]
    alog_sb = angc[:, 0:64]; ones16 = snt[0:16, 0:128]
    selv = gambet[0:16, 0:2048]
    b_bm = P_sb[0:16, 0:512]; b_cm = tt1[0:16, 0:512]
    spA = mff[:, 0:512]; spB = mff[:, 512:1024]
    c_gsb = hh_sb
    c_x1 = ar2f[:, 0:4096]
    bout_sb = lngrow_sb

    def wout_at(dc, oh):
        if dc < 4:
            return hT[:, dc * 1024 + oh * 512: dc * 1024 + (oh + 1) * 512]
        return big_kv[:, (dc - 4) * 1024 + oh * 512: (dc - 4) * 1024 + (oh + 1) * 512]

    p1 = nc.alloc_psum_tensor("p1", [128, 512], F32).ap()
    p2 = nc.alloc_psum_tensor("p2", [128, 512], F32).ap()
    p3 = nc.alloc_psum_tensor("p3", [128, 512], F32).ap()
    pS = nc.alloc_psum_tensor("pS", [128, 1024], F32).ap()
    pT = nc.alloc_psum_tensor("pT", [128, 512], F32).ap()
    pVa = nc.alloc_psum_tensor("pVa", [128, 512], F32).ap()
    pVb = nc.alloc_psum_tensor("pVb", [128, 512], F32).ap()
    pD48 = pT[0:48, :]
    p_y = [p1, p2, p3, pVa]
    p_db = [pVb, pS[:, 0:512]]
    p_cr = pS[:, 512:1024]
    p_tr = pT
    pM = [p1, p2, p3, pVa]

    ld = Ch(nc, "ld"); st = Ch(nc, "st"); cst = Ch(nc, "cst")
    coll = Ch(nc, "coll"); ldc = Ch(nc, "ldc"); c_pp = Ch(nc, "c_pp")
    a_tr = Ch(nc, "a_tr"); a_mm = Ch(nc, "a_mm"); a_xt = Ch(nc, "a_xt")
    a_ur = Ch(nc, "a_ur"); a_sz = Ch(nc, "a_sz"); a_uc = Ch(nc, "a_uc")
    a_xdb = Ch(nc, "a_xdb"); a_tail = Ch(nc, "a_tail")
    b_bmt = Ch(nc, "b_bmt"); b_bev = Ch(nc, "b_bev")
    b_duc = Ch(nc, "b_duc"); b_dut = Ch(nc, "b_dut"); b_due = Ch(nc, "b_due")
    b_da = Ch(nc, "b_da"); b_dgc = Ch(nc, "b_dgc")
    b_cmc = Ch(nc, "b_cmc"); b_ce = Ch(nc, "b_ce")
    b_db = Ch(nc, "b_db"); b_sc = Ch(nc, "b_sc")
    b_ytc = Ch(nc, "b_ytc"); b_ya = Ch(nc, "b_ya"); b_stc = Ch(nc, "b_stc")
    b_spa = Ch(nc, "b_spa"); b_dtm = Ch(nc, "b_dtm"); b_dta = Ch(nc, "b_dta")
    b_xb = Ch(nc, "b_xb"); b_gm = Ch(nc, "b_gm")
    c_mm = Ch(nc, "c_mm"); c_ev = Ch(nc, "c_ev"); c_v = Ch(nc, "c_v")
    c_a = Ch(nc, "c_a"); c_x1c = Ch(nc, "c_x1c"); c_tl = Ch(nc, "c_tl")

    CSTN = 12 * 16

    with nc.Block("prep0") as blk:
        @blk.gpsimd
        def _(g):
            cst.sig(g.dma_start(out=w_in_bf, in_=w_in_r), 16)
            cst.sig(g.dma_start(out=wout_bf, in_=wout_r), 16)
            for li in range(2):
                for kk in ("wqkv", "wo", "w1a", "w1g", "w2"):
                    cst.sig(g.dma_start(out=FWB[li][kk + "_bf"], in_=FWE[li][kk + "_r"]), 16)

        @blk.sync
        def _(s):
            for dst, src_ in [(a_wxp, wxp_r), (wdt_sb, wdt_r), (binu, bin_ur),
                              (binz, bin_zr), (bconvh, bconv_r), (bdth, bdt_r),
                              (wconvh, wconv_r), (bxpr, bxp_re), (alog_sb, alog_r),
                              (mb0, mb0_r), (mb1, mb1_r), (mt0, mt0_r), (mt1, mt1_r),
                              (pos_sb, pos_r), (shm_sb, shm_r), (bout_sb, bout_r)]:
                ld.sig(s.dma_start(dst, src_), 16)
            cst.wait(s, 16)
            ld.sig(s.dma_start(a_win, w_in_bf), 16)
            ld.wait(s, ld.n)

        @blk.vector
        def _(v):
            v.memset(ones1, 1.0)
            v.memset(eps, 1e-5)
            v.memset(e12, 1e-12)
            v.memset(halfpi, float(np.pi / 2))
            v.memset(attn[:, 0:128], 1.0)
            v.memset(x3_sb[:, 0:256], 0.0)
            v.memset(shB, 0.0)
            v.memset(ones16, 1.0)
            v.memset(bxp0, 0.0)
            v.memset(st_sb, 0.0)
            for dc in range(4):
                v.memset(x2_sb[:, dc * 515: dc * 515 + 3], 0.0)

    with nc.Block("prep1") as blk:
        @blk.gpsimd
        def _(g):
            g.affine_select(eye, attn[:, 0:128], pattern=[[1, 128]],
                            compare_op=ALU.is_equal, fill=0.0, base=0, channel_multiplier=-1)
            g.affine_select(shA, attn[:, 0:128], pattern=[[1, 128]],
                            compare_op=ALU.is_equal, fill=0.0, base=-1, channel_multiplier=-1)
            g.affine_select(shB[:, 0:1], attn[:, 0:1], pattern=[[1, 1]],
                            compare_op=ALU.is_equal, fill=0.0, base=-127, channel_multiplier=1)
            g.affine_select(x3_sb[:, 0:128], x3_sb[:, 0:128], pattern=[[1, 128]],
                            compare_op=ALU.is_ge, fill=NEG, base=0, channel_multiplier=-1)
            g.affine_select(x3_sb[:, 128:256], x3_sb[:, 128:256], pattern=[[-1, 128]],
                            compare_op=ALU.is_ge, fill=NEG, base=0, channel_multiplier=1)
            for nn in range(16):
                g.affine_select(selv[:, nn * 128:(nn + 1) * 128], ones16,
                                pattern=[[0, 128]], compare_op=ALU.is_equal,
                                fill=0.0, base=-nn, channel_multiplier=1)
            g.iota(invf, pattern=[[1, 32]], base=0, channel_multiplier=0,
                   allow_small_or_imprecise_dtypes=True)
            g.iota(lnbase, pattern=[[1, 32]], base=0, channel_multiplier=0,
                   allow_small_or_imprecise_dtypes=True)

    with nc.Block("prep2") as blk:
        @blk.scalar
        def _(s):
            s.copy(eyebf, eye)
            s.copy(maskL, x3_sb[:, 0:128])
            s.copy(maskR, x3_sb[:, 128:256])
            s.activation(invf, invf, AF.Exp, scale=float(-np.log(10000.0) / 32.0))
            c_pp.sig(s.activation(A_sb, alog_sb, AF.Exp))
            for dc in range(4):
                for j in range(4):
                    i = dc * 4 + j
                    s.activation(a_diag[:, i * 128:(i + 1) * 128], eye, AF.Copy,
                                 scale=wconvh[:, i:i + 1])
            c_pp.wait(s, 2)
            c_pp.sig(s.activation(lnbase, lnbase, AF.Ln))

        @blk.vector
        def _(v):
            c_pp.wait(v, 1)
            v.tensor_scalar(A_sb, A_sb, -1.0, None, ALU.mult)
            c_pp.sig(v.tensor_scalar(lnbase, lnbase, 2.0, 25.6, ALU.mult, ALU.add))
            c_pp.wait(v, 3)
            v.tensor_scalar(lnbase, lnbase, float(np.log(89.6)), None, ALU.subtract)

    for k in range(NCA):
        oc0 = k * Tc
        B = dict(ld=ld.n, st=st.n, tr=a_tr.n, mm=a_mm.n, xt=a_xt.n, ur=a_ur.n,
                 sz=a_sz.n, uc=a_uc.n, xdb=a_xdb.n)
        with nc.Block(f"A{k}") as blk:
            @blk.sync
            def _(s, k=k, B=B, oc0=oc0):
                for ti in range(4):
                    ld.sig(s.dma_start(x_sb[:, ti * D:(ti + 1) * D],
                                       x_a[k * Tc + ti * 128: k * Tc + (ti + 1) * 128, :]), 16)
                a_sz.wait(s, B["sz"] + 4)
                for dc in range(4):
                    st.sig(s.dma_start(sz_h[dc * 128:(dc + 1) * 128, oc0:oc0 + Tc],
                                       a_szt[:, dc * Tc:(dc + 1) * Tc]), 16)
                a_uc.wait(s, B["uc"] + 4)
                for dc in range(4):
                    st.sig(s.dma_start(u_h[dc * 128:(dc + 1) * 128, oc0:oc0 + Tc],
                                       a_u[:, dc * Tc:(dc + 1) * Tc]), 16)
                a_xdb.wait(s, B["xdb"] + 1)
                st.sig(s.dma_start(xdbp_h[:, oc0:oc0 + Tc], a_xdb_t), 16)
                st.wait(s, st.n)

            @blk.tensor
            def _(t, k=k, B=B):
                ld.wait(t, B["ld"] + 64)
                for dj in range(8):
                    if dj > 0:
                        a_xt.wait(t, B["xt"] + dj)
                    for ti in range(4):
                        inst = t.transpose(p1[:, ti * 128:(ti + 1) * 128],
                                           x_sb[:, ti * D + dj * 128: ti * D + (dj + 1) * 128], eye)
                    a_tr.sig(inst)
                a_xt.wait(t, B["xt"] + 8)
                for oc in range(8):
                    ps = p2 if (oc % 2 == 0) else p3
                    if oc >= 2:
                        if oc - 2 < 4:
                            a_ur.wait(t, B["ur"] + oc - 1)
                        else:
                            a_sz.wait(t, B["sz"] + oc - 5)
                    for dc in range(8):
                        inst = t.matmul(ps, a_win[:, dc * 1024 + oc * 128: dc * 1024 + (oc + 1) * 128],
                                        a_xT[:, dc * Tc:(dc + 1) * Tc],
                                        start=(dc == 0), stop=(dc == 7))
                    a_mm.sig(inst)
                a_ur.wait(t, B["ur"] + 4)
                for dc in range(4):
                    ps = p2 if (dc % 2 == 0) else p3
                    if dc < 2:
                        a_sz.wait(t, B["sz"] + 3 + dc)
                    else:
                        a_uc.wait(t, B["uc"] + dc - 1)
                    for j in range(4):
                        i = dc * 4 + j
                        inst = t.matmul(ps, a_diag[:, i * 128:(i + 1) * 128],
                                        a_uraw[:, dc * (Tc + 3) + j: dc * (Tc + 3) + j + Tc],
                                        start=(j == 0), stop=(j == 3))
                    a_mm.sig(inst)
                a_uc.wait(t, B["uc"] + 4)
                for dc in range(4):
                    inst = t.matmul(pD48, a_wxp[:, dc * 48:(dc + 1) * 48],
                                    a_u[:, dc * Tc:(dc + 1) * Tc],
                                    start=(dc == 0), stop=(dc == 3))
                a_mm.sig(inst)

            @blk.scalar
            def _(s, k=k, B=B):
                for dj in range(8):
                    a_tr.wait(s, B["tr"] + dj + 1)
                    a_xt.sig(s.copy(a_xT[:, dj * Tc:(dj + 1) * Tc], p1))
                for oc in range(4, 8):
                    ps = p2 if (oc % 2 == 0) else p3
                    a_mm.wait(s, B["mm"] + oc + 1)
                    a_sz.sig(s.activation(a_szt[:, (oc - 4) * Tc:(oc - 3) * Tc],
                                          ps, AF.Silu, bias=binz[:, oc - 4:oc - 3]))
                for dc in range(4):
                    ps = p2 if (dc % 2 == 0) else p3
                    a_mm.wait(s, B["mm"] + 8 + dc + 1)
                    a_uc.sig(s.activation(a_u[:, dc * Tc:(dc + 1) * Tc], ps, AF.Silu,
                                          bias=bconvh[:, dc:dc + 1]))

            @blk.vector
            def _(v, k=k, B=B):
                for oc in range(4):
                    ps = p2 if (oc % 2 == 0) else p3
                    a_mm.wait(v, B["mm"] + oc + 1)
                    a_ur.sig(v.tensor_scalar(a_uraw[:, oc * (Tc + 3) + 3: oc * (Tc + 3) + 3 + Tc],
                                             ps, binu[:, oc:oc + 1], None, ALU.add))
                a_mm.wait(v, B["mm"] + 13)
                a_xdb.sig(v.tensor_scalar(a_xdb_t, pD48, bxp0, None, ALU.add))

            @blk.gpsimd
            def _(g, B=B):
                for dc in range(4):
                    a_mm.wait(g, B["mm"] + 8 + dc + 1)
                    g.tensor_copy(a_uraw[:, dc * (Tc + 3): dc * (Tc + 3) + 3],
                                  a_uraw[:, dc * (Tc + 3) + Tc: dc * (Tc + 3) + Tc + 3])

    with nc.Block("AR") as blk:
        @blk.gpsimd
        def _(g):
            st.wait(g, st.n)
            coll.sig(g.collective_compute("AllReduce", ALU.add, replica_groups=GROUPS,
                                          ins=[xdbp_h.opt()], outs=[xdbs_h.opt()]), 1)

    Ts = 512
    for k in range(NCA):
        t0 = k * Ts
        tbs = t0 // NT
        c0 = t0 % NT
        B = dict(ld=ld.n, st=st.n, bmt=b_bmt.n, bev=b_bev.n, du=b_duc.n, dut=b_dut.n,
                 due=b_due.n, da=b_da.n, dg=b_dgc.n, cm=b_cmc.n, ce=b_ce.n, db=b_db.n,
                 sc=b_sc.n, yt=b_ytc.n, ya=b_ya.n, stc=b_stc.n, spa=b_spa.n,
                 dtm=b_dtm.n, dta=b_dta.n, xb=b_xb.n, gm=b_gm.n)
        with nc.Block(f"B{k}") as blk:
            @blk.sync
            def _(s, k=k, B=B, t0=t0, tbs=tbs, c0=c0):
                if k == 0:
                    coll.wait(s, 1)
                for pt in range(4):
                    ld.sig(s.dma_start(b_u[:, pt * Ts:(pt + 1) * Ts],
                                       u_h[pt * 128:(pt + 1) * 128, t0:t0 + Ts]), 16)
                ld.sig(s.dma_start(attn[0:16, 0:512], xdbs_h[0:16, t0:t0 + Ts]), 16)
                ld.sig(s.dma_start(b_bm, xdbs_h[16:32, t0:t0 + Ts]), 16)
                ld.sig(s.dma_start(b_cm, xdbs_h[32:48, t0:t0 + Ts]), 16)
                b_dut.wait(s, B["dut"] + 4)
                for pt in range(4):
                    ld.sig(s.dma_start(b_szc[:, pt * Ts:(pt + 1) * Ts],
                                       sz_h[pt * 128:(pt + 1) * 128, t0:t0 + Ts]), 16)
                b_gm.wait(s, B["gm"] + 8)
                for pt in range(4):
                    st.sig(s.dma_start(rs_gin[tbs, pt * 128:(pt + 1) * 128, c0:c0 + Ts],
                                       b_gst[:, pt * Ts:(pt + 1) * Ts]), 16)
                    st.sig(s.dma_start(rs_gin[tbs, 512 + pt * 128:512 + (pt + 1) * 128, c0:c0 + Ts],
                                       b_gst[:, 2048 + pt * Ts:2048 + (pt + 1) * Ts]), 16)
                st.wait(s, st.n)

            @blk.tensor
            def _(t, B=B):
                b_xb.wait(t, B["xb"] + 1)
                for pt in range(4):
                    if pt > 0:
                        b_spa.wait(t, B["spa"] + 3 * (pt - 1) + 1)
                    b_dtm.sig(t.matmul(p_cr, wdt_sb[0:16, pt * 128:(pt + 1) * 128],
                                       attn[0:16, 0:512], start=True, stop=True))
                b_xb.wait(t, B["xb"] + 2)
                for tb in range(4):
                    inst = t.transpose(p_tr[:, tb * 16:(tb + 1) * 16],
                                       b_bm[0:16, tb * 128:(tb + 1) * 128], eye[0:16, 0:16])
                b_bmt.sig(inst)
                b_bev.wait(t, B["bev"] + 1)
                for tt_ in range(4):
                    if tt_ > 0:
                        b_due.wait(t, B["due"] + tt_)
                    b_duc.wait(t, B["du"] + 4)
                    for pt in range(4):
                        inst = t.transpose(p_tr[:, pt * 128:(pt + 1) * 128],
                                           b_du[:, pt * Ts + tt_ * 128: pt * Ts + (tt_ + 1) * 128], eye)
                    b_dut.sig(inst)
                for nn in range(16):
                    par = nn % 2
                    b_ce.wait(t, B["ce"] + nn)
                    if nn == 0:
                        b_spa.wait(t, B["spa"] + 10)
                    if nn >= 2:
                        b_ytc.wait(t, B["yt"] + 4 * (nn - 1))
                    b_cmc.sig(t.matmul(p_cr, selv[:, nn * 128:(nn + 1) * 128], b_cm,
                                       start=True, stop=True))
                    b_dgc.wait(t, B["dg"] + nn + 1)
                    b_due.wait(t, B["due"] + 4)
                    for pt in range(4):
                        if nn > 0 or pt >= 2:
                            b_sc.wait(t, B["sc"] + 4 * nn + pt - 1)
                        for tb in range(4):
                            inst = t.matmul(p_db[pt % 2][:, tb * 128:(tb + 1) * 128],
                                            b_duT[:, tb * Ts + pt * 128: tb * Ts + (pt + 1) * 128],
                                            b_dg[par][:, tb * 128:(tb + 1) * 128],
                                            start=True, stop=True)
                        b_db.sig(inst)
                    if nn > 0:
                        parp = (nn - 1) % 2
                        for pt in range(4):
                            b_ytc.wait(t, B["yt"] + 4 * (nn - 1) + pt + 1)
                            b_ya.sig(t.matmul(p_y[pt], eye, b_yt[parp][:, pt * Ts:(pt + 1) * Ts],
                                              start=(nn - 1 == 0), stop=False))
                for pt in range(4):
                    b_ytc.wait(t, B["yt"] + 60 + pt + 1)
                    b_ya.sig(t.matmul(p_y[pt], eye, b_yt[1][:, pt * Ts:(pt + 1) * Ts],
                                      start=False, stop=True))

            @blk.scalar
            def _(s_, B=B):
                for pt in range(4):
                    b_dtm.wait(s_, B["dtm"] + pt + 1)
                    if pt > 0:
                        b_dta.wait(s_, B["dta"] + pt)
                    s_.activation(spA, p_cr, AF.Abs, bias=bdth[:, pt:pt + 1])
                    b_spa.sig(s_.activation(spB, p_cr, AF.Relu, bias=bdth[:, pt:pt + 1]))
                    b_spa.wait(s_, b_spa.n)
                    b_spa.sig(s_.activation(b_dt[:, pt * Ts:(pt + 1) * Ts], spA, AF.Exp,
                                            scale=-1.0))
                    b_spa.wait(s_, b_spa.n)
                    b_spa.sig(s_.activation(spA, b_dt[:, pt * Ts:(pt + 1) * Ts], AF.Ln,
                                            bias=1.0))
                b_bmt.wait(s_, B["bmt"] + 1)
                b_bev.sig(s_.copy(bmT, p_tr[:, 0:64]))
                for tt_ in range(4):
                    b_dut.wait(s_, B["dut"] + tt_ + 1)
                    b_due.sig(s_.copy(b_duT[:, tt_ * Ts:(tt_ + 1) * Ts], p_tr))
                for nn in range(16):
                    par = nn % 2
                    if nn >= 2:
                        b_sc.wait(s_, B["sc"] + 4 * (nn - 1))
                    for pt in range(4):
                        b_da.sig(s_.activation(b_dA[par][:, pt * Ts:(pt + 1) * Ts],
                                               b_dt[:, pt * Ts:(pt + 1) * Ts], AF.Exp,
                                               scale=A_sb[:, pt * 16 + nn: pt * 16 + nn + 1]))
                    if nn >= 2:
                        b_db.wait(s_, B["db"] + 4 * (nn - 1))
                    for tb in range(4):
                        inst = s_.activation(b_dg[par][:, tb * 128:(tb + 1) * 128], eye, AF.Copy,
                                             scale=bmT[:, tb * 16 + nn: tb * 16 + nn + 1])
                    b_dgc.sig(inst)
                    b_cmc.wait(s_, B["cm"] + nn + 1)
                    if nn >= 2:
                        b_ytc.wait(s_, B["yt"] + 4 * (nn - 1))
                    b_ce.sig(s_.copy(b_crep[par], p_cr))

            @blk.vector
            def _(v, B=B):
                ld.wait(v, B["ld"] + 112)
                b_xb.sig(v.tensor_scalar(attn[0:16, 0:512], attn[0:16, 0:512],
                                         bxpr[:, 0:1], None, ALU.add))
                b_xb.sig(v.tensor_scalar(b_bm, b_bm, bxpr[:, 1:2], None, ALU.add))
                b_xb.sig(v.tensor_scalar(b_cm, b_cm, bxpr[:, 2:3], None, ALU.add))
                for pt in range(4):
                    b_spa.wait(v, B["spa"] + 3 * pt + 3)
                    b_dta.sig(v.tensor_add(b_dt[:, pt * Ts:(pt + 1) * Ts], spA, spB))
                for pt in range(4):
                    b_duc.sig(v.tensor_mul(b_du[:, pt * Ts:(pt + 1) * Ts],
                                           b_dt[:, pt * Ts:(pt + 1) * Ts],
                                           b_u[:, pt * Ts:(pt + 1) * Ts]))
                for nn in range(16):
                    par = nn % 2
                    for pt in range(4):
                        b_da.wait(v, B["da"] + 4 * nn + pt + 1)
                        b_db.wait(v, B["db"] + 4 * nn + pt + 1)
                        if nn >= 2:
                            b_ytc.wait(v, B["yt"] + 4 * (nn - 1))
                            b_stc.wait(v, B["stc"] + 4 * (nn - 2) + pt + 1)
                        b_sc.sig(v.tensor_tensor_scan(b_h[par][:, pt * Ts:(pt + 1) * Ts],
                                                      b_dA[par][:, pt * Ts:(pt + 1) * Ts],
                                                      p_db[pt % 2],
                                                      st_sb[:, pt * 16 + nn: pt * 16 + nn + 1],
                                                      ALU.mult, ALU.add))
                ld.wait(v, B["ld"] + 176)
                b_ya.wait(v, B["ya"] + 64)
                b_ytc.wait(v, B["yt"] + 64)
                for pt in range(4):
                    v.tensor_mul(b_gtmp, p_y[pt], b_szc[:, pt * Ts:(pt + 1) * Ts])
                    b_gm.sig(v.tensor_scalar(b_gst[:, pt * Ts:(pt + 1) * Ts], b_gtmp,
                                             mb0, None, ALU.mult))
                    b_gm.sig(v.tensor_scalar(b_gst[:, 2048 + pt * Ts:2048 + (pt + 1) * Ts],
                                             b_gtmp, mb1, None, ALU.mult))

            @blk.gpsimd
            def _(g, B=B):
                for nn in range(16):
                    par = nn % 2
                    for pt in range(4):
                        b_sc.wait(g, B["sc"] + 4 * nn + pt + 1)
                        b_stc.sig(g.tensor_copy(st_sb[:, pt * 16 + nn: pt * 16 + nn + 1],
                                                b_h[par][:, (pt + 1) * Ts - 1:(pt + 1) * Ts]))
                        b_ce.wait(g, B["ce"] + nn + 1)
                        if nn >= 2:
                            b_ya.wait(g, B["ya"] + 4 * (nn - 1))
                        b_ytc.sig(g.tensor_tensor(b_yt[par][:, pt * Ts:(pt + 1) * Ts],
                                                  b_h[par][:, pt * Ts:(pt + 1) * Ts],
                                                  b_crep[par], ALU.mult))

    def wout_reg(dc):
        if dc < 4:
            return hT[:, dc * 1024:(dc + 1) * 1024]
        return big_kv[:, (dc - 4) * 1024:(dc - 3) * 1024]

    with nc.Block("RSg") as blk:
        @blk.gpsimd
        def _(g):
            st.wait(g, st.n)
            coll.sig(g.collective_compute("ReduceScatter", ALU.add, replica_groups=GROUPS,
                                          ins=[rs_gin.opt()], outs=[g_full.opt()]), 1)

        @blk.sync
        def _(s):
            st.wait(s, st.n)
            cst.wait(s, 32)
            for dc in range(8):
                ldc.sig(s.dma_start(wout_reg(dc), wout_bf[:, dc * 1024:(dc + 1) * 1024]), 16)

    for k in range(8):
        t0 = k * Tc
        B = dict(ld=ld.n, st=st.n, mm=c_mm.n, ev=c_ev.n, v=c_v.n, a=c_a.n,
                 x1=c_x1c.n, tl=c_tl.n)
        with nc.Block(f"C{k}") as blk:
            @blk.sync
            def _(s, k=k, B=B, t0=t0):
                if k == 0:
                    coll.wait(s, 2)
                for dc in range(8):
                    ld.sig(s.dma_start(c_gsb[:, dc * Tc:(dc + 1) * Tc],
                                       g_full[dc * 128:(dc + 1) * 128, t0:t0 + Tc]), 16)
                for m in range(4):
                    ld.sig(s.dma_start(x_sb[:, m * D:(m + 1) * D],
                                       x_own[t0 + m * 128: t0 + (m + 1) * 128, :]), 16)
                for m in range(4):
                    c_x1c.wait(s, B["x1"] + m + 1)
                    st.sig(s.dma_start(x1own_h[t0 + m * 128: t0 + (m + 1) * 128, :],
                                       c_x1[:, m * D:(m + 1) * D]), 16)
                    if k >= 6:
                        rr = t0 + m * 128 - 3072
                        c_tl.wait(s, B["tl"] + 4 * m + 4)
                        st.sig(s.dma_start(rs_t_in[0, rr:rr + 128, 0:512], tt1), 16)
                        st.sig(s.dma_start(rs_t_in[0, rr:rr + 128, 512:1024], tt2), 16)
                        st.sig(s.dma_start(rs_t_in[1, rr:rr + 128, 0:512], x3_sb[:, 0:512]), 16)
                        st.sig(s.dma_start(rs_t_in[1, rr:rr + 128, 512:1024], x3_sb[:, 512:1024]), 16)
                st.wait(s, st.n)

            @blk.tensor
            def _(t, B=B, k=k):
                if k == 0:
                    ldc.wait(t, 128)
                ld.wait(t, B["ld"] + 192)
                for m in range(4):
                    for oh in range(2):
                        ps = pM[2 * (m % 2) + oh]
                        if m >= 2:
                            c_ev.wait(t, B["ev"] + 2 * (m - 1) + oh + 1)
                        for dc in range(8):
                            t.matmul(ps, c_gsb[:, dc * Tc + m * 128: dc * Tc + m * 128 + 128],
                                     wout_at(dc, oh), start=(dc == 0), stop=False)
                        inst = t.matmul(ps, ones1, bout_sb[:, oh * 512:(oh + 1) * 512],
                                        start=False, stop=True)
                        c_mm.sig(inst)

            @blk.vector
            def _(v, B=B, k=k):
                ld.wait(v, B["ld"] + 192)
                for m in range(4):
                    for oh in range(2):
                        c_mm.wait(v, B["mm"] + 2 * m + oh + 1)
                        c_ev.sig(v.tensor_add(x2_sb[:, m * D + oh * 512: m * D + (oh + 1) * 512],
                                              pM[2 * (m % 2) + oh],
                                              x_sb[:, m * D + oh * 512: m * D + (oh + 1) * 512]))
                    c_ev.wait(v, c_ev.n)
                    c_v.sig(v.bn_stats(bns[:, 0:6], x2_sb[:, m * D: m * D + 512]))
                    c_v.sig(v.bn_stats(bns[:, 6:12], x2_sb[:, m * D + 512: m * D + 1024]))
                    c_v.wait(v, c_v.n)
                    c_v.sig(v.bn_aggr(mv, bns))
                    c_a.wait(v, B["a"] + m + 1)
                    c_v.sig(v.reciprocal(rs_t, sd))
                    c_v.wait(v, c_v.n)
                    c_x1c.sig(v.tensor_scalar(c_x1[:, m * D:(m + 1) * D],
                                              x2_sb[:, m * D:(m + 1) * D],
                                              mv[:, 0:1], rs_t, ALU.subtract, ALU.mult))
                    if k >= 6:
                        st.wait(v, B["st"] + 80 * m)
                        c_tl.sig(v.tensor_scalar(tt1, c_x1[:, m * D: m * D + 512],
                                                 mt0, None, ALU.mult))
                        c_tl.sig(v.tensor_scalar(tt2, c_x1[:, m * D + 512: m * D + 1024],
                                                 mt0, None, ALU.mult))
                        c_tl.sig(v.tensor_scalar(x3_sb[:, 0:512], c_x1[:, m * D: m * D + 512],
                                                 mt1, None, ALU.mult))
                        c_tl.sig(v.tensor_scalar(x3_sb[:, 512:1024],
                                                 c_x1[:, m * D + 512: m * D + 1024],
                                                 mt1, None, ALU.mult))

            @blk.scalar
            def _(s, B=B):
                for m in range(4):
                    c_v.wait(s, B["v"] + 4 * m + 3)
                    c_a.sig(s.activation(sd, mv[:, 1:2], AF.Sqrt, bias=eps))

    with nc.Block("RSx1") as blk:
        @blk.gpsimd
        def _(g):
            st.wait(g, st.n)
            coll.sig(g.collective_compute("ReduceScatter", ALU.add, replica_groups=GROUPS,
                                          ins=[rs_t_in.opt()], outs=[halo_h.opt()]), 1)

    em = Em(nc)
    for qn, kind, inc in [("pe", "tensor", 1), ("act", "scalar", 1), ("dve", "vector", 1),
                          ("gp", "gpsimd", 1), ("ld", "sync", 16), ("ldw", "sync", 16),
                          ("st", "sync", 16)]:
        em.queue(qn, kind, inc)

    V = dict(eye=eye, eyebf=eyebf, ones1=ones1, eps=eps, e12=e12, halfpi=halfpi,
             maskL=maskL, maskR=maskR, shA=shA, shB=shB, wqkv_sb=wqkv_sb, wo_sb=wo_sb,
             w2s_sb=w2s_sb, w1a_sb=w1a_sb, w1g_sb=w1g_sb, b1a_sb=b1a_sb, b1g_sb=b1g_sb,
             b2_sb=b2_sb, gam_rep=gam_rep, bet_rep=bet_rep, lngrow_sb=lngrow_sb,
             lnbrow_sb=lnbrow_sb, qsrow_sb=qsrow_sb, ksrow_sb=ksrow_sb, qscrep=qscrep,
             kscrep=kscrep, pos_sb=pos_sb, mf_sb=mf_sb, shm_sb=shm_sb, invf=invf,
             lnbase=lnbase, frq=frq, rrf=rrf, ang=ang, angc=angc, rri=rri, snt=snt,
             cstt=cstt, pwv=pwv, sclog=sclog, scl=scl, iscl=iscl, csx=csx, ssx=ssx,
             cski=cski, sski=sski, csq2=csq2, nsq2=nsq2, sq2=sq2, csk2=csk2, nsk2=nsk2,
             sk2=sk2, x_sb=x_sb, x1T=x1T, q_sb=q_sb, k_sb=k_sb, qss=qss, kss=kss,
             rq=rq, rk=rk, qr2=qr2, kr2=kr2, tt1=tt1, tt2=tt2, qT=qT, kring=kring,
             vring=vring, s8=s8, nm=nm, nmz=nmz, P_sb=P_sb, PT_sb=PT_sb, sume=sume,
             P_bf=P_bf, rsume=rsume, attn=attn, attnT=attnT, x2_sb=x2_sb, x2tail=x2tail, bns=bns,
             mv=mv, sd=sd, rs=rs_t, hh_sb=hh_sb, hT=hT, a_sb=a_sb, gg_sb=gg_sb,
             gate=gate, x3_sb=x3_sb, mres_sb=mres_sb, mt0=mt0, mt1=mt1,
             p1=p1, p2=p2, p3=p3, pS=pS, pT=pT, pVa=pVa, pVb=pVb,
             collsem=coll.sem, cstw=(cst.sem, CSTN))

    R = Res
    S = dict(r_eye=R(), r_masks=R(), r_sh=R(), r_invf=R(), r_lnb=R(),
             r_gam=R(), r_bet=R(), r_qsc=R(), r_ksc=R(),
             r_x=[R() for _ in range(4)], r_x1T=[R() for _ in range(8)],
             r_q=R(), r_k=R(), r_qss=R(), r_kss=R(), r_rq=R(), r_rk=R(),
             r_qr2=R(), r_kr2=R(), r_tt1=R(), r_tt2=R(), r_qT=R(),
             r_kring=[R() for _ in range(NRING)], r_vring=[R() for _ in range(NRING)],
             r_s8=R(), r_nm=R(), r_nmz=R(), r_P=R(), r_Pb=R(), r_PT=R(), r_sume=R(), r_rsume=R(),
             r_attn=R(), r_attnT=R(), r_x2=[R() for _ in range(4)], r_x2tail=R(),
             r_hh=[R() for _ in range(4)], r_hT=R(), r_a=R(), r_gg=R(),
             r_gate=[R() for _ in range(NHh)], r_x3=R(),
             r_w1a=[R(), R()], r_w1g=[R(), R()],
             r_p1=R(), r_p2=R(), r_p3=R(), r_pS=R(), r_pT=R(), r_pV=[R(), R()],
             r_tabs=R(), r_lns=R(), r_wq=R(), r_wo=R(), r_w2s=[R(), R()],
             r_b1a=R(), r_b1g=R(), r_b2=R(), r_lng=R(), r_lnbr=R(), r_qsr=R(),
             r_ksr=R(), r_mf=R())

    xrow0 = [[] for _ in range(8)]
    tails0 = []
    W0 = dict(FWB[0])
    for kk in ("b1a_r", "b1g_r", "b2_r", "lng_row", "lnb_row", "qs_row", "ks_row"):
        W0[kk] = FWE[0][kk]
    W0["mf_r"] = mf_r
    W0["rs_t2"] = rs_t2_in

    def x_at0(r):
        if r < H4:
            return halo_h[r:r + 128, :]
        return x1own_h[r - H4:r - H4 + 128, :]

    if not skip_de:
        emit_ffn(nc, em, V, S, 0, W0, x_at0, x2own_h, None, xrow0, tails0, None, False)

    coll2 = em.t("gp", lambda g: g.collective_compute(
        "ReduceScatter", ALU.add, replica_groups=GROUPS,
        ins=[rs_t2_in.opt()], outs=[halo2_h.opt()]), tails0)

    W1 = dict(FWB[1])
    for kk in ("b1a_r", "b1g_r", "b2_r", "lng_row", "lnb_row", "qs_row", "ks_row"):
        W1[kk] = FWE[1][kk]
    W1["mres_ap"] = x1own_h

    def x_at1(r):
        if r < H4:
            return halo2_h[r:r + 128, :]
        return x2own_h[r - H4:r - H4 + 128, :]

    if not skip_de:
        emit_ffn(nc, em, V, S, 1, W1, x_at1, xo, xrow0, None, [], coll2, True)

    with nc.Block("emmain") as blk:
        em.flush(blk)
    with nc.Block("fin") as blk:
        @blk.sync
        def _(s):
            s.wait_ge(em.qs["st"]["ch"].sem, em.qs["st"]["ch"].n)
    return nc


def _ffn_arrays(ins, li):
    HID = 2816
    NHh = 22
    wqkv = ins["L0_ffn_wqkv"][li]; wo_ = ins["L0_ffn_wo"][li]
    w1 = ins["L0_ffn_w1"][li]; b1 = ins["L0_ffn_b1"][li]
    w2 = ins["L0_ffn_w2"][li]; b2 = ins["L0_ffn_b2"][li]
    lng = ins["L0_ffn_ln_g"][li]; lnb = ins["L0_ffn_ln_b"][li]
    qs = ins["L0_ffn_q_scale"][li]; ks = ins["L0_ffn_k_scale"][li]
    w1a = np.zeros((1024, HID), np.float32); w1a[:, :2730] = w1[:, :2730]
    w1g = np.zeros((1024, HID), np.float32); w1g[:, :2730] = w1[:, 2730:]
    b1a = np.zeros(HID, np.float32); b1a[:2730] = b1[:2730]
    b1g = np.zeros(HID, np.float32); b1g[:2730] = b1[2730:]
    w2p = np.zeros((HID, 1024), np.float32); w2p[:2730] = w2
    rsh = lambda w: np.ascontiguousarray(
        w.reshape(w.shape[0] // 128, 128, w.shape[1]).transpose(1, 0, 2)
        .reshape(128, (w.shape[0] // 128) * w.shape[1]), dtype=np.float32)
    rshh = lambda w: np.ascontiguousarray(
        w.reshape(8, 128, NHh, 128).transpose(1, 2, 0, 3)
        .reshape(128, NHh * 1024), dtype=np.float32)
    return {f"wqkv_r{li}": rsh(wqkv), f"wo_r{li}": rsh(wo_),
            f"w1a_r{li}": rshh(w1a), f"w1g_r{li}": rshh(w1g), f"w2_r{li}": rsh(w2p),
            f"b1a_r{li}": np.ascontiguousarray(b1a.reshape(NHh, 128).T),
            f"b1g_r{li}": np.ascontiguousarray(b1g.reshape(NHh, 128).T),
            f"b2_r{li}": b2.reshape(1, 1024).astype(np.float32),
            f"lng_row{li}": lng.reshape(1, 1024).astype(np.float32),
            f"lnb_row{li}": lnb.reshape(1, 1024).astype(np.float32),
            f"qs_row{li}": qs.reshape(1, 64).astype(np.float32),
            f"ks_row{li}": ks.reshape(1, 64).astype(np.float32)}


def stage_fused(ins):
    import ml_dtypes
    x = ins["L0_attn_x"]
    W_in = ins["L0_attn_W_in"]; b_in = ins["L0_attn_b_in"]
    W_conv = ins["L0_attn_W_conv"]; b_conv = ins["L0_attn_b_conv"]
    W_xp = ins["L0_attn_W_xp"]; b_xp = ins["L0_attn_b_xp"]
    W_dt = ins["L0_attn_W_dt"]; b_dt = ins["L0_attn_b_dt"]
    A_log = ins["L0_attn_A_log"]
    W_out = ins["L0_attn_W_out"]; b_out = ins["L0_attn_b_out"]
    wout_r = np.ascontiguousarray(
        W_out.reshape(8, 128, 1024).transpose(1, 0, 2).reshape(128, 8 * 1024), dtype=np.float32)
    bout_r = np.ascontiguousarray(b_out.reshape(1, 1024), dtype=np.float32)
    ffn = [_ffn_arrays(ins, 0), _ffn_arrays(ins, 1)]
    r4 = lambda v: np.ascontiguousarray(v.reshape(4, 128).T, dtype=np.float32)
    maps = []
    for c in range(NCORE):
        b, h = c // 2, c % 2
        dh = h
        s0 = h * NT
        d0 = dh * 512
        wcat = np.concatenate([W_in[:, d0:d0 + 512], W_in[:, 1024 + d0:1024 + d0 + 512]], axis=1)
        w_in_h = np.ascontiguousarray(
            wcat.reshape(8, 128, 1024).transpose(1, 0, 2).reshape(128, 8192), dtype=np.float32)
        mp = {"x_a": np.ascontiguousarray(x[b], dtype=np.float32),
              "x_own": np.ascontiguousarray(x[b, s0:s0 + NT, :], dtype=np.float32),
              "w_in_h": w_in_h,
              "bin_uh": r4(b_in[d0:d0 + 512]),
              "bin_zh": r4(b_in[1024 + d0:1024 + d0 + 512]),
              "bconvh": r4(b_conv[d0:d0 + 512]),
              "wconvh": np.ascontiguousarray(
                  W_conv[d0:d0 + 512, 0, :].reshape(4, 128, 4).transpose(1, 0, 2)
                  .reshape(128, 16), dtype=np.float32),
              "wxp_h": np.ascontiguousarray(
                  W_xp[d0:d0 + 512].reshape(4, 128, 48).transpose(1, 0, 2)
                  .reshape(128, 192), dtype=np.float32),
              "bxp_r": np.ascontiguousarray(b_xp.reshape(3, 16).T, dtype=np.float32),
              "wdt_h": np.ascontiguousarray(W_dt[:, d0:d0 + 512], dtype=np.float32),
              "bdth": r4(b_dt[d0:d0 + 512]),
              "alog_h": np.ascontiguousarray(
                  A_log[d0:d0 + 512].reshape(4, 128, 16).transpose(1, 0, 2)
                  .reshape(128, 64), dtype=np.float32),
              "mb0": np.full((128, 1), 1.0 if dh == 0 else 0.0, np.float32),
              "mb1": np.full((128, 1), 1.0 if dh == 1 else 0.0, np.float32),
              "mt0": np.full((128, 1), 1.0 if h == 1 else 0.0, np.float32),
              "mt1": np.full((128, 1), 1.0 if h == 0 else 0.0, np.float32),
              "wout_r": wout_r, "bout_r": bout_r}
        pos = np.zeros((128, 10 * 4), np.float32)
        for ti in range(10 * 4):
            pos[:, ti] = s0 - H4 + ti * 128 + np.arange(128)
        mf = np.zeros((128, 4 * 640), ml_dtypes.bfloat16)
        if h == 0:
            for m in range(4):
                cut = 512 - m * 128
                mf[:, m * 640: m * 640 + cut] = NEG
        shm = (np.ones((128, 1), np.float32) if h == 1
               else np.zeros((128, 1), np.float32))
        mp["pos_r"] = pos; mp["mf_r"] = mf; mp["shm_r"] = shm
        mp.update(ffn[0]); mp.update(ffn[1])
        maps.append(mp)
    return maps


def kernel(**inputs):
    nc = build_fused()
    maps = stage_fused(inputs)
    res = run_bass_kernel_spmd(nc, maps, core_ids=list(range(NCORE)))
    out = np.empty((4, 2 * NT, D), np.float32)
    for c in range(NCORE):
        b, h = c // 2, c % 2
        out[b, h * NT:(h + 1) * NT] = res.results[c]["xo_t"]
    return out
